# version 08
# baseline (speedup 1.0000x reference)
import numpy as np
import ml_dtypes
from contextlib import ExitStack

import concourse.bass as bass
import concourse.mybir as mybir
from concourse.bass_utils import run_bass_kernel_spmd

F32 = mybir.dt.float32
BF16 = mybir.dt.bfloat16
AF = mybir.ActivationFunctionType
ALU = mybir.AluOpType
AX = mybir.AxisListType
BFNP = ml_dtypes.bfloat16

P3_LIMIT = (8, 16, 5)
T = 4096
DM = 1024
NKC = 8
TT = 512
NTT = 8


class Buf:
    __slots__ = ("w", "rd", "name")
    def __init__(self, name=""):
        self.w = None
        self.rd = {}
        self.name = name

class APB:
    __slots__ = ("ap", "buf")
    def __init__(self, ap, buf):
        self.ap = ap
        self.buf = buf

def _a(x):
    return x.ap if isinstance(x, APB) else x

def _bufs(*xs):
    out = []
    for x in xs:
        if isinstance(x, APB) and x.buf is not None:
            if isinstance(x.buf, list):
                out.extend(x.buf)
            else:
                out.append(x.buf)
    return out

class Rec:
    __slots__ = ("emit", "waits", "eff", "tick", "is_dma", "inc")
    def __init__(self, emit, waits, eff, tick, is_dma):
        self.emit = emit
        self.waits = waits
        self.eff = eff
        self.tick = tick
        self.is_dma = is_dma
        self.inc = None

ENG_ORDER = ["pe", "dve", "act", "pool", "sp"]
ENG_SECTION = {"pe": "tensor", "dve": "vector", "act": "scalar",
               "pool": "gpsimd", "sp": "sync"}

class Prog:
    def __init__(self, nc):
        self.nc = nc
        self.streams = list(ENG_ORDER)
        self.sidx = {n: i for i, n in enumerate(self.streams)}
        self.cnt = [0] * 5
        self.known = [dict() for _ in range(5)]
        self.recs = {n: [] for n in ENG_ORDER}
        self.needed = None
        self.dma_issuer = {}

    def _stream_for_dma(self, queue, dst_name):
        key = "q:%s:%s" % (queue, dst_name)
        if key not in self.sidx:
            self.sidx[key] = len(self.streams)
            self.streams.append(key)
            self.cnt.append(0)
            self.dma_issuer[self.sidx[key]] = self.sidx[queue]
        return self.sidx[key]

    def _record(self, eng, emit, reads, writes, eff=None, is_dma=False):
        ei = self.sidx[eng]
        if eff is None:
            eff = ei
        need = {}
        def add(st, t):
            if st == ei and st == 0:
                return
            if st >= 5:
                t = self.cnt[st]
            if self.known[ei].get(st, 0) >= t:
                return
            if need.get(st, 0) < t:
                need[st] = t
        for b in reads:
            if b.w:
                add(*b.w)
        for b in writes:
            if b.w:
                if not (is_dma and b.w[0] == eff):
                    add(*b.w)
            for st, t in b.rd.items():
                add(st, t)
        for st, t in need.items():
            self.known[ei][st] = t
        if is_dma:
            self.cnt[eff] += 16
        else:
            self.cnt[eff] += 1
        tick = self.cnt[eff]
        rec = Rec(emit, sorted(need.items()), eff, tick, is_dma)
        self.recs[eng].append(rec)
        for b in reads:
            b.rd[eff] = tick
        for b in writes:
            b.w = (eff, tick)
            b.rd = {}
        return rec

    def mm(self, out, lhsT, rhs, start=True, stop=True, tile_position=None):
        oa, la, ra = _a(out), _a(lhsT), _a(rhs)
        tp_ = tile_position
        def emit(E, oa=oa, la=la, ra=ra, start=start, stop=stop, tp_=tp_):
            return E.matmul(oa, la, ra, start=start, stop=stop,
                            tile_position=tp_, skip_group_check=True)
        self._record("pe", emit, _bufs(lhsT, rhs), _bufs(out))

    def tp(self, out, in_, ident):
        oa, ia, da = _a(out), _a(in_), _a(ident)
        def emit(E, oa=oa, ia=ia, da=da):
            return E.transpose(oa, ia, da)
        self._record("pe", emit, _bufs(in_, ident), _bufs(out))

    def act(self, out, in_, func, bias=0.0, scale=1.0, eng="act"):
        oa, ia = _a(out), _a(in_)
        ba, sa = _a(bias), _a(scale)
        def emit(E, oa=oa, ia=ia, func=func, ba=ba, sa=sa):
            return E.activation(oa, ia, func, bias=ba, scale=sa)
        self._record(eng, emit, _bufs(in_, bias, scale), _bufs(out))

    def cp(self, out, in_, eng="dve"):
        oa, ia = _a(out), _a(in_)
        if eng == "act":
            def emit(E, oa=oa, ia=ia):
                return E.copy(oa, ia)
        else:
            def emit(E, oa=oa, ia=ia):
                return E.tensor_copy(oa, ia)
        self._record(eng, emit, _bufs(in_), _bufs(out))

    def tt(self, out, a, b, alu, eng="dve"):
        oa, aa, ba = _a(out), _a(a), _a(b)
        def emit(E, oa=oa, aa=aa, ba=ba, alu=alu):
            return E.tensor_tensor(oa, aa, ba, alu)
        self._record(eng, emit, _bufs(a, b), _bufs(out))

    def ts(self, out, a, s1, op0, s2=None, op1=None, eng="dve"):
        oa, aa = _a(out), _a(a)
        s1a, s2a = _a(s1), _a(s2) if s2 is not None else None
        if s2 is None:
            def emit(E, oa=oa, aa=aa, s1a=s1a, op0=op0):
                return E.tensor_single_scalar(oa, aa, s1a, op0)
        else:
            def emit(E, oa=oa, aa=aa, s1a=s1a, s2a=s2a, op0=op0, op1=op1):
                return E.tensor_scalar(oa, aa, s1a, s2a, op0, op1)
        self._record(eng, emit, _bufs(a, s1, s2), _bufs(out))

    def stt(self, out, in0, scalar, in1, op0, op1, eng="dve"):
        oa, i0, sc, i1 = _a(out), _a(in0), _a(scalar), _a(in1)
        def emit(E, oa=oa, i0=i0, sc=sc, i1=i1, op0=op0, op1=op1):
            return E.scalar_tensor_tensor(oa, i0, sc, i1, op0, op1)
        self._record(eng, emit, _bufs(in0, scalar, in1), _bufs(out))

    def red(self, out, in_, alu, negate=False, absval=False, eng="dve"):
        oa, ia = _a(out), _a(in_)
        def emit(E, oa=oa, ia=ia, alu=alu, negate=negate, absval=absval):
            return E.tensor_reduce(oa, ia, AX.X, alu,
                                   apply_absolute_value=absval, negate=negate)
        self._record(eng, emit, _bufs(in_), _bufs(out))

    def recip(self, out, in_):
        oa, ia = _a(out), _a(in_)
        def emit(E, oa=oa, ia=ia):
            return E.reciprocal(oa, ia)
        self._record("dve", emit, _bufs(in_), _bufs(out))

    def mset(self, dst, val, eng="pool"):
        oa = _a(dst)
        def emit(E, oa=oa, val=val):
            return E.memset(oa, val)
        self._record(eng, emit, [], _bufs(dst))

    def barrier(self):
        snap = [(st, self.cnt[st]) for st in range(len(self.streams)) if self.cnt[st] > 0]
        for eng in ENG_ORDER:
            ei = self.sidx[eng]
            waits = []
            for st, v in snap:
                if st == ei:
                    continue
                if self.known[ei].get(st, 0) >= v:
                    continue
                waits.append((st, v))
                self.known[ei][st] = v
            def emit(E):
                return E.nop()
            self.cnt[ei] += 1
            rec = Rec(emit, sorted(waits), ei, self.cnt[ei], False)
            self.recs[eng].append(rec)

    def dma(self, queue, dst, src, dst_name=None, accum=None):
        oa, ia = _a(dst), _a(src)
        if dst_name is None:
            dst_name = oa.tensor.name if hasattr(oa, "tensor") else "anon"
        st = self._stream_for_dma(queue, dst_name)
        if accum is None:
            def emit(E, oa=oa, ia=ia):
                return E.dma_start(oa, ia)
        else:
            def emit(E, oa=oa, ia=ia, accum=accum):
                return E.dma_start(oa, ia, accum_op=accum)
        self._record(queue, emit, _bufs(src), _bufs(dst), eff=st, is_dma=True)

    def finalize_and_emit(self):
        nc = self.nc
        needed = [set() for _ in range(len(self.streams))]
        for eng in ENG_ORDER:
            for rec in self.recs[eng]:
                for st, v in rec.waits:
                    if st < 5:
                        needed[st].add(v)
        import bisect
        ranks = {}
        for si in range(5):
            ranks[si] = sorted(needed[si])
        def remap(st, v):
            if st >= 5:
                return v
            return bisect.bisect_right(ranks[st], v)
        for eng in ENG_ORDER:
            ei = self.sidx[eng]
            for rec in self.recs[eng]:
                rec.waits = [(st, remap(st, v)) for st, v in rec.waits]
                if rec.is_dma:
                    rec.inc = 16
                elif rec.tick in needed[rec.eff]:
                    rec.inc = 1
        final_sp = [(st, self.cnt[st]) for st in range(5, len(self.streams))
                    if self.cnt[st] > 0]

        with ExitStack() as ctx:
            sems = []
            for i, name in enumerate(self.streams):
                sems.append(ctx.enter_context(nc.semaphore("sem%d" % i)))
            block = ctx.enter_context(nc.Block())
            for eng in ENG_ORDER:
                recs = self.recs[eng]
                if not recs and eng != "sp":
                    continue
                sec = getattr(block, ENG_SECTION[eng])
                def body(E, recs=recs, eng=eng):
                    for rec in recs:
                        for st, v in rec.waits:
                            E.wait_ge(sems[st], v)
                        ins = rec.emit(E)
                        if rec.inc:
                            ins.then_inc(sems[rec.eff], rec.inc)
                    if eng == "sp":
                        for st, v in final_sp:
                            E.wait_ge(sems[st], v)
                sec(body)
        return nc



def _hs_perms():
    p_hs = np.concatenate([np.arange(0, 64, 2), np.arange(1, 64, 2)])
    p_sw = np.concatenate([p_hs[32:], p_hs[:32]])
    return p_hs, p_sw

def _permute_cols(w, perm):
    nh = w.shape[1] // 64
    w2 = np.asarray(w).reshape(w.shape[0], nh, 64)
    return np.ascontiguousarray(w2[:, :, perm].reshape(w.shape[0], -1))

def _host_prep(I):
    S = {}
    f32 = lambda a: np.ascontiguousarray(np.asarray(a, dtype=np.float32))
    p_hs, p_sw = _hs_perms()
    S["wq1a"] = _permute_cols(I["L0_attn_Wq1"], p_hs)
    S["wq1b"] = _permute_cols(I["L0_attn_Wq1"], p_sw)
    S["wq2a"] = _permute_cols(I["L0_attn_Wq2"], p_hs)
    S["wq2b"] = _permute_cols(I["L0_attn_Wq2"], p_sw)
    S["wk1a"] = _permute_cols(I["L0_attn_Wk1"], p_hs)
    S["wk1b"] = _permute_cols(I["L0_attn_Wk1"], p_sw)
    S["wk2a"] = _permute_cols(I["L0_attn_Wk2"], p_hs)
    S["wk2b"] = _permute_cols(I["L0_attn_Wk2"], p_sw)
    S["wv0"] = f32(I["L0_attn_Wv"])
    S["wout0"] = f32(I["L0_attn_Wout"])
    S["bout_t"] = f32(np.asarray(I["L0_attn_bout"]).reshape(8, 128).T)
    S["lrow"] = f32(np.stack([I["L0_attn_lq1"], I["L0_attn_lk1"],
                              I["L0_attn_lq2"], I["L0_attn_lk2"]], 0))
    pos = np.arange(T, dtype=np.float64)
    div = np.exp(np.arange(0, 64, 2, dtype=np.float64) * (-np.log(10000.0) / 64.0))
    ang = div[:, None] * pos[None, :]
    cos = np.cos(ang).astype(np.float32)
    sin = np.sin(ang).astype(np.float32)
    S["cc"] = np.ascontiguousarray(np.tile(cos, (4, 1))).astype(BFNP)
    S["ss"] = np.ascontiguousarray(np.tile(np.concatenate([-sin, sin], 0), (2, 1))).astype(BFNP)
    S["wred"] = f32(I["L0_ffn_w_red"])
    S["wg"] = f32(I["L0_ffn_wg"])
    S["wgt"] = f32(np.asarray(I["L0_ffn_wg"]).T)
    S["w1_0"] = f32(I["L0_ffn_W1"])
    S["w2_0"] = f32(I["L0_ffn_W2"])
    S["b1t_0"] = f32(np.asarray(I["L0_ffn_b1"]).reshape(8, 16, 128).transpose(0, 2, 1))
    S["b2_0"] = f32(I["L0_ffn_b2"])
    S["wq_r"] = f32(I["L1_attn_Wq"])
    S["wk_r"] = f32(I["L1_attn_Wk"])
    S["wv_r"] = f32(I["L1_attn_Wv"])
    S["wo_r"] = f32(I["L1_attn_Wo"])
    wg5 = np.concatenate([I["L1_attn_Wb"], I["L1_attn_Wfd"], I["L1_attn_Wsd"],
                          I["L1_attn_Wfg"], I["L1_attn_Wsg"]], axis=1)
    S["wg5"] = f32(wg5)
    brow = np.zeros((1, 80), np.float32)
    brow[0, 16:32] = np.asarray(I["L1_attn_fd_bias"])
    brow[0, 32:48] = np.asarray(I["L1_attn_sd_bias"])
    S["brow5"] = brow
    S["g1t"] = f32(np.asarray(I["L1_ffn_g1"]).reshape(8, 128).T)
    S["g2t"] = f32(np.asarray(I["L1_ffn_g2"]).reshape(8, 128).T)
    S["wq_m"] = f32(I["L1_ffn_Wq"])
    S["wdkv"] = f32(I["L1_ffn_Wdkv"])
    S["wuk"] = f32(I["L1_ffn_Wuk"])
    S["wuv"] = f32(I["L1_ffn_Wuv"])
    S["wo_m"] = f32(I["L1_ffn_Wo"])
    S["gatew"] = f32(I["L1_ffn_gate_w"])
    S["w1_1"] = f32(I["L1_ffn_w1"])
    S["w2_1"] = f32(I["L1_ffn_w2"])
    S["ident"] = np.eye(128, dtype=np.float32)
    e2 = np.zeros((2, 128), np.float32)
    e2[0, 0:64] = 1.0
    e2[1, 64:128] = 1.0
    S["e2"] = e2.astype(BFNP)
    mk = np.zeros((4, 128, 512), np.float32)
    kp = np.arange(128)[:, None]
    qf = np.arange(512)[None, :]
    for i in range(4):
        mk[i] = (qf >= kp + 128 * i).astype(np.float32)
    S["mk"] = mk.astype(BFNP)
    return S



def build(phases=None, dbg=()):
    nc = bass.Bass("TRN2", target_bir_lowering=False)
    P = Prog(nc)

    def has(name):
        return phases is None or name in phases

    EXT = {}
    def ein(name, shape, dt=F32):
        h = nc.dram_tensor(name, list(shape), dt, kind="ExternalInput")
        EXT[name] = (h, Buf(name))
        return EXT[name]
    def eap(name):
        h, b = EXT[name]
        return h, b

    ein("x", (DM, T))
    for n, shape in [
        ("wq1a", (DM, DM)), ("wq1b", (DM, DM)), ("wq2a", (DM, DM)), ("wq2b", (DM, DM)),
        ("wk1a", (DM, 128)), ("wk1b", (DM, 128)), ("wk2a", (DM, 128)), ("wk2b", (DM, 128)),
        ("wv0", (DM, 128)), ("wout0", (DM, DM)), ("bout_t", (128, 8)), ("lrow", (4, 64)),
        ("wred", (DM, 16)), ("wg", (8, 16)), ("wgt", (16, 8)),
        ("w1_0", (8, DM, 2048)), ("w2_0", (8, 2048, DM)),
        ("b1t_0", (8, 128, 16)), ("b2_0", (8, DM)),
        ("wq_r", (DM, DM)), ("wk_r", (DM, DM)), ("wv_r", (DM, DM)), ("wo_r", (DM, DM)),
        ("wg5", (DM, 80)), ("brow5", (1, 80)),
        ("g1t", (128, 8)), ("g2t", (128, 8)),
        ("wq_m", (DM, DM)), ("wdkv", (DM, 512)), ("wuk", (512, DM)), ("wuv", (512, DM)),
        ("wo_m", (DM, DM)), ("gatew", (DM, 8)),
        ("w1_1", (8, DM, 2048)), ("w2_1", (8, 2048, DM)),
        ("ident", (128, 128)),
    ]:
        ein(n, shape)
    ein("e2", (2, 128), BF16)
    ein("cc", (128, T), BF16)
    ein("ss", (128, T), BF16)
    ein("mk", (4, 128, 512), BF16)

    y = nc.dram_tensor("y", [DM, T], F32, kind="ExternalOutput")
    y_b = [Buf("y%d" % i) for i in range(NTT)]

    DRAM = {}
    def dscr(name, shape, dt, nb=NTT):
        h = nc.dram_tensor(name, list(shape), dt)
        DRAM[name] = (h, [Buf("%s/%d" % (name, i)) for i in range(nb)])
        return DRAM[name]
    dscr("xA", (DM, T), F32)
    dscr("xB", (DM, T), F32)
    dscr("xC", (DM, T), F32)
    dscr("hbf", (DM, T), BF16)
    dscr("q1r", (DM, T), BF16)
    dscr("q2r", (DM, T), BF16)
    dscr("k1r", (128, T), BF16)
    dscr("k2r", (128, T), BF16)
    dscr("vpack0", (2, 128, 32, 65), BF16)
    dscr("a0", (DM, T), BF16)
    dscr("scores2d", (32, 128), F32, nb=NTT)
    dscr("macc", (DM, T), BF16)
    dscr("qfm", (DM, T), BF16)
    dscr("kpr", (8, 128, 32, 128), BF16)
    dscr("vpr", (8, 128, 32, 128), BF16)
    dscr("cfr", (16, T), BF16)
    dscr("csr", (16, T), BF16)
    dscr("ofm", (DM, T), BF16)
    dscr("cfm", (512, T), BF16)
    dscr("kfm", (DM, T), BF16)
    dscr("vpack1", (16, 128, 32, 65), BF16)
    dscr("cexp", (8, T), BF16)
    dscr("scoresf", (1, T), F32)

    def dv(name, idx, k=0):
        h, bl = DRAM[name]
        return APB(h[idx], bl[k])
    def dvm(name, idx, ks):
        h, bl = DRAM[name]
        return h[idx], [bl[k] for k in ks]

    SB = {}
    def sal(name, shape, dt):
        h = nc.alloc_sbuf_tensor(name, list(shape), dt)
        SB[name] = (h, Buf(name))
        return SB[name]
    def sv(name, idx=np.s_[:, :]):
        h, b = SB[name]
        return APB(h[idx], b)

    for i in range(12):
        sal("A%d" % i, (128, 4096), BF16)
    sal("HT0", (128, NKC * TT), BF16)
    sal("HT1", (128, NKC * TT), BF16)
    sal("XT0", (128, TT), F32)
    sal("XT1", (128, TT), F32)
    sal("SQ0", (128, TT), F32)
    sal("SQ1", (128, TT), F32)
    sal("HID", (128, 16 * TT), BF16)
    sal("OUTS", (128, 8 * TT), BF16)
    sal("OCB2", (128, 512), BF16)
    sal("CBC", (128, TT), BF16)
    sal("SBC", (128, TT), F32)
    sal("ROWS", (1, 2048), F32)
    sal("CSB", (8, T), BF16)
    sal("XRED", (16, TT), BF16)
    sal("PSI", (128, TT), F32)
    sal("SUR", (128, TT), F32)
    sal("GT", (128, 80), F32)
    sal("DBCF", (128, 256), F32)
    sal("DBCS", (128, 256), F32)
    sal("DEVN", (1, 512), F32)
    sal("DSLW", (1, 512), F32)
    sal("CCT0", (128, TT), BF16)
    sal("CCT1", (128, TT), BF16)
    sal("SST0", (128, TT), BF16)
    sal("SST1", (128, TT), BF16)
    sal("MKSB", (128, 4 * 512), BF16)
    sal("IDENT", (128, 128), F32)
    sal("ONESF", (128, 128), F32)
    sal("ONESB", (128, 128), BF16)
    sal("C1", (128, 8), F32)
    sal("WGN", (16, 8), BF16)
    sal("WREDS", (128, 8 * 16), BF16)
    sal("B2SB", (8, DM), BF16)
    sal("B1T", (128, 16), F32)
    sal("GROW", (1, 80), BF16)
    sal("E2SB", (2, 128), BF16)
    sal("LSMALL", (4, 64), F32)
    sal("BOUT", (128, 8), F32)
    sal("CROW", (1, 512), BF16)
    sal("G1T", (128, 8), F32)
    sal("G2T", (128, 8), F32)
    sal("TMPA", (128, TT), F32)
    sal("TMPB", (128, TT), F32)
    sal("KTV", (128, 1024), BF16)
    sal("VTV", (128, 1024), BF16)
    sal("R16A", (128, 16), F32)
    sal("R16B", (128, 16), F32)
    sal("R16C", (128, 16), F32)
    sal("R16D", (128, 16), F32)
    sal("STF", (128, 64), F32)
    sal("STS", (128, 64), F32)
    sal("STFB", (128, 64), BF16)
    sal("STSB", (128, 64), BF16)
    sal("KSL", (128, 128), BF16)
    sal("VSL", (128, 128), BF16)
    sal("CFB", (128, 512), BF16)
    sal("CSB2", (128, 512), BF16)
    sal("CF2", (2, 512), BF16)
    sal("CS2", (2, 512), BF16)
    sal("OCB", (128, 520), BF16)
    sal("PTS", (128, 2048), BF16)

    used = 224 * 1024 - 16 * 1024 - nc.sbuf_bytes_remaining if hasattr(nc, "sbuf_bytes_remaining") else -1

    PS = []
    for i in range(8):
        h = nc.alloc_psum_tensor("ps%d" % i, [128, 512], F32)
        PS.append((h, Buf("ps%d" % i)))
    def ps(i, idx=np.s_[:, :]):
        h, b = PS[i]
        return APB(h[idx], b)

    ONE_F = lambda idx=np.s_[:, 0:1]: sv("ONESF", idx)
    ONE_B = lambda idx=np.s_[:, 0:1]: sv("ONESB", idx)

    P.mset(sv("ONESF"), 1.0)
    P.mset(sv("ONESB"), 1.0)
    ih, ib = eap("ident")
    P.dma("sp", sv("IDENT"), APB(ih[:, :], ib))
    P.mset(sv("C1", np.s_[:, 1:2]), -5.0)
    P.mset(sv("C1", np.s_[:, 2:3]), 1e-5)
    P.mset(sv("C1", np.s_[:, 3:4]), 1e-6)
    SKIP_LAM = globals().get("_SKIP_LAM", False)
    lh, lb = eap("lrow")
    P.dma("sp", sv("LSMALL"), APB(lh[:, :], lb))
    if SKIP_LAM:
        P.mset(sv("C1", np.s_[:, 0:1]), -0.5)
    if not SKIP_LAM:
      P.tp(ps(0, np.s_[0:64, 0:4]), sv("LSMALL"), sv("IDENT", np.s_[0:4, 0:4]))
      LT = sv("TMPA", np.s_[0:64, 0:4])
      P.cp(LT, ps(0, np.s_[0:64, 0:4]))
      P.mm(ps(1, np.s_[0:1, 0:1]), APB(SB["TMPA"][0][0:64, 0:1], SB["TMPA"][1]),
         APB(SB["TMPA"][0][0:64, 1:2], SB["TMPA"][1]))
      P.mm(ps(1, np.s_[0:1, 1:2]), APB(SB["TMPA"][0][0:64, 2:3], SB["TMPA"][1]),
         APB(SB["TMPA"][0][0:64, 3:4], SB["TMPA"][1]))
      P.act(sv("ROWS", np.s_[0:1, 0:2]), ps(1, np.s_[0:1, 0:2]), AF.Exp)
      P.tt(sv("ROWS", np.s_[0:1, 2:3]), sv("ROWS", np.s_[0:1, 0:1]), sv("ROWS", np.s_[0:1, 1:2]), ALU.subtract)
      P.ts(sv("ROWS", np.s_[0:1, 2:3]), sv("ROWS", np.s_[0:1, 2:3]), 0.8, ALU.add, 0.1, ALU.max)
      P.ts(sv("ROWS", np.s_[0:1, 2:3]), sv("ROWS", np.s_[0:1, 2:3]), 0.9, ALU.min, -1.0, ALU.mult)
      P.mm(ps(1, np.s_[0:128, 2:3]), ONE_F(np.s_[0:1, 0:128]), sv("ROWS", np.s_[0:1, 2:3]))
      P.cp(sv("C1", np.s_[:, 0:1]), ps(1, np.s_[0:128, 2:3]))
    mh, mb = eap("mk")
    P.dma("sp", APB(SB["MKSB"][0].ap().rearrange("p (a q) -> p a q", q=512), SB["MKSB"][1]),
          APB(mh.ap().rearrange("a p q -> p a q"), mb))
    e2h, e2b = eap("e2")
    P.dma("sp", sv("E2SB"), APB(e2h[:, :], e2b))
    for _n, _s in (("bout_t", "BOUT"), ("g1t", "G1T"), ("g2t", "G2T")):
        _h, _b = eap(_n)
        P.dma("sp", sv(_s), APB(_h[:, :], _b))

    XV = {"x": None}

    def x_apb(name, kc, t0, t1, k):
        if name == "x":
            h, b = eap("x")
            return APB(h[kc * 128:(kc + 1) * 128, t0:t1], b)
        if name == "y":
            return APB(y[kc * 128:(kc + 1) * 128, t0:t1], y_b[k])
        return dv(name, np.s_[kc * 128:(kc + 1) * 128, t0:t1], k)

    def load_fm_tile(queue, dst_name, dram_name, t, nkc=NKC, cw=TT):
        h, bl = DRAM[dram_name]
        hs, bs = SB[dst_name]
        src3 = h.ap().rearrange("(kc p) q -> p kc q", p=128)[:, :, t * cw:(t + 1) * cw]
        dst3 = hs[:, 0:nkc * cw].rearrange("p (kc q) -> p kc q", q=cw)
        P.dma(queue, APB(dst3, bs), APB(src3, bl[t]))

    def ln_phase(src, kind="ln", g=None, ln_out=None):
        eps_ln, eps_rms = 1e-5, 1e-6
        hidf = SB["HID"][0].ap().bitcast(F32)
        hidb = SB["HID"][1]
        xt_ = lambda kc: APB(hidf[:, kc * TT:(kc + 1) * TT], hidb)
        for t in range(NTT):
            t0, t1 = t * TT, (t + 1) * TT
            SQ = [sv("SQ0"), sv("SQ1")]
            h_, bl_ = (EXT["x"][0], [EXT["x"][1]] * NTT) if src == "x" else DRAM[src]
            src3 = h_.ap().rearrange("(kc p) q -> p kc q", p=128)[:, :, t0:t1]
            dst3 = APB(hidf.rearrange("p (kc q) -> p kc q", q=TT), hidb)
            P.dma("sp", dst3, APB(src3, bl_[t]))
            for kc in range(NKC):
                xs = xt_(kc)
                if kind != "rms":
                    P.mm(ps(0, np.s_[0:1, :]), ONE_F(), xs, start=(kc == 0), stop=(kc == NKC - 1))
                P.act(SQ[kc % 2], xs, AF.Square)
                P.mm(ps(1, np.s_[0:1, :]), ONE_F(), SQ[kc % 2], start=(kc == 0), stop=(kc == NKC - 1))
            R = lambda a, b_: sv("ROWS", np.s_[0:1, a:b_])
            if kind == "rms":
                P.act(R(0, 512), ps(1, np.s_[0:1, :]), AF.Ln, bias=sv("C1", np.s_[0:1, 3:4]), scale=1.0 / DM)
                P.act(R(0, 512), R(0, 512), AF.Exp, scale=-0.5)
                P.mm(ps(2, np.s_[:, :]), ONE_F(np.s_[0:1, 0:128]), R(0, 512))
                P.cp(sv("TMPA"), ps(2, np.s_[:, :]), eng="act")
                for kc in range(NKC):
                    xs = xt_(kc)
                    gcol = sv(g, np.s_[:, kc:kc + 1])
                    P.stt(sv("HT0" if t % 2 == 0 else "HT1",
                             np.s_[:, kc * TT:(kc + 1) * TT]), xs, gcol, sv("TMPA"),
                          ALU.mult, ALU.mult)
                    P.dma("act", dv("hbf", np.s_[kc * 128:(kc + 1) * 128, t0:t1], t),
                          sv("HT0" if t % 2 == 0 else "HT1", np.s_[:, kc * TT:(kc + 1) * TT]),
                          dst_name="hbf@%d" % (t % 2))
                continue
            P.ts(R(0, 512), ps(0, np.s_[0:1, :]), 1.0 / DM, ALU.mult)
            P.ts(R(512, 1024), ps(1, np.s_[0:1, :]), 1.0 / DM, ALU.mult)
            P.tt(R(1024, 1536), R(0, 512), R(0, 512), ALU.mult)
            P.tt(R(512, 1024), R(512, 1024), R(1024, 1536), ALU.subtract)
            P.act(R(1024, 1536), R(512, 1024), AF.Ln, bias=sv("C1", np.s_[0:1, 2:3]))
            P.act(R(1024, 1536), R(1024, 1536), AF.Exp, scale=-0.5)
            if kind == "ln+rms":
                P.tt(R(1536, 2048), R(1024, 1536), R(1024, 1536), ALU.mult)
                P.tt(R(1536, 2048), R(1536, 2048), R(512, 1024), ALU.mult)
                P.act(R(1536, 2048), R(1536, 2048), AF.Ln, bias=sv("C1", np.s_[0:1, 3:4]))
                P.act(R(1536, 2048), R(1536, 2048), AF.Exp, scale=-0.5)
            P.tt(R(512, 1024), R(0, 512), R(1024, 1536), ALU.mult)
            P.mm(ps(2, np.s_[:, :]), ONE_F(np.s_[0:1, 0:128]), R(1024, 1536))
            P.mm(ps(3, np.s_[:, :]), ONE_F(np.s_[0:1, 0:128]), R(512, 1024))
            P.cp(sv("TMPA"), ps(2, np.s_[:, :]), eng="act")
            P.cp(sv("TMPB"), ps(3, np.s_[:, :]), eng="act")
            if kind == "ln+rms":
                P.mm(ps(2, np.s_[:, :]), ONE_F(np.s_[0:1, 0:128]), R(1536, 2048))
                P.cp(sv("SBC"), ps(2, np.s_[:, :]), eng="act")
            HT = "HT0" if t % 2 == 0 else "HT1"
            for kc in range(NKC):
                xs = xt_(kc)
                P.tt(sv("SQ0") if kc % 2 == 0 else sv("SQ1"), xs, sv("TMPA"), ALU.mult)
                tv = sv("SQ0") if kc % 2 == 0 else sv("SQ1")
                if kind == "ln":
                    P.tt(sv(HT, np.s_[:, kc * TT:(kc + 1) * TT]), tv, sv("TMPB"), ALU.subtract)
                    P.dma("act", dv("hbf", np.s_[kc * 128:(kc + 1) * 128, t0:t1], t),
                          sv(HT, np.s_[:, kc * TT:(kc + 1) * TT]),
                          dst_name="hbf@%d" % (t % 2))
                else:
                    P.tt(tv, tv, sv("TMPB"), ALU.subtract)
                    P.dma("sp", dv(ln_out, np.s_[kc * 128:(kc + 1) * 128, t0:t1], t), tv,
                          dst_name="%s@%d" % (ln_out, t % 2))
                    gcol = sv(g, np.s_[:, kc:kc + 1])
                    P.stt(sv(HT, np.s_[:, kc * TT:(kc + 1) * TT]), tv, gcol, sv("SBC"),
                          ALU.mult, ALU.mult)
                    P.dma("act", dv("hbf", np.s_[kc * 128:(kc + 1) * 128, t0:t1], t),
                          sv(HT, np.s_[:, kc * TT:(kc + 1) * TT]),
                          dst_name="hbf@%d" % (t % 2))

    def wload(ext_name, slots, rows=DM, cols=DM, row_off=0, e=None):
        h, b = EXT[ext_name]
        nkc = rows // 128
        per = nkc // len(slots)
        assert per * len(slots) == nkc
        for si, sl in enumerate(slots):
            hsl, bsl = SB[sl]
            src = h.ap() if e is None else h[e]
            src = src.rearrange("(kc p) m -> p kc m", p=128)
            P.dma("pool", APB(hsl[:, 0:per * cols], bsl),
                  APB(src[:, row_off // 128 + si * per:(row_off // 128) + (si + 1) * per, 0:cols], b))
        def acc(kc, c0, c1):
            si, ko = kc // per, kc % per
            hsl, bsl = SB[slots[si]]
            return APB(hsl[:, ko * cols + c0:ko * cols + c1], bsl)
        return acc

    if has("p1"):
        ln_phase("x", "ln")

    if has("p2"):
        _mark_p2 = True
        wq1a = wload("wq1a", ["A0", "A1"])
        wq1b = wload("wq1b", ["A2", "A3"])
        wq2a = wload("wq2a", ["A4", "A5"])
        wq2b = wload("wq2b", ["A6", "A7"])
        wk1a = wload("wk1a", ["A8"], cols=128)
        def wk_load(name, slot, coff):
            h, b = EXT[name]
            hsl, bsl = SB[slot]
            src = h.ap().rearrange("(kc p) m -> p kc m", p=128)
            P.dma("pool", APB(hsl[:, coff:coff + 1024], bsl), APB(src, b))
            def acc(kc, c0, c1):
                return APB(hsl[:, coff + kc * 128 + c0:coff + kc * 128 + c1], bsl)
            return acc
        wk1b = wk_load("wk1b", "A8", 1024)
        wk2a = wk_load("wk2a", "A8", 2048)
        wk2b = wk_load("wk2b", "A8", 3072)
        wv0 = wk_load("wv0", "A9", 0)
        cch, ccb = eap("cc")
        ssh, ssb = eap("ss")
        for t in range(NTT):
            t0, t1 = t * TT, (t + 1) * TT
            HT = "HT0" if t % 2 == 0 else "HT1"
            load_fm_tile("sp", HT, "hbf", t)
            CCT = "CCT0" if t % 2 == 0 else "CCT1"
            SST = "SST0" if t % 2 == 0 else "SST1"
            P.dma("sp", sv(CCT), APB(cch[:, t0:t1], ccb))
            P.dma("sp", sv(SST), APB(ssh[:, t0:t1], ssb))
            hT = lambda kc: sv(HT, np.s_[:, kc * TT:(kc + 1) * TT])
            def rope_proj(wa, wb_, dst, mchunks, kdim):
                for m in range(mchunks):
                    for kc in range(NKC):
                        P.mm(ps(0 + (m % 2) * 2, np.s_[:, :]), wa(kc, m * 128, (m + 1) * 128),
                             hT(kc), start=(kc == 0), stop=(kc == NKC - 1))
                        P.mm(ps(1 + (m % 2) * 2, np.s_[:, :]), wb_(kc, m * 128, (m + 1) * 128),
                             hT(kc), start=(kc == 0), stop=(kc == NKC - 1))
                    ob = sv("OCB", np.s_[:, 0:512]) if m % 2 == 0 else sv("OCB2")
                    P.tt(sv("TMPA"), ps(0 + (m % 2) * 2), sv(CCT), ALU.mult)
                    P.tt(sv("TMPB"), ps(1 + (m % 2) * 2), sv(SST), ALU.mult)
                    P.tt(ob, sv("TMPA"), sv("TMPB"), ALU.add)
                    P.dma("act", dv(dst, np.s_[m * 128:(m + 1) * 128, t0:t1], t), ob)
            rope_proj(wq1a, wq1b, "q1r", 8, DM)
            rope_proj(wq2a, wq2b, "q2r", 8, DM)
            rope_proj(wk1a, wk1b, "k1r", 1, 128)
            rope_proj(wk2a, wk2b, "k2r", 1, 128)
            for s4 in range(4):
                for kc in range(NKC):
                    P.mm(ps(4, np.s_[:, 0:128]),
                         sv(HT, np.s_[:, kc * TT + s4 * 128:kc * TT + (s4 + 1) * 128]),
                         wv0(kc, 0, 128), start=(kc == 0), stop=(kc == NKC - 1))
                P.mset(sv("OCB", np.s_[:, 0:130]), 1.0, eng="pool")
                ocv = SB["OCB"][0][:, 0:130].rearrange("p (g k) -> p g k", k=65)
                P.cp(APB(ocv[:, 0:2, 0:64], SB["OCB"][1]),
                     APB(PS[4][0][:, 0:128].rearrange("p (g k) -> p g k", k=64), PS[4][1]),
                     eng="act")
                c = t * 4 + s4
                vp, vpb = DRAM["vpack0"]
                for g in range(2):
                    P.dma("act", APB(vp[g, :, c, :], vpb[t]),
                          APB(ocv[:, g, 0:65], SB["OCB"][1]))

    if has("p3"):
        HP_N, WIN_N, P3_PARTS = P3_LIMIT
        q1h, q1bl = DRAM["q1r"]
        q2h, q2bl = DRAM["q2r"]
        P.dma("sp", sv("A2"), APB(DRAM["k1r"][0].ap(), DRAM["k1r"][1]))
        P.dma("sp", sv("A3"), APB(DRAM["k2r"][0].ap(), DRAM["k2r"][1]))
        vph, vpbl = DRAM["vpack0"]
        P.dma("sp", sv("A4", np.s_[:, 0:2080]),
              APB(vph[0].rearrange("p c k -> p (c k)"), vpbl))
        P.dma("sp", sv("A5", np.s_[:, 0:2080]),
              APB(vph[1].rearrange("p c k -> p (c k)"), vpbl))
        neglam = lambda: sv("C1", np.s_[0:64, 0:1])
        for hp in range(HP_N):
            P.dma("sp", sv("A0"), APB(q1h[hp * 128:(hp + 1) * 128, :], q1bl))
            P.dma("sp", sv("A1"), APB(q2h[hp * 128:(hp + 1) * 128, :], q2bl))
            def emit_S3(n):
                w0 = 128 * n
                for si, (kt, qt) in enumerate((("A2", "A0"), ("A3", "A1"))):
                    for ks in range(2):
                        for head, base in ((0, 0), (1, 64)):
                            bk, co = head * 2 + si, ks * 256
                            P.mm(ps(bk, np.s_[:, co:co + 256]),
                                 sv(kt, np.s_[base:base + 64, w0 + ks * 128:w0 + (ks + 1) * 128]),
                                 sv(qt, np.s_[base:base + 64, w0:w0 + 256]))
            PTS2 = lambda n, a, b: sv("PTS" if n % 2 == 0 else "HID", np.s_[:, a:b])
            def emit_exp3(n):
                for si in range(2):
                    for head in range(2):
                        bk = head * 2 + si
                        idx2 = si * 2 + head
                        P.act(PTS2(n, idx2 * 512, (idx2 + 1) * 512),
                              ps(bk, np.s_[:, :]), AF.Exp, scale=0.125)
            def emit_PV3(n):
                ob = 4 if n % 2 == 0 else 6
                for si in range(2):
                    for head in range(2):
                        idx2 = si * 2 + head
                        vsl = "A4" if head == 0 else "A5"
                        for ks in range(2):
                            P.mm(ps(ob + idx2 // 2,
                                    np.s_[0:65, (idx2 % 2) * 256:(idx2 % 2) * 256 + 256]),
                                 sv(vsl, np.s_[:, (n + ks) * 65:(n + ks + 1) * 65]),
                                 PTS2(n, idx2 * 512 + ks * 256, idx2 * 512 + ks * 256 + 256),
                                 start=(ks == 0), stop=(ks == 1))
            def emit_tail3(n):
                ob = 4 if n % 2 == 0 else 6
                rb = 6 if n % 2 == 0 else 4
                rdst = [("SBC", 0), ("TMPA", 0)]
                for bi in range(2):
                    rn, rc = rdst[bi]
                    P.act(sv(rn, np.s_[64:65, 0:512]), ps(ob + bi, np.s_[64:65, :]), AF.Ln)
                    P.act(sv(rn, np.s_[64:65, 0:512]), sv(rn, np.s_[64:65, 0:512]),
                          AF.Exp, scale=-1.0)
                for idx2 in range(4):
                    rn, rc = rdst[idx2 // 2]
                    P.mm(ps(rb + idx2 // 2, np.s_[0:64, (idx2 % 2) * 256:(idx2 % 2) * 256 + 256]),
                         ONE_F(np.s_[64:65, 0:64]),
                         sv(rn, np.s_[64:65, (idx2 % 2) * 256:(idx2 % 2) * 256 + 256]))
                for idx2 in range(4):
                    si, head = idx2 // 2, idx2 % 2
                    dstn = "SQ0" if si == 0 else "SQ1"
                    P.cp(sv(dstn, np.s_[0:64, head * 256:head * 256 + 256]),
                         ps(rb + idx2 // 2, np.s_[0:64, (idx2 % 2) * 256:(idx2 % 2) * 256 + 256]))
                for head in range(2):
                    tmpn = "TMPB" if head == 0 else "TMPA"
                    o1i, o2i = head, 2 + head
                    P.tt(sv(tmpn, np.s_[0:64, 0:256]),
                         ps(ob + o1i // 2, np.s_[0:64, (o1i % 2) * 256:(o1i % 2) * 256 + 256]),
                         sv("SQ0", np.s_[0:64, head * 256:head * 256 + 256]), ALU.mult)
                    P.tt(sv(tmpn, np.s_[0:64, 256:512]),
                         ps(ob + o2i // 2, np.s_[0:64, (o2i % 2) * 256:(o2i % 2) * 256 + 256]),
                         sv("SQ1", np.s_[0:64, head * 256:head * 256 + 256]), ALU.mult)
                    P.stt(sv("OCB", np.s_[0:64, head * 256:head * 256 + 256]),
                          sv(tmpn, np.s_[0:64, 256:512]), neglam(),
                          sv(tmpn, np.s_[0:64, 0:256]), ALU.mult, ALU.add)
                a0h, a0bl = DRAM["a0"]
                for head in range(2):
                    P.dma("act", APB(a0h[hp * 128 + head * 64:hp * 128 + head * 64 + 64,
                                         256 * n:256 * n + 256], a0bl[n // 2]),
                          sv("OCB", np.s_[0:64, head * 256:head * 256 + 256]))
            emit_S3(0)
            for n in range(WIN_N):
                emit_exp3(n)
                if n + 1 < WIN_N:
                    emit_S3(n + 1)
                emit_PV3(n)
                emit_tail3(n)

    def proj_residual_tile(t, act_name, w_acc, src_x, dst_x, bias_col=None, banks=(0, 1)):
        t0, t1 = t * TT, (t + 1) * TT
        HT = "HT0" if t % 2 == 0 else "HT1"
        load_fm_tile("sp", HT, act_name, t)
        for m in range(NKC):
            bk = banks[m % 2]
            for kc in range(NKC):
                P.mm(ps(bk, np.s_[:, :]), w_acc(kc, m * 128, (m + 1) * 128),
                     sv(HT, np.s_[:, kc * TT:(kc + 1) * TT]),
                     start=(kc == 0), stop=(kc == NKC - 1))
            xv = sv("XT0") if m % 2 == 0 else sv("XT1")
            P.dma("pool", xv, x_apb(src_x, m, t0, t1, t))
            ov = sv("SQ0") if m % 2 == 0 else sv("SQ1")
            if bias_col is not None:
                P.stt(ov, ps(bk, np.s_[:, :]), bias_col(m), xv, ALU.add, ALU.add)
            else:
                P.tt(ov, ps(bk, np.s_[:, :]), xv, ALU.add)
            P.dma("act", x_apb(dst_x, m, t0, t1, t), ov,
                  dst_name="%s@%d" % (dst_x, t % 2))

    def proj_residual(act_name, w_acc, src_x, dst_x, bias_col=None):
        for t in range(NTT):
            proj_residual_tile(t, act_name, w_acc, src_x, dst_x, bias_col)

    if has("p4"):
        wout = wload("wout0", ["A0", "A1"])
        proj_residual("a0", wout, "x", "xA",
                      bias_col=lambda m: sv("BOUT", np.s_[:, m:m + 1]))

    if has("p5"):
        ln_phase("xA", "ln")

    def moe_gating_l0():
        wgh, wgb = eap("wg")
        wgth, wgtb = eap("wgt")
        P.dma("sp", sv("R16A", np.s_[0:8, 0:16]), APB(wgh[:, :], wgb))
        P.dma("sp", sv("R16B", np.s_[0:16, 0:8]), APB(wgth[:, :], wgtb))
        P.act(sv("R16C", np.s_[0:8, 0:16]), sv("R16A", np.s_[0:8, 0:16]), AF.Square)
        P.red(sv("R16D", np.s_[0:8, 0:1]), sv("R16C", np.s_[0:8, 0:16]), ALU.add)
        P.act(sv("R16D", np.s_[0:8, 0:1]), sv("R16D", np.s_[0:8, 0:1]), AF.Sqrt)
        P.ts(sv("R16D", np.s_[0:8, 0:1]), sv("R16D", np.s_[0:8, 0:1]), 1e-4, ALU.max)
        P.recip(sv("R16D", np.s_[0:8, 0:1]), sv("R16D", np.s_[0:8, 0:1]))
        P.tp(ps(0, np.s_[0:1, 0:8]), sv("R16D", np.s_[0:8, 0:1]), sv("IDENT", np.s_[0:8, 0:8]))
        P.cp(sv("ROWS", np.s_[0:1, 0:8]), ps(0, np.s_[0:1, 0:8]))
        P.mm(ps(1, np.s_[0:16, 0:8]), ONE_F(np.s_[0:1, 0:16]), sv("ROWS", np.s_[0:1, 0:8]))
        P.tt(sv("WGN"), sv("R16B", np.s_[0:16, 0:8]), ps(1, np.s_[0:16, 0:8]), ALU.mult)
        wrh, wrb = eap("wred")
        P.dma("pool", sv("WREDS"),
              APB(wrh.ap().rearrange("(kc p) m -> p kc m", p=128), wrb))
        hb, hbl = DRAM["hbf"]
        sc_h, sc_bl = DRAM["scoresf"]
        for t in range(NTT):
            t0, t1 = t * TT, (t + 1) * TT
            HT = "HT0" if t % 2 == 0 else "HT1"
            load_fm_tile("sp", HT, "hbf", t)
            for kc in range(NKC):
                P.mm(ps(0, np.s_[0:16, :]),
                     APB(SB["WREDS"][0][:, kc * 16:(kc + 1) * 16], SB["WREDS"][1]),
                     sv(HT, np.s_[:, kc * TT:(kc + 1) * TT]),
                     start=(kc == 0), stop=(kc == NKC - 1))
            P.cp(sv("XRED", np.s_[0:16, 0:TT]), ps(0, np.s_[0:16, :]), eng="act")
            for s4 in range(4):
                c = t * 4 + s4
                P.mm(ps(1, np.s_[0:128, 0:8]),
                     sv("XRED", np.s_[0:16, s4 * 128:(s4 + 1) * 128]), sv("WGN"))
                P.red(sv("R16A", np.s_[:, 0:1]), ps(1, np.s_[0:128, 0:8]), ALU.max, negate=True)
                P.red(sv("R16A", np.s_[:, 3:4]), ps(1, np.s_[0:128, 0:8]), ALU.max)
                P.act(sv("R16B", np.s_[0:128, 0:8]), ps(1, np.s_[0:128, 0:8]), AF.Exp,
                      bias=sv("R16A", np.s_[:, 0:1]))
                P.red(sv("R16A", np.s_[:, 1:2]), sv("R16B", np.s_[0:128, 0:8]), ALU.add)
                P.recip(sv("R16A", np.s_[:, 2:3]), sv("R16A", np.s_[:, 1:2]))
                P.ts(sv("R16C", np.s_[0:128, 0:8]), ps(1, np.s_[0:128, 0:8]),
                     sv("R16A", np.s_[:, 3:4]), ALU.is_ge)
                P.dma("act", APB(sc_h[0:1, 128 * c:128 * c + 128].rearrange("a k -> k a"),
                                 sc_bl[c // 4]), sv("R16A", np.s_[:, 2:3]))
                if t == 0 and s4 == 0 and globals().get("_P6DBG"):
                    for nm2 in ("WGN", "XRED", "R16A", "R16B", "R16C"):
                        hh2, bb2 = SB[nm2]
                        dd = nc.dram_tensor("dbg6_" + nm2, list(hh2.shape), hh2.dtype,
                                            kind="ExternalOutput")
                        P.dma("sp", APB(dd.ap(), Buf()), APB(hh2.ap(), bb2))
                P.tp(ps(2, np.s_[0:8, 0:128]), sv("R16C", np.s_[0:128, 0:8]),
                     sv("IDENT"))
                P.cp(sv("CSB", np.s_[0:8, 128 * c:128 * c + 128]), ps(2, np.s_[0:8, 0:128]),
                     eng="act")
        ch, cbl = DRAM["cexp"]
        for t in range(NTT):
            P.dma("sp", APB(ch[:, t * TT:(t + 1) * TT], cbl[t]),
                  sv("CSB", np.s_[:, t * TT:(t + 1) * TT]), dst_name="cexp@%d" % (t % 2))

    def moe_experts(layer):
        w1n, w2n = ("w1_0", "w2_0") if layer == 0 else ("w1_1", "w2_1")
        hbl = DRAM["hbf"][1]
        hb = DRAM["hbf"][0]
        ch, cbl = DRAM["cexp"]
        mh, mbl = DRAM["macc"]
        if layer == 0:
            b2h, b2b = eap("b2_0")
            P.dma("pool", sv("B2SB"), APB(b2h[:, :], b2b))
            b1h, b1b = eap("b1t_0")
        w1slots = [["A0", "A1", "A2", "A3"], ["A8", "A9", "A10", "A11"]]
        w2slots = ["A4", "A5", "A6", "A7"]
        def wexp_load(name, e, slots, ncol):
            h, b = EXT[name]
            nkc = h.shape[1] // 128
            per = nkc // len(slots)
            for si, sl in enumerate(slots):
                hsl, bsl = SB[sl]
                src = h[e].rearrange("(kc p) m -> p kc m", p=128)
                P.dma("pool", APB(hsl[:, 0:per * ncol], bsl),
                      APB(src[:, per * si:per * si + per, :], b))
            def acc(kc, c0, c1):
                hsl, bsl = SB[slots[kc // per]]
                off = (kc % per) * ncol
                return APB(hsl[:, off + c0:off + c1], bsl)
            return acc
        w1a = [None, None]
        w1a[0] = wexp_load(w1n, 0, w1slots[0], 2048)
        for e in range(8):
            if e < 7:
                w1a[(e + 1) % 2] = wexp_load(w1n, e + 1, w1slots[(e + 1) % 2], 2048)
            w2a = wexp_load(w2n, e, w2slots, 1024)
            w1 = w1a[e % 2]
            if layer == 0:
                P.dma("sp", sv("B1T"), APB(b1h[e], b1b))
            for t in range(NTT):
                t0, t1 = t * TT, (t + 1) * TT
                HT = "HT0" if t % 2 == 0 else "HT1"
                load_fm_tile("sp", HT, "hbf", t)
                P.dma("sp", sv("CROW"), APB(ch[e:e + 1, t0:t1], cbl[t]))
                P.mm(ps(6, np.s_[:, :]), ONE_B(np.s_[0:1, 0:128]), sv("CROW"))
                P.cp(sv("CBC"), ps(6, np.s_[:, :]), eng="act")
                for hc in range(16):
                    bank = 4 + (hc % 2)
                    for kc in range(NKC):
                        P.mm(ps(bank, np.s_[:, :]), w1(kc, hc * 128, (hc + 1) * 128),
                             sv(HT, np.s_[:, kc * TT:(kc + 1) * TT]),
                             start=(kc == 0), stop=(kc == NKC - 1))
                    hv = sv("HID", np.s_[:, hc * TT:(hc + 1) * TT])
                    if layer == 0:
                        P.act(hv, ps(bank, np.s_[:, :]), AF.Gelu, bias=sv("B1T", np.s_[:, hc:hc + 1]))
                    else:
                        P.act(hv, ps(bank, np.s_[:, :]), AF.Silu)
                    P.tt(hv, hv, sv("CBC"), ALU.mult)
                half = (t % 2) * 4
                for m in range(NKC):
                    bank = m % 4
                    for hc in range(16):
                        P.mm(ps(bank, np.s_[:, :]), w2a(hc, m * 128, (m + 1) * 128),
                             sv("HID", np.s_[:, hc * TT:(hc + 1) * TT]),
                             start=(hc == 0), stop=(hc == 15 and not (layer == 0 and e == 0)))
                    if layer == 0 and e == 0:
                        P.mm(ps(bank, np.s_[:, :]),
                             APB(SB["B2SB"][0][0:8, m * 128:(m + 1) * 128], SB["B2SB"][1]),
                             sv("CSB", np.s_[0:8, t0:t1]), start=False, stop=True)
                    P.cp(sv("OUTS", np.s_[:, (half + bank) * TT:(half + bank + 1) * TT]),
                         ps(bank, np.s_[:, :]), eng="dve")
                    if m % 4 == 3:
                        g4 = m // 4
                        dst = mh[g4 * 512:(g4 + 1) * 512, t0:t1].rearrange(
                            "(mc p) q -> p mc q", p=128)
                        src3 = SB["OUTS"][0][:, half * TT:(half + 4) * TT].rearrange(
                            "p (mc q) -> p mc q", q=TT)
                        P.dma("pool", APB(dst, mbl[t]), APB(src3, SB["OUTS"][1]),
                              accum=(None if e == 0 else ALU.add),
                              dst_name="macc@%d" % (t % 2))

    def moe_epilogue_l0():
        mh, mbl = DRAM["macc"]
        sc_h, sc_bl = DRAM["scoresf"]
        for t in range(NTT):
            t0, t1 = t * TT, (t + 1) * TT
            P.dma("sp", sv("ROWS", np.s_[0:1, 0:512]), APB(sc_h[0:1, t0:t1], sc_bl[t]))
            P.mm(ps(7, np.s_[:, :]), ONE_F(np.s_[0:1, 0:128]), sv("ROWS", np.s_[0:1, 0:512]))
            P.cp(sv("SBC"), ps(7, np.s_[:, :]), eng="act")
            for m in range(NKC):
                mv = sv("CFB") if m % 2 == 0 else sv("CSB2")
                P.dma("sp", mv, APB(mh[m * 128:(m + 1) * 128, t0:t1], mbl[t]))
                xv = sv("XT0") if m % 2 == 0 else sv("XT1")
                P.dma("pool", xv, x_apb("xA", m, t0, t1, t))
                tv = sv("TMPA") if m % 2 == 0 else sv("TMPB")
                P.tt(tv, mv, sv("SBC"), ALU.mult)
                P.tt(tv, tv, xv, ALU.add)
                P.dma("act", x_apb("xB", m, t0, t1, t), tv, dst_name="xB@%d" % (t % 2))

    if has("p6"):
        moe_gating_l0()
        moe_experts(0)
        moe_epilogue_l0()

    def p8_l1_proj():
        wq = wload("wq_r", ["A0", "A1"])
        wk = wload("wk_r", ["A2", "A3"])
        wv = wload("wv_r", ["A4", "A5"])
        wg5h, wg5b = EXT["wg5"]
        hsl6, bsl6 = SB["A6"]
        P.dma("pool", APB(hsl6[:, 0:640], bsl6),
              APB(wg5h.ap().rearrange("(kc p) m -> p kc m", p=128), wg5b))
        wg5 = lambda kc: APB(hsl6[:, kc * 80:(kc + 1) * 80], bsl6)
        grh, grb = eap("brow5")
        P.dma("pool", sv("GROW"), APB(grh[:, :], grb))
        hb, hbl = DRAM["hbf"]
        qh, qbl = DRAM["qfm"]
        kprh, kprbl = DRAM["kpr"]
        vprh, vprbl = DRAM["vpr"]
        cfh, cfbl = DRAM["cfr"]
        csh, csbl = DRAM["csr"]
        NEG5 = sv("C1", np.s_[:, 1:2])
        for t in range(NTT):
            t0, t1 = t * TT, (t + 1) * TT
            HT = "HT0" if t % 2 == 0 else "HT1"
            load_fm_tile("sp", HT, "hbf", t)
            hT = lambda kc: sv(HT, np.s_[:, kc * TT:(kc + 1) * TT])
            hTs = lambda kc, s4: sv(HT, np.s_[:, kc * TT + s4 * 128:kc * TT + (s4 + 1) * 128])
            for m in range(NKC):
                bank = 2 + (m % 2)
                for kc in range(NKC):
                    P.mm(ps(bank, np.s_[:, :]), wq(kc, m * 128, (m + 1) * 128), hT(kc),
                         start=(kc == 0), stop=(kc == NKC - 1))
                ob = sv("OCB", np.s_[:, 0:512]) if m % 2 == 0 else sv("OCB2")
                P.act(ob, ps(bank, np.s_[:, :]), AF.Silu)
                P.dma("act", APB(qh[m * 128:(m + 1) * 128, t0:t1], qbl[t]), ob)
            for s4 in range(4):
                c = t * 4 + s4
                for kc in range(NKC):
                    P.mm(ps(0, np.s_[0:128, 0:80]), hTs(kc, s4), wg5(kc),
                         start=(kc == 0), stop=False)
                P.mm(ps(0, np.s_[0:128, 0:80]), ONE_B(np.s_[0:1, 0:128]), sv("GROW"),
                     start=False, stop=True)
                P.act(sv("GT"), ps(0, np.s_[0:128, 0:80]), AF.Sigmoid)
                for half, (dst, wacc) in enumerate((("KTV", wk), ("VTV", wv))):
                    pass
                for half in range(2):
                    for name_, wacc, bank in (("KTV", wk, 4), ("VTV", wv, 5)):
                        for kc in range(NKC):
                            P.mm(ps(bank, np.s_[:, :]), hTs(kc, s4),
                                 wacc(kc, half * 512, (half + 1) * 512),
                                 start=(kc == 0), stop=(kc == NKC - 1))
                        dstv = sv(name_, np.s_[:, half * 512:(half + 1) * 512])
                        P.act(dstv, ps(bank, np.s_[:, :]), AF.Silu)
                        bview = SB["GT"][0][:, half * 8:(half + 1) * 8].rearrange(
                            "p (h o) -> p h o", o=1).broadcast_to([128, 8, 64])
                        dv3 = SB[name_][0][:, half * 512:(half + 1) * 512].rearrange(
                            "p (h k) -> p h k", k=64)
                        P.tt(APB(dv3, SB[name_][1]), APB(dv3, SB[name_][1]),
                             APB(bview, SB["GT"][1]), ALU.mult)
                for hp in range(8):
                    name_ = "KTV" if True else None
                    P.dma("act", APB(kprh[hp, :, c, :], kprbl[t]),
                          sv("KTV", np.s_[:, hp * 128:(hp + 1) * 128]))
                    P.dma("act", APB(vprh[hp, :, c, :], vprbl[t]),
                          sv("VTV", np.s_[:, hp * 128:(hp + 1) * 128]))
                for half, tmpn in ((0, "TMPA"), (1, "TMPB")):
                    P.act(sv(tmpn), sv("KTV", np.s_[:, half * 512:(half + 1) * 512]), AF.Square)
                    P.red(sv("R16A", np.s_[:, half * 8:(half + 1) * 8]),
                          APB(SB[tmpn][0].ap().rearrange("p (h k) -> p h k", k=64), SB[tmpn][1]),
                          ALU.add)
                P.act(sv("R16A"), sv("R16A"), AF.Sqrt)
                P.ts(sv("R16A"), sv("R16A"), 1e-8, ALU.add)
                P.recip(sv("R16B"), sv("R16A"))
                for half, tmpn in ((0, "TMPA"), (1, "TMPB")):
                    P.act(sv(tmpn), sv("VTV", np.s_[:, half * 512:(half + 1) * 512]), AF.Square)
                    P.red(sv("R16A", np.s_[:, half * 8:(half + 1) * 8]),
                          APB(SB[tmpn][0].ap().rearrange("p (h k) -> p h k", k=64), SB[tmpn][1]),
                          ALU.add)
                P.act(sv("R16A"), sv("R16A"), AF.Sqrt)
                P.ts(sv("R16A"), sv("R16A"), 1e-8, ALU.add)
                P.recip(sv("R16C"), sv("R16A"))
                for half, tmpn in ((0, "TMPA"), (1, "TMPB")):
                    P.tt(sv(tmpn), sv("KTV", np.s_[:, half * 512:(half + 1) * 512]),
                         sv("VTV", np.s_[:, half * 512:(half + 1) * 512]), ALU.mult)
                    P.red(sv("R16D", np.s_[:, half * 8:(half + 1) * 8]),
                          APB(SB[tmpn][0].ap().rearrange("p (h k) -> p h k", k=64), SB[tmpn][1]),
                          ALU.add)
                P.tt(sv("R16D"), sv("R16D"), sv("R16B"), ALU.mult)
                P.tt(sv("R16D"), sv("R16D"), sv("R16C"), ALU.mult)
                P.act(sv("R16D"), sv("R16D"), AF.Abs)
                PSIc = sv("PSI", np.s_[:, c * 16:(c + 1) * 16])
                SURc = sv("SUR", np.s_[:, c * 16:(c + 1) * 16])
                P.act(PSIc, sv("R16D"), AF.Sigmoid, scale=3.0)
                P.act(SURc, PSIc, AF.Sigmoid, scale=10.0, bias=NEG5)
                for dn, gofs, coef, drow in (("f", 16, -0.1, "DEVN"), ("s", 32, -0.05, "DSLW")):
                    P.ts(sv("R16A"), PSIc, coef, ALU.mult, 1.0, ALU.add)
                    P.tt(sv("R16B"), APB(SB["GT"][0][:, gofs:gofs + 16], SB["GT"][1]),
                         sv("R16A"), ALU.mult)
                    P.mm(ps(1, np.s_[0:1, 0:16]), ONE_F(), sv("R16B"))
                    P.act(sv("ROWS", np.s_[0:1, 1024:1040]), ps(1, np.s_[0:1, 0:16]),
                          AF.Ln, scale=1.0 / 128)
                    P.act(sv("ROWS", np.s_[0:1, 1024:1040]), sv("ROWS", np.s_[0:1, 1024:1040]),
                          AF.Exp, scale=128.0)
                    P.cp(sv(drow, np.s_[0:1, c * 8:(c + 1) * 8]),
                         sv("ROWS", np.s_[0:1, 1024:1040:2]))
                    P.cp(sv(drow, np.s_[0:1, 256 + c * 8:256 + (c + 1) * 8]),
                         sv("ROWS", np.s_[0:1, 1025:1040:2]))
                P.ts(sv("R16A"), PSIc, 0.3, ALU.mult, 0.5, ALU.add)
                P.tt(sv("R16B"), sv("R16A"), APB(SB["GT"][0][:, 48:64], SB["GT"][1]), ALU.mult)
                P.ts(sv("R16C"), sv("R16A"), -1.0, ALU.mult, 1.0, ALU.add)
                P.tt(sv("R16C"), sv("R16C"), APB(SB["GT"][0][:, 64:80], SB["GT"][1]), ALU.mult)
                P.tp(ps(1, np.s_[0:16, 0:128]), sv("R16B"), sv("IDENT"))
                P.cp(sv("CBC", np.s_[0:16, 0:128]), ps(1, np.s_[0:16, 0:128]), eng="act")
                P.dma("act", APB(cfh[:, c * 128:(c + 1) * 128], cfbl[t]),
                      sv("CBC", np.s_[0:16, 0:128]))
                P.tp(ps(1, np.s_[0:16, 0:128]), sv("R16C"), sv("IDENT"))
                P.cp(sv("CBC", np.s_[0:16, 0:128]), ps(1, np.s_[0:16, 0:128]), eng="act")
                P.dma("act", APB(csh[:, c * 128:(c + 1) * 128], csbl[t]),
                      sv("CBC", np.s_[0:16, 0:128]))
        for drow, dbc in (("DEVN", "DBCF"), ("DSLW", "DBCS")):
            P.mm(ps(6, np.s_[0:64, 0:256]), ONE_F(np.s_[0:1, 0:64]),
                 sv(drow, np.s_[0:1, 0:256]))
            P.mm(ps(6, np.s_[64:128, 0:256]), ONE_F(np.s_[0:1, 0:64]),
                 sv(drow, np.s_[0:1, 256:512]), tile_position=(0, 64))
            P.cp(sv(dbc), ps(6, np.s_[:, 0:256]))

    def p9_recurrence():
        kprh, kprbl = DRAM["kpr"]
        vprh, vprbl = DRAM["vpr"]
        qh, qbl = DRAM["qfm"]
        ofh, ofbl = DRAM["ofm"]
        cfh, cfbl = DRAM["cfr"]
        csh, csbl = DRAM["csr"]
        for hp in range(8):
            sA = ["A0", "A1", "A2"] if hp % 2 == 0 else ["A3", "A4", "A5"]
            kst, vst, qst = sA
            P.dma("sp", sv(kst), APB(kprh[hp].rearrange("p c n -> p (c n)"), kprbl))
            P.dma("sp", sv(vst), APB(vprh[hp].rearrange("p c n -> p (c n)"), vprbl))
            P.dma("sp", sv(qst), APB(qh[hp * 128:(hp + 1) * 128, :], qbl))
            P.mset(sv("STF"), 0.0)
            P.mset(sv("STS"), 0.0)
            for c in range(32):
                cb = np.s_[:, c * 128:(c + 1) * 128]
                sview = SB["SUR"][0][:, c * 16 + 2 * hp:c * 16 + 2 * hp + 2].rearrange(
                    "p (h o) -> p h o", o=1).broadcast_to([128, 2, 64])
                for src_, dst_ in ((kst, "KSL"), (vst, "VSL")):
                    v3 = SB[src_][0][:, c * 128:(c + 1) * 128].rearrange(
                        "p (h k) -> p h k", k=64)
                    d3 = SB[dst_][0][:, :].rearrange("p (h k) -> p h k", k=64)
                    P.tt(APB(d3, SB[dst_][1]), APB(v3, SB[src_][1]),
                         APB(sview, SB["SUR"][1]), ALU.mult, eng="pool")
                for head in range(2):
                    o0 = c * 128 + head * 64
                    tp_ = None if head == 0 else (0, 64)
                    osl = np.s_[0:64, 0:64] if head == 0 else np.s_[64:128, 0:64]
                    P.mm(ps(0, osl), APB(SB[vst][0][:, o0:o0 + 64], SB[vst][1]),
                         APB(SB[kst][0][:, o0:o0 + 64], SB[kst][1]), tile_position=tp_)
                    P.mm(ps(1, osl), APB(SB["VSL"][0][:, head * 64:head * 64 + 64], SB["VSL"][1]),
                         APB(SB["KSL"][0][:, head * 64:head * 64 + 64], SB["KSL"][1]),
                         tile_position=tp_)
                P.stt(sv("STF"), sv("STF"), sv("DBCF", np.s_[:, c * 8 + hp:c * 8 + hp + 1]),
                      ps(0, np.s_[:, 0:64]), ALU.mult, ALU.add)
                P.cp(sv("STFB"), sv("STF"), eng="act")
                P.stt(sv("STS"), sv("STS"), sv("DBCS", np.s_[:, c * 8 + hp:c * 8 + hp + 1]),
                      ps(1, np.s_[:, 0:64]), ALU.mult, ALU.add)
                P.cp(sv("STSB"), sv("STS"), eng="act")
                co = np.s_[0:64, (c % 4) * 128:(c % 4) * 128 + 128]
                co2 = np.s_[64:128, (c % 4) * 128:(c % 4) * 128 + 128]
                qA = APB(SB[qst][0][0:64, c * 128:(c + 1) * 128], SB[qst][1])
                qB = APB(SB[qst][0][64:128, c * 128:(c + 1) * 128], SB[qst][1])
                P.mm(ps(2, co), sv("STFB", np.s_[0:64, :]), qA)
                P.mm(ps(3, co2), sv("STFB", np.s_[64:128, :]), qB, tile_position=(64, 64))
                P.mm(ps(4, co), sv("STSB", np.s_[0:64, :]), qA)
                P.mm(ps(5, co2), sv("STSB", np.s_[64:128, :]), qB, tile_position=(64, 64))
                if c % 4 == 3:
                    g = c // 4
                    g0, g1 = g * 512, (g + 1) * 512
                    P.dma("sp", sv("CF2"), APB(cfh[2 * hp:2 * hp + 2, g0:g1], cfbl[g]))
                    P.dma("sp", sv("CS2"), APB(csh[2 * hp:2 * hp + 2, g0:g1], csbl[g]))
                    P.mm(ps(6, np.s_[:, :]), sv("E2SB"), sv("CF2"))
                    P.mm(ps(7, np.s_[:, :]), sv("E2SB"), sv("CS2"))
                    P.cp(sv("CFB"), ps(6, np.s_[:, :]), eng="act")
                    P.cp(sv("CSB2"), ps(7, np.s_[:, :]), eng="act")
                    P.tt(sv("TMPA", np.s_[0:64, :]), ps(2, np.s_[0:64, :]),
                         sv("CFB", np.s_[0:64, :]), ALU.mult)
                    P.tt(sv("TMPA", np.s_[64:128, :]), ps(3, np.s_[64:128, :]),
                         sv("CFB", np.s_[64:128, :]), ALU.mult)
                    P.tt(sv("TMPB", np.s_[0:64, :]), ps(4, np.s_[0:64, :]),
                         sv("CSB2", np.s_[0:64, :]), ALU.mult)
                    P.tt(sv("TMPB", np.s_[64:128, :]), ps(5, np.s_[64:128, :]),
                         sv("CSB2", np.s_[64:128, :]), ALU.mult)
                    ob = sv("OCB", np.s_[:, 0:512]) if g % 2 == 0 else sv("OCB2")
                    P.tt(ob, sv("TMPA"), sv("TMPB"), ALU.add)
                    P.dma("act", APB(ofh[hp * 128:(hp + 1) * 128, g0:g1], ofbl[g]), ob,
                          dst_name="ofm9@%d" % (g % 2))

    if has("p7"):
        ln_phase("xB", "ln")
    if has("p8"):
        p8_l1_proj()
    if has("p9"):
        p9_recurrence()
    if has("p10"):
        wo = wload("wo_r", ["A6", "A7"])
        proj_residual("ofm", wo, "xB", "xA")

    def p12_mla_proj():
        wqm = wload("wq_m", ["A0", "A1"])
        wdkvh, wdkvb = EXT["wdkv"]
        h2, b2_ = SB["A2"]
        P.dma("pool", APB(h2[:, 0:4096], b2_),
              APB(wdkvh.ap().rearrange("(kc p) m -> p kc m", p=128), wdkvb))
        wdkv = lambda kc, c0, c1: APB(h2[:, kc * 512 + c0:kc * 512 + c1], b2_)
        wukh, wukb = EXT["wuk"]
        h3_, b3_ = SB["A3"]
        P.dma("pool", APB(h3_[:, 0:4096], b3_),
              APB(wukh.ap().rearrange("(kc p) m -> p kc m", p=128), wukb))
        wuk = lambda kc, c0, c1: APB(h3_[:, kc * 1024 + c0:kc * 1024 + c1], b3_)
        wuvh, wuvb = EXT["wuv"]
        h4_, b4_ = SB["A4"]
        P.dma("pool", APB(h4_[:, 0:4096], b4_),
              APB(wuvh.ap().rearrange("(kc p) m -> p kc m", p=128), wuvb))
        wuv = lambda kc, c0, c1: APB(h4_[:, kc * 1024 + c0:kc * 1024 + c1], b4_)
        hb, hbl = DRAM["hbf"]
        qh, qbl = DRAM["qfm"]
        ch, cbl = DRAM["cfm"]
        kh, kbl = DRAM["kfm"]
        vph, vpbl = DRAM["vpack1"]
        for t in range(NTT):
            t0, t1 = t * TT, (t + 1) * TT
            HT = "HT0" if t % 2 == 0 else "HT1"
            load_fm_tile("sp", HT, "hbf", t)
            hT = lambda kc: sv(HT, np.s_[:, kc * TT:(kc + 1) * TT])
            for m in range(NKC):
                bank = 0 + (m % 2)
                for kc in range(NKC):
                    P.mm(ps(bank, np.s_[:, :]), wqm(kc, m * 128, (m + 1) * 128), hT(kc),
                         start=(kc == 0), stop=(kc == NKC - 1))
                ob = sv("OCB", np.s_[:, 0:512]) if m % 2 == 0 else sv("OCB2")
                P.cp(ob, ps(bank, np.s_[:, :]), eng="act")
                P.dma("act", APB(qh[m * 128:(m + 1) * 128, t0:t1], qbl[t]), ob)
            for m in range(4):
                bank = 2 + (m % 2)
                for kc in range(NKC):
                    P.mm(ps(bank, np.s_[:, :]), wdkv(kc, m * 128, (m + 1) * 128), hT(kc),
                         start=(kc == 0), stop=(kc == NKC - 1))
                P.cp(sv("HID", np.s_[:, m * TT:(m + 1) * TT]), ps(bank, np.s_[:, :]), eng="act")
                P.dma("act", APB(ch[m * 128:(m + 1) * 128, t0:t1], cbl[t]),
                      sv("HID", np.s_[:, m * TT:(m + 1) * TT]))
            cT = lambda cc: sv("HID", np.s_[:, cc * TT:(cc + 1) * TT])
            for m in range(NKC):
                bank = 4 + (m % 2)
                for cc in range(4):
                    P.mm(ps(bank, np.s_[:, :]), wuk(cc, m * 128, (m + 1) * 128), cT(cc),
                         start=(cc == 0), stop=(cc == 3))
                ob = sv("OCB", np.s_[:, 0:512]) if m % 2 == 0 else sv("OCB2")
                P.cp(ob, ps(bank, np.s_[:, :]), eng="act")
                P.dma("act", APB(kh[m * 128:(m + 1) * 128, t0:t1], kbl[t]), ob)
            for s4 in range(4):
                c = t * 4 + s4
                for half in range(2):
                    bank = 6 + half
                    for cc in range(4):
                        P.mm(ps(bank, np.s_[:, :]),
                             sv("HID", np.s_[:, cc * TT + s4 * 128:cc * TT + (s4 + 1) * 128]),
                             wuv(cc, half * 512, (half + 1) * 512),
                             start=(cc == 0), stop=(cc == 3))
                    P.mset(sv("OCB", np.s_[:, 0:520]), 1.0, eng="pool")
                    ocv = SB["OCB"][0][:, 0:520].rearrange("p (g k) -> p g k", k=65)
                    P.cp(APB(ocv[:, 0:8, 0:64], SB["OCB"][1]),
                         APB(PS[bank][0][:, :].rearrange("p (g k) -> p g k", k=64), PS[bank][1]),
                         eng="act")
                    for j in range(8):
                        head = half * 8 + j
                        P.dma("act", APB(vph[head, :, c, :], vpbl[t]),
                              APB(ocv[:, j, 0:65], SB["OCB"][1]))

    def p13_mla_attention():
        qh, qbl = DRAM["qfm"]
        kh, kbl = DRAM["kfm"]
        vph, vpbl = DRAM["vpack1"]
        ofh, ofbl = DRAM["ofm"]
        wom = wload("wo_m", ["A0", "A1"])
        for qt in range(NTT):
            q0, q1 = qt * TT, (qt + 1) * TT
            nkt = 4 * qt + 4
            for hp in range(8):
                ksl = "A5" if hp % 2 == 0 else "A2"
                va = "A6" if hp % 2 == 0 else "A3"
                vb = "A7" if hp % 2 == 0 else "A4"
                P.dma("sp", sv(ksl, np.s_[:, 0:nkt * 128]),
                      APB(kh[hp * 128:(hp + 1) * 128, 0:nkt * 128], kbl))
                P.dma("sp", sv(va, np.s_[:, 0:nkt * 65]),
                      APB(vph[2 * hp, :, 0:nkt, :].rearrange("p c k -> p (c k)"), vpbl))
                P.dma("sp", sv(vb, np.s_[:, 0:nkt * 65]),
                      APB(vph[2 * hp + 1, :, 0:nkt, :].rearrange("p c k -> p (c k)"), vpbl))
                qb = "CBC" if hp % 2 == 0 else "CFB"
                P.dma("sp", sv(qb), APB(qh[hp * 128:(hp + 1) * 128, q0:q1], qbl[qt]))
                def emit_S(kt):
                    par = kt % 2
                    for head, base in ((0, 0), (1, 64)):
                        bank = (0 if par == 0 else 4) + head
                        P.mm(ps(bank, np.s_[:, :]),
                             sv(ksl, np.s_[base:base + 64, kt * 128:(kt + 1) * 128]),
                             sv(qb, np.s_[base:base + 64, :]))
                def emit_exp(kt):
                    par = kt % 2
                    for head in range(2):
                        bank = (0 if par == 0 else 4) + head
                        pts = np.s_[:, par * 1024 + head * 512:par * 1024 + (head + 1) * 512]
                        P.act(sv("PTS", pts), ps(bank, np.s_[:, :]), AF.Exp, scale=0.125)
                        if kt >= 4 * qt:
                            i = kt - 4 * qt
                            P.tt(sv("PTS", pts), sv("PTS", pts),
                                 sv("MKSB", np.s_[:, i * 512:(i + 1) * 512]), ALU.mult)
                def emit_PV(kt):
                    par = kt % 2
                    for head in range(2):
                        pts = np.s_[:, par * 1024 + head * 512:par * 1024 + (head + 1) * 512]
                        vsl = va if head == 0 else vb
                        P.mm(ps(2 + head, np.s_[0:65, :]),
                             sv(vsl, np.s_[:, kt * 65:(kt + 1) * 65]),
                             sv("PTS", pts), start=(kt == 0), stop=(kt == nkt - 1))
                emit_S(0)
                for kt in range(nkt):
                    emit_exp(kt)
                    if kt + 1 < nkt:
                        emit_S(kt + 1)
                    emit_PV(kt)
                for head, (rn, sq) in enumerate((("SBC", "SQ0"), ("TMPA", "SQ1"))):
                    P.act(sv(rn, np.s_[64:65, :]), ps(2 + head, np.s_[64:65, :]), AF.Ln)
                    P.act(sv(rn, np.s_[64:65, :]), sv(rn, np.s_[64:65, :]), AF.Exp, scale=-1.0)
                    P.mm(ps(4 + head, np.s_[0:64, :]), ONE_F(np.s_[64:65, 0:64]),
                         sv(rn, np.s_[64:65, :]))
                    P.cp(sv(sq, np.s_[0:64, :]), ps(4 + head, np.s_[0:64, :]))
                    ob = sv("OCB", np.s_[0:64, 0:512]) if head == 0 else sv("OCB2", np.s_[0:64, :])
                    P.tt(ob, ps(2 + head, np.s_[0:64, :]), sv(sq, np.s_[0:64, :]), ALU.mult)
                    P.dma("act", APB(ofh[hp * 128 + head * 64:hp * 128 + head * 64 + 64, q0:q1],
                                     ofbl[qt]), ob, dst_name="ofm@%d" % (qt % 2))
            proj_residual_tile(qt, "ofm", wom, "xC", "xB", banks=(6, 7))

    def moe_gating_l1():
        gwh, gwb = EXT["gatew"]
        P.dma("pool", sv("WREDS", np.s_[:, 0:64]),
              APB(gwh.ap().rearrange("(kc p) m -> p kc m", p=128), gwb))
        hb, hbl = DRAM["hbf"]
        for t in range(NTT):
            t0, t1 = t * TT, (t + 1) * TT
            HT = "HT0" if t % 2 == 0 else "HT1"
            load_fm_tile("sp", HT, "hbf", t)
            for s4 in range(4):
                c = t * 4 + s4
                for kc in range(NKC):
                    P.mm(ps(1, np.s_[0:128, 0:8]),
                         sv(HT, np.s_[:, kc * TT + s4 * 128:kc * TT + (s4 + 1) * 128]),
                         APB(SB["WREDS"][0][:, kc * 8:(kc + 1) * 8], SB["WREDS"][1]),
                         start=(kc == 0), stop=(kc == NKC - 1))
                P.red(sv("R16A", np.s_[:, 1:2]), ps(1, np.s_[0:128, 0:8]), ALU.max)
                P.ts(sv("R16B", np.s_[:, 0:8]), ps(1, np.s_[0:128, 0:8]),
                     sv("R16A", np.s_[:, 1:2]), ALU.is_ge)
                P.stt(sv("R16C", np.s_[:, 0:8]), sv("R16B", np.s_[:, 0:8]), -1e30,
                      ps(1, np.s_[0:128, 0:8]), ALU.mult, ALU.add)
                P.red(sv("R16A", np.s_[:, 2:3]), sv("R16C", np.s_[:, 0:8]), ALU.max)
                P.ts(sv("R16D", np.s_[:, 0:8]), sv("R16C", np.s_[:, 0:8]),
                     sv("R16A", np.s_[:, 2:3]), ALU.is_ge)
                P.tt(sv("R16A", np.s_[:, 3:4]), sv("R16A", np.s_[:, 1:2]),
                     sv("R16A", np.s_[:, 2:3]), ALU.subtract)
                P.act(sv("R16A", np.s_[:, 4:5]), sv("R16A", np.s_[:, 3:4]), AF.Sigmoid)
                P.ts(sv("R16A", np.s_[:, 5:6]), sv("R16A", np.s_[:, 4:5]), -1.0,
                     ALU.mult, 1.0, ALU.add)
                P.ts(sv("R16B", np.s_[:, 0:8]), sv("R16B", np.s_[:, 0:8]),
                     sv("R16A", np.s_[:, 4:5]), ALU.mult)
                P.ts(sv("R16D", np.s_[:, 0:8]), sv("R16D", np.s_[:, 0:8]),
                     sv("R16A", np.s_[:, 5:6]), ALU.mult)
                P.tt(sv("R16C", np.s_[:, 0:8]), sv("R16B", np.s_[:, 0:8]),
                     sv("R16D", np.s_[:, 0:8]), ALU.add)
                P.tp(ps(2, np.s_[0:8, 0:128]), sv("R16C", np.s_[0:128, 0:8]), sv("IDENT"))
                P.cp(sv("CSB", np.s_[0:8, 128 * c:128 * c + 128]), ps(2, np.s_[0:8, 0:128]),
                     eng="act")
        ch, cbl = DRAM["cexp"]
        for t in range(NTT):
            P.dma("sp", APB(ch[:, t * TT:(t + 1) * TT], cbl[t]),
                  sv("CSB", np.s_[:, t * TT:(t + 1) * TT]), dst_name="cexp@%d" % (t % 2))

    def final_epilogue():
        mh, mbl = DRAM["macc"]
        for t in range(NTT):
            t0, t1 = t * TT, (t + 1) * TT
            for m in range(NKC):
                mv = sv("CFB") if m % 2 == 0 else sv("CSB2")
                P.dma("sp", mv, APB(mh[m * 128:(m + 1) * 128, t0:t1], mbl[t]))
                xv = sv("XT0") if m % 2 == 0 else sv("XT1")
                P.dma("pool", xv, x_apb("xA", m, t0, t1, t))
                bv = sv("TMPA") if m % 2 == 0 else sv("TMPB")
                P.dma("act", bv, x_apb("xB", m, t0, t1, t))
                ov = sv("SQ0") if m % 2 == 0 else sv("SQ1")
                P.tt(ov, mv, xv, ALU.add)
                P.tt(ov, ov, bv, ALU.add)
                P.dma("act", x_apb("y", m, t0, t1, t), ov, dst_name="y@%d" % (t % 2))

    if has("p11"):
        ln_phase("xA", "ln+rms", g="G1T", ln_out="xC")
    if has("p12"):
        p12_mla_proj()
    if has("p13"):
        p13_mla_attention()
    if has("p15"):
        ln_phase("xB", "rms", g="G2T")
    if has("p16"):
        moe_gating_l1()
        moe_experts(1)
        final_epilogue()



    return nc, P, EXT, DRAM, SB, y


def _emit_dbg(nc, P, DRAM, dbg):
    outs = []
    for name in dbg:
        h, bl = DRAM[name]
        sh = list(h.shape) if hasattr(h, "shape") else None
        o = nc.dram_tensor("dbg_" + name, sh, h.dtype)
    return outs


def _build_full(phases=None, dbg=()):
    nc, P, EXT, DRAM, SB, y = build(phases, dbg)
    for name in dbg:
        h, bl = DRAM[name]
        o = nc.dram_tensor("dbg_" + name, list(h.shape), h.dtype, kind="ExternalOutput")
        P.dma("sp", APB(o.ap(), Buf("dbg")), APB(h.ap(), bl[0]))
        for b in bl[1:]:
            P._record("sp", lambda E: E.nop(), [b], [])
    P.finalize_and_emit()
    return nc


def kernel(**inputs):
    S = _host_prep(inputs)
    x = np.asarray(inputs["L0_attn_x"], dtype=np.float32)
    nc = _build_full()
    in_maps = []
    for b in range(8):
        m = dict(S)
        m["x"] = np.ascontiguousarray(x[b].T)
        in_maps.append(m)
    res = run_bass_kernel_spmd(nc, in_maps, core_ids=list(range(8)))
    out = np.stack([np.asarray(r["y"]).T for r in res.results], axis=0)
    return out


# version 11
# speedup vs baseline: 1.0171x; 1.0171x over previous
import numpy as np
import ml_dtypes
from contextlib import ExitStack

import concourse.bass as bass
import concourse.mybir as mybir
from concourse.bass_utils import run_bass_kernel_spmd

F32 = mybir.dt.float32
BF16 = mybir.dt.bfloat16
AF = mybir.ActivationFunctionType
ALU = mybir.AluOpType
AX = mybir.AxisListType
BFNP = ml_dtypes.bfloat16

P3_LIMIT = (8, 16, 5)
T = 4096
DM = 1024
NKC = 8
TT = 512
NTT = 8


class Buf:
    __slots__ = ("w", "rd", "name")
    def __init__(self, name=""):
        self.w = None
        self.rd = {}
        self.name = name

class APB:
    __slots__ = ("ap", "buf")
    def __init__(self, ap, buf):
        self.ap = ap
        self.buf = buf

def _a(x):
    return x.ap if isinstance(x, APB) else x

def _small(*xs):
    for x in xs:
        if isinstance(x, APB):
            try:
                if x.ap.free_size < 128:
                    return True
            except Exception:
                return True
    return False

def _bufs(*xs):
    out = []
    for x in xs:
        if isinstance(x, APB) and x.buf is not None:
            if isinstance(x.buf, list):
                out.extend(x.buf)
            else:
                out.append(x.buf)
    return out

class Rec:
    __slots__ = ("emit", "waits", "eff", "tick", "is_dma", "inc")
    def __init__(self, emit, waits, eff, tick, is_dma):
        self.emit = emit
        self.waits = waits
        self.eff = eff
        self.tick = tick
        self.is_dma = is_dma
        self.inc = None

ENG_ORDER = ["pe", "dve", "act", "pool", "sp"]
ENG_SECTION = {"pe": "tensor", "dve": "vector", "act": "scalar",
               "pool": "gpsimd", "sp": "sync"}

class Prog:
    def __init__(self, nc):
        self.nc = nc
        self.streams = list(ENG_ORDER)
        self.sidx = {n: i for i, n in enumerate(self.streams)}
        self.cnt = [0] * 5
        self.known = [dict() for _ in range(5)]
        self.recs = {n: [] for n in ENG_ORDER}
        self.needed = None
        self.dma_issuer = {}

    def _stream_for_dma(self, queue, dst_name):
        key = "q:%s:%s" % (queue, dst_name)
        if key not in self.sidx:
            self.sidx[key] = len(self.streams)
            self.streams.append(key)
            self.cnt.append(0)
            self.dma_issuer[self.sidx[key]] = self.sidx[queue]
        return self.sidx[key]

    def _record(self, eng, emit, reads, writes, eff=None, is_dma=False, small=True):
        ei = self.sidx[eng]
        if eff is None:
            eff = ei
        need = {}
        def add(st, t, src_small=True):
            if st == ei and st < 5:
                if ei == 0:
                    return
                if ei == 1 and not src_small:
                    return
            if st >= 5:
                t = self.cnt[st]
            if self.known[ei].get(st, 0) >= t:
                return
            if need.get(st, 0) < t:
                need[st] = t
        for b in reads:
            if b.w:
                add(b.w[0], b.w[1], b.w[2])
        for b in writes:
            if b.w:
                if not (is_dma and b.w[0] == eff):
                    add(b.w[0], b.w[1], b.w[2])
            for st, (t, sm) in b.rd.items():
                add(st, t, sm)
        for st, t in need.items():
            self.known[ei][st] = t
        if is_dma:
            self.cnt[eff] += 16
        else:
            self.cnt[eff] += 1
        tick = self.cnt[eff]
        rec = Rec(emit, sorted(need.items()), eff, tick, is_dma)
        self.recs[eng].append(rec)
        for b in reads:
            prev = b.rd.get(eff)
            b.rd[eff] = (tick, small or (prev[1] if prev else False))
        for b in writes:
            b.w = (eff, tick, small)
            b.rd = {}
        return rec

    def mm(self, out, lhsT, rhs, start=True, stop=True, tile_position=None):
        oa, la, ra = _a(out), _a(lhsT), _a(rhs)
        tp_ = tile_position
        def emit(E, oa=oa, la=la, ra=ra, start=start, stop=stop, tp_=tp_):
            return E.matmul(oa, la, ra, start=start, stop=stop,
                            tile_position=tp_, skip_group_check=True)
        self._record("pe", emit, _bufs(lhsT, rhs), _bufs(out), small=False)

    def tp(self, out, in_, ident):
        oa, ia, da = _a(out), _a(in_), _a(ident)
        def emit(E, oa=oa, ia=ia, da=da):
            return E.transpose(oa, ia, da)
        self._record("pe", emit, _bufs(in_, ident), _bufs(out), small=False)

    def act(self, out, in_, func, bias=0.0, scale=1.0, eng="act"):
        oa, ia = _a(out), _a(in_)
        ba, sa = _a(bias), _a(scale)
        def emit(E, oa=oa, ia=ia, func=func, ba=ba, sa=sa):
            return E.activation(oa, ia, func, bias=ba, scale=sa)
        self._record(eng, emit, _bufs(in_, bias, scale), _bufs(out),
                     small=_small(out, in_))

    def cp(self, out, in_, eng="dve"):
        oa, ia = _a(out), _a(in_)
        if eng == "act":
            def emit(E, oa=oa, ia=ia):
                return E.copy(oa, ia)
        else:
            def emit(E, oa=oa, ia=ia):
                return E.tensor_copy(oa, ia)
        self._record(eng, emit, _bufs(in_), _bufs(out), small=_small(out, in_))

    def tt(self, out, a, b, alu, eng="dve"):
        oa, aa, ba = _a(out), _a(a), _a(b)
        def emit(E, oa=oa, aa=aa, ba=ba, alu=alu):
            return E.tensor_tensor(oa, aa, ba, alu)
        self._record(eng, emit, _bufs(a, b), _bufs(out), small=_small(out, a, b))

    def ts(self, out, a, s1, op0, s2=None, op1=None, eng="dve"):
        oa, aa = _a(out), _a(a)
        s1a, s2a = _a(s1), _a(s2) if s2 is not None else None
        if s2 is None:
            def emit(E, oa=oa, aa=aa, s1a=s1a, op0=op0):
                return E.tensor_single_scalar(oa, aa, s1a, op0)
        else:
            def emit(E, oa=oa, aa=aa, s1a=s1a, s2a=s2a, op0=op0, op1=op1):
                return E.tensor_scalar(oa, aa, s1a, s2a, op0, op1)
        self._record(eng, emit, _bufs(a, s1, s2), _bufs(out), small=_small(out, a))

    def stt(self, out, in0, scalar, in1, op0, op1, eng="dve"):
        oa, i0, sc, i1 = _a(out), _a(in0), _a(scalar), _a(in1)
        def emit(E, oa=oa, i0=i0, sc=sc, i1=i1, op0=op0, op1=op1):
            return E.scalar_tensor_tensor(oa, i0, sc, i1, op0, op1)
        self._record(eng, emit, _bufs(in0, scalar, in1), _bufs(out),
                     small=_small(out, in0, in1))

    def red(self, out, in_, alu, negate=False, absval=False, eng="dve"):
        oa, ia = _a(out), _a(in_)
        def emit(E, oa=oa, ia=ia, alu=alu, negate=negate, absval=absval):
            return E.tensor_reduce(oa, ia, AX.X, alu,
                                   apply_absolute_value=absval, negate=negate)
        self._record(eng, emit, _bufs(in_), _bufs(out), small=_small(out, in_))

    def recip(self, out, in_):
        oa, ia = _a(out), _a(in_)
        def emit(E, oa=oa, ia=ia):
            return E.reciprocal(oa, ia)
        self._record("dve", emit, _bufs(in_), _bufs(out), small=_small(out, in_))

    def mset(self, dst, val, eng="pool"):
        oa = _a(dst)
        def emit(E, oa=oa, val=val):
            return E.memset(oa, val)
        self._record(eng, emit, [], _bufs(dst), small=_small(dst))

    def barrier(self):
        snap = [(st, self.cnt[st]) for st in range(len(self.streams)) if self.cnt[st] > 0]
        for eng in ENG_ORDER:
            ei = self.sidx[eng]
            waits = []
            for st, v in snap:
                if st == ei:
                    continue
                if self.known[ei].get(st, 0) >= v:
                    continue
                waits.append((st, v))
                self.known[ei][st] = v
            def emit(E):
                return E.nop()
            self.cnt[ei] += 1
            rec = Rec(emit, sorted(waits), ei, self.cnt[ei], False)
            self.recs[eng].append(rec)

    def dma(self, queue, dst, src, dst_name=None, accum=None):
        oa, ia = _a(dst), _a(src)
        if dst_name is None:
            dst_name = oa.tensor.name if hasattr(oa, "tensor") else "anon"
        st = self._stream_for_dma(queue, dst_name)
        if accum is None:
            def emit(E, oa=oa, ia=ia):
                return E.dma_start(oa, ia)
        else:
            def emit(E, oa=oa, ia=ia, accum=accum):
                return E.dma_start(oa, ia, accum_op=accum)
        self._record(queue, emit, _bufs(src), _bufs(dst), eff=st, is_dma=True)

    def finalize_and_emit(self):
        nc = self.nc
        needed = [set() for _ in range(len(self.streams))]
        for eng in ENG_ORDER:
            for rec in self.recs[eng]:
                for st, v in rec.waits:
                    if st < 5:
                        needed[st].add(v)
        import bisect
        ranks = {}
        for si in range(5):
            ranks[si] = sorted(needed[si])
        def remap(st, v):
            if st >= 5:
                return v
            return bisect.bisect_right(ranks[st], v)
        for eng in ENG_ORDER:
            ei = self.sidx[eng]
            for rec in self.recs[eng]:
                rec.waits = [(st, remap(st, v)) for st, v in rec.waits]
                if rec.is_dma:
                    rec.inc = 16
                elif rec.tick in needed[rec.eff]:
                    rec.inc = 1
        final_sp = [(st, self.cnt[st]) for st in range(5, len(self.streams))
                    if self.cnt[st] > 0]

        with ExitStack() as ctx:
            sems = []
            for i, name in enumerate(self.streams):
                sems.append(ctx.enter_context(nc.semaphore("sem%d" % i)))
            block = ctx.enter_context(nc.Block())
            for eng in ENG_ORDER:
                recs = self.recs[eng]
                if not recs and eng != "sp":
                    continue
                sec = getattr(block, ENG_SECTION[eng])
                def body(E, recs=recs, eng=eng):
                    for rec in recs:
                        for st, v in rec.waits:
                            E.wait_ge(sems[st], v)
                        ins = rec.emit(E)
                        if rec.inc:
                            ins.then_inc(sems[rec.eff], rec.inc)
                    if eng == "sp":
                        for st, v in final_sp:
                            E.wait_ge(sems[st], v)
                sec(body)
        return nc



def _hs_perms():
    p_hs = np.concatenate([np.arange(0, 64, 2), np.arange(1, 64, 2)])
    p_sw = np.concatenate([p_hs[32:], p_hs[:32]])
    return p_hs, p_sw

def _permute_cols(w, perm):
    nh = w.shape[1] // 64
    w2 = np.asarray(w).reshape(w.shape[0], nh, 64)
    return np.ascontiguousarray(w2[:, :, perm].reshape(w.shape[0], -1))

def _host_prep(I):
    S = {}
    f32 = lambda a: np.ascontiguousarray(np.asarray(a, dtype=np.float32))
    p_hs, p_sw = _hs_perms()
    S["wq1a"] = _permute_cols(I["L0_attn_Wq1"], p_hs)
    S["wq1b"] = _permute_cols(I["L0_attn_Wq1"], p_sw)
    S["wq2a"] = _permute_cols(I["L0_attn_Wq2"], p_hs)
    S["wq2b"] = _permute_cols(I["L0_attn_Wq2"], p_sw)
    S["wk1a"] = _permute_cols(I["L0_attn_Wk1"], p_hs)
    S["wk1b"] = _permute_cols(I["L0_attn_Wk1"], p_sw)
    S["wk2a"] = _permute_cols(I["L0_attn_Wk2"], p_hs)
    S["wk2b"] = _permute_cols(I["L0_attn_Wk2"], p_sw)
    S["wv0"] = f32(I["L0_attn_Wv"])
    S["wout0"] = f32(I["L0_attn_Wout"])
    S["bout_t"] = f32(np.asarray(I["L0_attn_bout"]).reshape(8, 128).T)
    S["lrow"] = f32(np.stack([I["L0_attn_lq1"], I["L0_attn_lk1"],
                              I["L0_attn_lq2"], I["L0_attn_lk2"]], 0))
    pos = np.arange(T, dtype=np.float64)
    div = np.exp(np.arange(0, 64, 2, dtype=np.float64) * (-np.log(10000.0) / 64.0))
    ang = div[:, None] * pos[None, :]
    cos = np.cos(ang).astype(np.float32)
    sin = np.sin(ang).astype(np.float32)
    S["cc"] = np.ascontiguousarray(np.tile(cos, (4, 1))).astype(BFNP)
    S["ss"] = np.ascontiguousarray(np.tile(np.concatenate([-sin, sin], 0), (2, 1))).astype(BFNP)
    S["wred"] = f32(I["L0_ffn_w_red"])
    S["wg"] = f32(I["L0_ffn_wg"])
    S["wgt"] = f32(np.asarray(I["L0_ffn_wg"]).T)
    S["w1_0"] = f32(I["L0_ffn_W1"])
    S["w2_0"] = f32(I["L0_ffn_W2"])
    S["b1t_0"] = f32(np.asarray(I["L0_ffn_b1"]).reshape(8, 16, 128).transpose(0, 2, 1))
    S["b2_0"] = f32(I["L0_ffn_b2"])
    S["wq_r"] = f32(I["L1_attn_Wq"])
    S["wk_r"] = f32(I["L1_attn_Wk"])
    S["wv_r"] = f32(I["L1_attn_Wv"])
    S["wo_r"] = f32(I["L1_attn_Wo"])
    wg5 = np.concatenate([I["L1_attn_Wb"], I["L1_attn_Wfd"], I["L1_attn_Wsd"],
                          I["L1_attn_Wfg"], I["L1_attn_Wsg"]], axis=1)
    S["wg5"] = f32(wg5)
    brow = np.zeros((1, 80), np.float32)
    brow[0, 16:32] = np.asarray(I["L1_attn_fd_bias"])
    brow[0, 32:48] = np.asarray(I["L1_attn_sd_bias"])
    S["brow5"] = brow
    S["g1t"] = f32(np.asarray(I["L1_ffn_g1"]).reshape(8, 128).T)
    S["g2t"] = f32(np.asarray(I["L1_ffn_g2"]).reshape(8, 128).T)
    S["wq_m"] = f32(I["L1_ffn_Wq"])
    S["wdkv"] = f32(I["L1_ffn_Wdkv"])
    S["wuk"] = f32(I["L1_ffn_Wuk"])
    S["wuv"] = f32(I["L1_ffn_Wuv"])
    S["wo_m"] = f32(I["L1_ffn_Wo"])
    S["gatew"] = f32(I["L1_ffn_gate_w"])
    S["w1_1"] = f32(I["L1_ffn_w1"])
    S["w2_1"] = f32(I["L1_ffn_w2"])
    S["ident"] = np.eye(128, dtype=np.float32)
    e2 = np.zeros((2, 128), np.float32)
    e2[0, 0:64] = 1.0
    e2[1, 64:128] = 1.0
    S["e2"] = e2.astype(BFNP)
    mk = np.zeros((4, 128, 512), np.float32)
    kp = np.arange(128)[:, None]
    qf = np.arange(512)[None, :]
    for i in range(4):
        mk[i] = (qf >= kp + 128 * i).astype(np.float32)
    S["mk"] = mk.astype(BFNP)
    return S



def build(phases=None, dbg=()):
    nc = bass.Bass("TRN2", target_bir_lowering=False)
    P = Prog(nc)

    def has(name):
        return phases is None or name in phases

    EXT = {}
    def ein(name, shape, dt=F32):
        h = nc.dram_tensor(name, list(shape), dt, kind="ExternalInput")
        EXT[name] = (h, Buf(name))
        return EXT[name]
    def eap(name):
        h, b = EXT[name]
        return h, b

    ein("x", (DM, T))
    for n, shape in [
        ("wq1a", (DM, DM)), ("wq1b", (DM, DM)), ("wq2a", (DM, DM)), ("wq2b", (DM, DM)),
        ("wk1a", (DM, 128)), ("wk1b", (DM, 128)), ("wk2a", (DM, 128)), ("wk2b", (DM, 128)),
        ("wv0", (DM, 128)), ("wout0", (DM, DM)), ("bout_t", (128, 8)), ("lrow", (4, 64)),
        ("wred", (DM, 16)), ("wg", (8, 16)), ("wgt", (16, 8)),
        ("w1_0", (8, DM, 2048)), ("w2_0", (8, 2048, DM)),
        ("b1t_0", (8, 128, 16)), ("b2_0", (8, DM)),
        ("wq_r", (DM, DM)), ("wk_r", (DM, DM)), ("wv_r", (DM, DM)), ("wo_r", (DM, DM)),
        ("wg5", (DM, 80)), ("brow5", (1, 80)),
        ("g1t", (128, 8)), ("g2t", (128, 8)),
        ("wq_m", (DM, DM)), ("wdkv", (DM, 512)), ("wuk", (512, DM)), ("wuv", (512, DM)),
        ("wo_m", (DM, DM)), ("gatew", (DM, 8)),
        ("w1_1", (8, DM, 2048)), ("w2_1", (8, 2048, DM)),
        ("ident", (128, 128)),
    ]:
        ein(n, shape)
    ein("e2", (2, 128), BF16)
    ein("cc", (128, T), BF16)
    ein("ss", (128, T), BF16)
    ein("mk", (4, 128, 512), BF16)

    y = nc.dram_tensor("y", [DM, T], F32, kind="ExternalOutput")
    y_b = [Buf("y%d" % i) for i in range(NTT)]

    DRAM = {}
    def dscr(name, shape, dt, nb=NTT):
        h = nc.dram_tensor(name, list(shape), dt)
        DRAM[name] = (h, [Buf("%s/%d" % (name, i)) for i in range(nb)])
        return DRAM[name]
    dscr("xA", (DM, T), F32)
    dscr("xB", (DM, T), F32)
    dscr("xC", (DM, T), F32)
    dscr("hbf", (DM, T), BF16)
    dscr("q1r", (DM, T), BF16)
    dscr("q2r", (DM, T), BF16)
    dscr("k1r", (128, T), BF16)
    dscr("k2r", (128, T), BF16)
    dscr("vpack0", (2, 128, 32, 65), BF16)
    dscr("a0", (DM, T), BF16)
    dscr("scores2d", (32, 128), F32, nb=NTT)
    dscr("macc", (DM, T), BF16)
    dscr("qfm", (DM, T), BF16)
    dscr("kpr", (8, 128, 32, 128), BF16)
    dscr("vpr", (8, 128, 32, 128), BF16)
    dscr("cfr", (16, T), BF16)
    dscr("csr", (16, T), BF16)
    dscr("ofm", (DM, T), BF16)
    dscr("cfm", (512, T), BF16)
    dscr("kfm", (DM, T), BF16)
    dscr("vpack1", (16, 128, 32, 65), BF16)
    dscr("cexp", (8, T), BF16)
    dscr("scoresf", (1, T), F32)

    def dv(name, idx, k=0):
        h, bl = DRAM[name]
        return APB(h[idx], bl[k])
    def dvm(name, idx, ks):
        h, bl = DRAM[name]
        return h[idx], [bl[k] for k in ks]

    SB = {}
    def sal(name, shape, dt):
        h = nc.alloc_sbuf_tensor(name, list(shape), dt)
        SB[name] = (h, Buf(name))
        return SB[name]
    def sv(name, idx=np.s_[:, :]):
        h, b = SB[name]
        return APB(h[idx], b)

    for i in range(12):
        sal("A%d" % i, (128, 4096), BF16)
    sal("HT0", (128, NKC * TT), BF16)
    sal("HT1", (128, NKC * TT), BF16)
    sal("XT0", (128, TT), F32)
    sal("XT1", (128, TT), F32)
    sal("SQ0", (128, TT), F32)
    sal("SQ1", (128, TT), F32)
    sal("HID", (128, 16 * TT), BF16)
    sal("OUTS", (128, 8 * TT), BF16)
    sal("OCB2", (128, 512), BF16)
    sal("CBC", (128, TT), BF16)
    sal("SBC", (128, TT), F32)
    sal("ROWS", (1, 2048), F32)
    sal("CSB", (8, T), BF16)
    sal("XRED", (16, TT), BF16)
    sal("PSI", (128, TT), F32)
    sal("SUR", (128, TT), F32)
    sal("GT", (128, 80), F32)
    sal("DBCF", (128, 256), F32)
    sal("DBCS", (128, 256), F32)
    sal("DEVN", (1, 512), F32)
    sal("DSLW", (1, 512), F32)
    sal("CCT0", (128, TT), BF16)
    sal("CCT1", (128, TT), BF16)
    sal("SST0", (128, TT), BF16)
    sal("SST1", (128, TT), BF16)
    sal("MKSB", (128, 4 * 512), BF16)
    sal("IDENT", (128, 128), F32)
    sal("ONESF", (128, 128), F32)
    sal("ONESB", (128, 128), BF16)
    sal("C1", (128, 8), F32)
    sal("WGN", (16, 8), BF16)
    sal("WREDS", (128, 8 * 16), BF16)
    sal("B2SB", (8, DM), BF16)
    sal("B1T", (128, 16), F32)
    sal("GROW", (1, 80), BF16)
    sal("E2SB", (2, 128), BF16)
    sal("LSMALL", (4, 64), F32)
    sal("BOUT", (128, 8), F32)
    sal("CROW", (1, 512), BF16)
    sal("G1T", (128, 8), F32)
    sal("G2T", (128, 8), F32)
    sal("TMPA", (128, TT), F32)
    sal("TMPB", (128, TT), F32)
    sal("KTV", (128, 1024), BF16)
    sal("VTV", (128, 1024), BF16)
    sal("R16A", (128, 16), F32)
    sal("R16B", (128, 16), F32)
    sal("R16C", (128, 16), F32)
    sal("R16D", (128, 16), F32)
    sal("STF", (128, 64), F32)
    sal("STS", (128, 64), F32)
    sal("STFB", (128, 64), BF16)
    sal("STSB", (128, 64), BF16)
    sal("KSL", (128, 128), BF16)
    sal("VSL", (128, 128), BF16)
    sal("CFB", (128, 512), BF16)
    sal("CSB2", (128, 512), BF16)
    sal("CF2", (2, 512), BF16)
    sal("CS2", (2, 512), BF16)
    sal("OCB", (128, 520), BF16)
    sal("PTS", (128, 2048), BF16)

    used = 224 * 1024 - 16 * 1024 - nc.sbuf_bytes_remaining if hasattr(nc, "sbuf_bytes_remaining") else -1

    PS = []
    for i in range(8):
        h = nc.alloc_psum_tensor("ps%d" % i, [128, 512], F32)
        PS.append((h, Buf("ps%d" % i)))
    def ps(i, idx=np.s_[:, :]):
        h, b = PS[i]
        return APB(h[idx], b)

    ONE_F = lambda idx=np.s_[:, 0:1]: sv("ONESF", idx)
    ONE_B = lambda idx=np.s_[:, 0:1]: sv("ONESB", idx)

    P.mset(sv("ONESF"), 1.0)
    P.mset(sv("ONESB"), 1.0)
    ih, ib = eap("ident")
    P.dma("sp", sv("IDENT"), APB(ih[:, :], ib))
    P.mset(sv("C1", np.s_[:, 1:2]), -5.0)
    P.mset(sv("C1", np.s_[:, 2:3]), 1e-5)
    P.mset(sv("C1", np.s_[:, 3:4]), 1e-6)
    SKIP_LAM = globals().get("_SKIP_LAM", False)
    lh, lb = eap("lrow")
    P.dma("sp", sv("LSMALL"), APB(lh[:, :], lb))
    if SKIP_LAM:
        P.mset(sv("C1", np.s_[:, 0:1]), -0.5)
    if not SKIP_LAM:
      P.tp(ps(0, np.s_[0:64, 0:4]), sv("LSMALL"), sv("IDENT", np.s_[0:4, 0:4]))
      LT = sv("TMPA", np.s_[0:64, 0:4])
      P.cp(LT, ps(0, np.s_[0:64, 0:4]))
      P.mm(ps(1, np.s_[0:1, 0:1]), APB(SB["TMPA"][0][0:64, 0:1], SB["TMPA"][1]),
         APB(SB["TMPA"][0][0:64, 1:2], SB["TMPA"][1]))
      P.mm(ps(1, np.s_[0:1, 1:2]), APB(SB["TMPA"][0][0:64, 2:3], SB["TMPA"][1]),
         APB(SB["TMPA"][0][0:64, 3:4], SB["TMPA"][1]))
      P.act(sv("ROWS", np.s_[0:1, 0:2]), ps(1, np.s_[0:1, 0:2]), AF.Exp)
      P.tt(sv("ROWS", np.s_[0:1, 2:3]), sv("ROWS", np.s_[0:1, 0:1]), sv("ROWS", np.s_[0:1, 1:2]), ALU.subtract)
      P.ts(sv("ROWS", np.s_[0:1, 2:3]), sv("ROWS", np.s_[0:1, 2:3]), 0.8, ALU.add, 0.1, ALU.max)
      P.ts(sv("ROWS", np.s_[0:1, 2:3]), sv("ROWS", np.s_[0:1, 2:3]), 0.9, ALU.min, -1.0, ALU.mult)
      P.mm(ps(1, np.s_[0:128, 2:3]), ONE_F(np.s_[0:1, 0:128]), sv("ROWS", np.s_[0:1, 2:3]))
      P.cp(sv("C1", np.s_[:, 0:1]), ps(1, np.s_[0:128, 2:3]))
    mh, mb = eap("mk")
    P.dma("sp", APB(SB["MKSB"][0].ap().rearrange("p (a q) -> p a q", q=512), SB["MKSB"][1]),
          APB(mh.ap().rearrange("a p q -> p a q"), mb))
    e2h, e2b = eap("e2")
    P.dma("sp", sv("E2SB"), APB(e2h[:, :], e2b))
    for _n, _s in (("bout_t", "BOUT"), ("g1t", "G1T"), ("g2t", "G2T")):
        _h, _b = eap(_n)
        P.dma("sp", sv(_s), APB(_h[:, :], _b))

    XV = {"x": None}

    def x_apb(name, kc, t0, t1, k):
        if name == "x":
            h, b = eap("x")
            return APB(h[kc * 128:(kc + 1) * 128, t0:t1], b)
        if name == "y":
            return APB(y[kc * 128:(kc + 1) * 128, t0:t1], y_b[k])
        return dv(name, np.s_[kc * 128:(kc + 1) * 128, t0:t1], k)

    def load_fm_tile(queue, dst_name, dram_name, t, nkc=NKC, cw=TT):
        h, bl = DRAM[dram_name]
        hs, bs = SB[dst_name]
        src3 = h.ap().rearrange("(kc p) q -> p kc q", p=128)[:, :, t * cw:(t + 1) * cw]
        dst3 = hs[:, 0:nkc * cw].rearrange("p (kc q) -> p kc q", q=cw)
        P.dma(queue, APB(dst3, bs), APB(src3, bl[t]))

    def ln_phase(src, kind="ln", g=None, ln_out=None):
        eps_ln, eps_rms = 1e-5, 1e-6
        hidf = SB["HID"][0].ap().bitcast(F32)
        hidb = SB["HID"][1]
        xt_ = lambda kc: APB(hidf[:, kc * TT:(kc + 1) * TT], hidb)
        for t in range(NTT):
            t0, t1 = t * TT, (t + 1) * TT
            SQ = [sv("SQ0"), sv("SQ1")]
            h_, bl_ = (EXT["x"][0], [EXT["x"][1]] * NTT) if src == "x" else DRAM[src]
            src3 = h_.ap().rearrange("(kc p) q -> p kc q", p=128)[:, :, t0:t1]
            dst3 = APB(hidf.rearrange("p (kc q) -> p kc q", q=TT), hidb)
            P.dma("sp", dst3, APB(src3, bl_[t]))
            for kc in range(NKC):
                xs = xt_(kc)
                if kind != "rms":
                    P.mm(ps(0, np.s_[0:1, :]), ONE_F(), xs, start=(kc == 0), stop=(kc == NKC - 1))
                P.act(SQ[kc % 2], xs, AF.Square)
                P.mm(ps(1, np.s_[0:1, :]), ONE_F(), SQ[kc % 2], start=(kc == 0), stop=(kc == NKC - 1))
            R = lambda a, b_: sv("ROWS", np.s_[0:1, a:b_])
            if kind == "rms":
                P.act(R(0, 512), ps(1, np.s_[0:1, :]), AF.Ln, bias=sv("C1", np.s_[0:1, 3:4]), scale=1.0 / DM)
                P.act(R(0, 512), R(0, 512), AF.Exp, scale=-0.5)
                P.mm(ps(2, np.s_[:, :]), ONE_F(np.s_[0:1, 0:128]), R(0, 512))
                P.cp(sv("TMPA"), ps(2, np.s_[:, :]), eng="act")
                for kc in range(NKC):
                    xs = xt_(kc)
                    gcol = sv(g, np.s_[:, kc:kc + 1])
                    P.stt(sv("HT0" if t % 2 == 0 else "HT1",
                             np.s_[:, kc * TT:(kc + 1) * TT]), xs, gcol, sv("TMPA"),
                          ALU.mult, ALU.mult)
                    P.dma("act", dv("hbf", np.s_[kc * 128:(kc + 1) * 128, t0:t1], t),
                          sv("HT0" if t % 2 == 0 else "HT1", np.s_[:, kc * TT:(kc + 1) * TT]),
                          dst_name="hbf@%d" % (t % 2))
                continue
            P.ts(R(0, 512), ps(0, np.s_[0:1, :]), 1.0 / DM, ALU.mult)
            P.ts(R(512, 1024), ps(1, np.s_[0:1, :]), 1.0 / DM, ALU.mult)
            P.tt(R(1024, 1536), R(0, 512), R(0, 512), ALU.mult)
            P.tt(R(512, 1024), R(512, 1024), R(1024, 1536), ALU.subtract)
            P.act(R(1024, 1536), R(512, 1024), AF.Ln, bias=sv("C1", np.s_[0:1, 2:3]))
            P.act(R(1024, 1536), R(1024, 1536), AF.Exp, scale=-0.5)
            if kind == "ln+rms":
                P.tt(R(1536, 2048), R(1024, 1536), R(1024, 1536), ALU.mult)
                P.tt(R(1536, 2048), R(1536, 2048), R(512, 1024), ALU.mult)
                P.act(R(1536, 2048), R(1536, 2048), AF.Ln, bias=sv("C1", np.s_[0:1, 3:4]))
                P.act(R(1536, 2048), R(1536, 2048), AF.Exp, scale=-0.5)
            P.tt(R(512, 1024), R(0, 512), R(1024, 1536), ALU.mult)
            P.mm(ps(2, np.s_[:, :]), ONE_F(np.s_[0:1, 0:128]), R(1024, 1536))
            P.mm(ps(3, np.s_[:, :]), ONE_F(np.s_[0:1, 0:128]), R(512, 1024))
            P.cp(sv("TMPA"), ps(2, np.s_[:, :]), eng="act")
            P.cp(sv("TMPB"), ps(3, np.s_[:, :]), eng="act")
            if kind == "ln+rms":
                P.mm(ps(2, np.s_[:, :]), ONE_F(np.s_[0:1, 0:128]), R(1536, 2048))
                P.cp(sv("SBC"), ps(2, np.s_[:, :]), eng="act")
            HT = "HT0" if t % 2 == 0 else "HT1"
            for kc in range(NKC):
                xs = xt_(kc)
                P.tt(sv("SQ0") if kc % 2 == 0 else sv("SQ1"), xs, sv("TMPA"), ALU.mult)
                tv = sv("SQ0") if kc % 2 == 0 else sv("SQ1")
                if kind == "ln":
                    P.tt(sv(HT, np.s_[:, kc * TT:(kc + 1) * TT]), tv, sv("TMPB"), ALU.subtract)
                    P.dma("act", dv("hbf", np.s_[kc * 128:(kc + 1) * 128, t0:t1], t),
                          sv(HT, np.s_[:, kc * TT:(kc + 1) * TT]),
                          dst_name="hbf@%d" % (t % 2))
                else:
                    P.tt(tv, tv, sv("TMPB"), ALU.subtract)
                    P.dma("sp", dv(ln_out, np.s_[kc * 128:(kc + 1) * 128, t0:t1], t), tv,
                          dst_name="%s@%d" % (ln_out, t % 2))
                    gcol = sv(g, np.s_[:, kc:kc + 1])
                    P.stt(sv(HT, np.s_[:, kc * TT:(kc + 1) * TT]), tv, gcol, sv("SBC"),
                          ALU.mult, ALU.mult)
                    P.dma("act", dv("hbf", np.s_[kc * 128:(kc + 1) * 128, t0:t1], t),
                          sv(HT, np.s_[:, kc * TT:(kc + 1) * TT]),
                          dst_name="hbf@%d" % (t % 2))

    def wload(ext_name, slots, rows=DM, cols=DM, row_off=0, e=None):
        h, b = EXT[ext_name]
        nkc = rows // 128
        per = nkc // len(slots)
        assert per * len(slots) == nkc
        for si, sl in enumerate(slots):
            hsl, bsl = SB[sl]
            src = h.ap() if e is None else h[e]
            src = src.rearrange("(kc p) m -> p kc m", p=128)
            P.dma("pool", APB(hsl[:, 0:per * cols], bsl),
                  APB(src[:, row_off // 128 + si * per:(row_off // 128) + (si + 1) * per, 0:cols], b))
        def acc(kc, c0, c1):
            si, ko = kc // per, kc % per
            hsl, bsl = SB[slots[si]]
            return APB(hsl[:, ko * cols + c0:ko * cols + c1], bsl)
        return acc

    if has("p1"):
        ln_phase("x", "ln")

    if has("p2"):
        _mark_p2 = True
        wq1a = wload("wq1a", ["A0", "A1"])
        wq1b = wload("wq1b", ["A2", "A3"])
        wq2a = wload("wq2a", ["A4", "A5"])
        wq2b = wload("wq2b", ["A6", "A7"])
        wk1a = wload("wk1a", ["A8"], cols=128)
        def wk_load(name, slot, coff):
            h, b = EXT[name]
            hsl, bsl = SB[slot]
            src = h.ap().rearrange("(kc p) m -> p kc m", p=128)
            P.dma("pool", APB(hsl[:, coff:coff + 1024], bsl), APB(src, b))
            def acc(kc, c0, c1):
                return APB(hsl[:, coff + kc * 128 + c0:coff + kc * 128 + c1], bsl)
            return acc
        wk1b = wk_load("wk1b", "A8", 1024)
        wk2a = wk_load("wk2a", "A8", 2048)
        wk2b = wk_load("wk2b", "A8", 3072)
        wv0 = wk_load("wv0", "A9", 0)
        cch, ccb = eap("cc")
        ssh, ssb = eap("ss")
        for t in range(NTT):
            t0, t1 = t * TT, (t + 1) * TT
            HT = "HT0" if t % 2 == 0 else "HT1"
            load_fm_tile("sp", HT, "hbf", t)
            CCT = "CCT0" if t % 2 == 0 else "CCT1"
            SST = "SST0" if t % 2 == 0 else "SST1"
            P.dma("sp", sv(CCT), APB(cch[:, t0:t1], ccb))
            P.dma("sp", sv(SST), APB(ssh[:, t0:t1], ssb))
            hT = lambda kc: sv(HT, np.s_[:, kc * TT:(kc + 1) * TT])
            def rope_proj(wa, wb_, dst, mchunks, kdim):
                for m in range(mchunks):
                    for kc in range(NKC):
                        P.mm(ps(0 + (m % 2) * 2, np.s_[:, :]), wa(kc, m * 128, (m + 1) * 128),
                             hT(kc), start=(kc == 0), stop=(kc == NKC - 1))
                        P.mm(ps(1 + (m % 2) * 2, np.s_[:, :]), wb_(kc, m * 128, (m + 1) * 128),
                             hT(kc), start=(kc == 0), stop=(kc == NKC - 1))
                    ob = sv("OCB", np.s_[:, 0:512]) if m % 2 == 0 else sv("OCB2")
                    P.tt(sv("TMPA"), ps(0 + (m % 2) * 2), sv(CCT), ALU.mult)
                    P.tt(sv("TMPB"), ps(1 + (m % 2) * 2), sv(SST), ALU.mult)
                    P.tt(ob, sv("TMPA"), sv("TMPB"), ALU.add)
                    P.dma("act", dv(dst, np.s_[m * 128:(m + 1) * 128, t0:t1], t), ob)
            rope_proj(wq1a, wq1b, "q1r", 8, DM)
            rope_proj(wq2a, wq2b, "q2r", 8, DM)
            rope_proj(wk1a, wk1b, "k1r", 1, 128)
            rope_proj(wk2a, wk2b, "k2r", 1, 128)
            for s4 in range(4):
                for kc in range(NKC):
                    P.mm(ps(4, np.s_[:, 0:128]),
                         sv(HT, np.s_[:, kc * TT + s4 * 128:kc * TT + (s4 + 1) * 128]),
                         wv0(kc, 0, 128), start=(kc == 0), stop=(kc == NKC - 1))
                P.mset(sv("OCB", np.s_[:, 0:130]), 1.0, eng="pool")
                ocv = SB["OCB"][0][:, 0:130].rearrange("p (g k) -> p g k", k=65)
                P.cp(APB(ocv[:, 0:2, 0:64], SB["OCB"][1]),
                     APB(PS[4][0][:, 0:128].rearrange("p (g k) -> p g k", k=64), PS[4][1]),
                     eng="act")
                c = t * 4 + s4
                vp, vpb = DRAM["vpack0"]
                for g in range(2):
                    P.dma("act", APB(vp[g, :, c, :], vpb[t]),
                          APB(ocv[:, g, 0:65], SB["OCB"][1]))

    if has("p3"):
        HP_N, WIN_N, P3_PARTS = P3_LIMIT
        q1h, q1bl = DRAM["q1r"]
        q2h, q2bl = DRAM["q2r"]
        P.dma("sp", sv("A2"), APB(DRAM["k1r"][0].ap(), DRAM["k1r"][1]))
        P.dma("sp", sv("A3"), APB(DRAM["k2r"][0].ap(), DRAM["k2r"][1]))
        vph, vpbl = DRAM["vpack0"]
        P.dma("sp", sv("A4", np.s_[:, 0:2080]),
              APB(vph[0].rearrange("p c k -> p (c k)"), vpbl))
        P.dma("sp", sv("A5", np.s_[:, 0:2080]),
              APB(vph[1].rearrange("p c k -> p (c k)"), vpbl))
        neglam = lambda: sv("C1", np.s_[0:64, 0:1])
        for hp in range(HP_N):
            P.dma("sp", sv("A0"), APB(q1h[hp * 128:(hp + 1) * 128, :], q1bl))
            P.dma("sp", sv("A1"), APB(q2h[hp * 128:(hp + 1) * 128, :], q2bl))
            def emit_S3(n):
                w0 = 128 * n
                for si, (kt, qt) in enumerate((("A2", "A0"), ("A3", "A1"))):
                    for ks in range(2):
                        for head, base in ((0, 0), (1, 64)):
                            bk, co = head * 2 + si, ks * 256
                            P.mm(ps(bk, np.s_[:, co:co + 256]),
                                 sv(kt, np.s_[base:base + 64, w0 + ks * 128:w0 + (ks + 1) * 128]),
                                 sv(qt, np.s_[base:base + 64, w0:w0 + 256]))
            PTS2 = lambda n, a, b: sv("PTS" if n % 2 == 0 else "HID", np.s_[:, a:b])
            def emit_exp3(n):
                for si in range(2):
                    for head in range(2):
                        bk = head * 2 + si
                        idx2 = si * 2 + head
                        P.act(PTS2(n, idx2 * 512, (idx2 + 1) * 512),
                              ps(bk, np.s_[:, :]), AF.Exp, scale=0.125)
            def emit_PV3(n):
                ob = 4 if n % 2 == 0 else 6
                for si in range(2):
                    for head in range(2):
                        idx2 = si * 2 + head
                        vsl = "A4" if head == 0 else "A5"
                        for ks in range(2):
                            P.mm(ps(ob + idx2 // 2,
                                    np.s_[0:65, (idx2 % 2) * 256:(idx2 % 2) * 256 + 256]),
                                 sv(vsl, np.s_[:, (n + ks) * 65:(n + ks + 1) * 65]),
                                 PTS2(n, idx2 * 512 + ks * 256, idx2 * 512 + ks * 256 + 256),
                                 start=(ks == 0), stop=(ks == 1))
            def emit_tail3(n):
                ob = 4 if n % 2 == 0 else 6
                rb = 6 if n % 2 == 0 else 4
                rdst = [("SBC", 0), ("TMPA", 0)]
                for bi in range(2):
                    rn, rc = rdst[bi]
                    P.act(sv(rn, np.s_[64:65, 0:512]), ps(ob + bi, np.s_[64:65, :]), AF.Ln)
                    P.act(sv(rn, np.s_[64:65, 0:512]), sv(rn, np.s_[64:65, 0:512]),
                          AF.Exp, scale=-1.0)
                for idx2 in range(4):
                    rn, rc = rdst[idx2 // 2]
                    P.mm(ps(rb + idx2 // 2, np.s_[0:64, (idx2 % 2) * 256:(idx2 % 2) * 256 + 256]),
                         ONE_F(np.s_[64:65, 0:64]),
                         sv(rn, np.s_[64:65, (idx2 % 2) * 256:(idx2 % 2) * 256 + 256]))
                for idx2 in range(4):
                    si, head = idx2 // 2, idx2 % 2
                    dstn = "SQ0" if si == 0 else "SQ1"
                    P.cp(sv(dstn, np.s_[0:64, head * 256:head * 256 + 256]),
                         ps(rb + idx2 // 2, np.s_[0:64, (idx2 % 2) * 256:(idx2 % 2) * 256 + 256]))
                for head in range(2):
                    tmpn = "TMPB" if head == 0 else "TMPA"
                    o1i, o2i = head, 2 + head
                    P.tt(sv(tmpn, np.s_[0:64, 0:256]),
                         ps(ob + o1i // 2, np.s_[0:64, (o1i % 2) * 256:(o1i % 2) * 256 + 256]),
                         sv("SQ0", np.s_[0:64, head * 256:head * 256 + 256]), ALU.mult)
                    P.tt(sv(tmpn, np.s_[0:64, 256:512]),
                         ps(ob + o2i // 2, np.s_[0:64, (o2i % 2) * 256:(o2i % 2) * 256 + 256]),
                         sv("SQ1", np.s_[0:64, head * 256:head * 256 + 256]), ALU.mult)
                    P.stt(sv("OCB", np.s_[0:64, head * 256:head * 256 + 256]),
                          sv(tmpn, np.s_[0:64, 256:512]), neglam(),
                          sv(tmpn, np.s_[0:64, 0:256]), ALU.mult, ALU.add)
                a0h, a0bl = DRAM["a0"]
                for head in range(2):
                    P.dma("act", APB(a0h[hp * 128 + head * 64:hp * 128 + head * 64 + 64,
                                         256 * n:256 * n + 256], a0bl[n // 2]),
                          sv("OCB", np.s_[0:64, head * 256:head * 256 + 256]))
            emit_S3(0)
            for n in range(WIN_N):
                emit_exp3(n)
                if n + 1 < WIN_N:
                    emit_S3(n + 1)
                emit_PV3(n)
                emit_tail3(n)

    def proj_residual_tile(t, act_name, w_acc, src_x, dst_x, bias_col=None, banks=(0, 1)):
        t0, t1 = t * TT, (t + 1) * TT
        HT = "HT0" if t % 2 == 0 else "HT1"
        load_fm_tile("sp", HT, act_name, t)
        for m in range(NKC):
            bk = banks[m % 2]
            for kc in range(NKC):
                P.mm(ps(bk, np.s_[:, :]), w_acc(kc, m * 128, (m + 1) * 128),
                     sv(HT, np.s_[:, kc * TT:(kc + 1) * TT]),
                     start=(kc == 0), stop=(kc == NKC - 1))
            xv = sv("XT0") if m % 2 == 0 else sv("XT1")
            P.dma("pool", xv, x_apb(src_x, m, t0, t1, t))
            ov = sv("SQ0") if m % 2 == 0 else sv("SQ1")
            if bias_col is not None:
                P.stt(ov, ps(bk, np.s_[:, :]), bias_col(m), xv, ALU.add, ALU.add)
            else:
                P.tt(ov, ps(bk, np.s_[:, :]), xv, ALU.add)
            P.dma("act", x_apb(dst_x, m, t0, t1, t), ov,
                  dst_name="%s@%d" % (dst_x, t % 2))

    def proj_residual(act_name, w_acc, src_x, dst_x, bias_col=None):
        for t in range(NTT):
            proj_residual_tile(t, act_name, w_acc, src_x, dst_x, bias_col)

    if has("p4"):
        wout = wload("wout0", ["A0", "A1"])
        proj_residual("a0", wout, "x", "xA",
                      bias_col=lambda m: sv("BOUT", np.s_[:, m:m + 1]))

    if has("p5"):
        ln_phase("xA", "ln")

    def moe_gating_l0():
        wgh, wgb = eap("wg")
        wgth, wgtb = eap("wgt")
        P.dma("sp", sv("R16A", np.s_[0:8, 0:16]), APB(wgh[:, :], wgb))
        P.dma("sp", sv("R16B", np.s_[0:16, 0:8]), APB(wgth[:, :], wgtb))
        P.act(sv("R16C", np.s_[0:8, 0:16]), sv("R16A", np.s_[0:8, 0:16]), AF.Square)
        P.red(sv("R16D", np.s_[0:8, 0:1]), sv("R16C", np.s_[0:8, 0:16]), ALU.add)
        P.act(sv("R16D", np.s_[0:8, 0:1]), sv("R16D", np.s_[0:8, 0:1]), AF.Sqrt)
        P.ts(sv("R16D", np.s_[0:8, 0:1]), sv("R16D", np.s_[0:8, 0:1]), 1e-4, ALU.max)
        P.recip(sv("R16D", np.s_[0:8, 0:1]), sv("R16D", np.s_[0:8, 0:1]))
        P.tp(ps(0, np.s_[0:1, 0:8]), sv("R16D", np.s_[0:8, 0:1]), sv("IDENT", np.s_[0:8, 0:8]))
        P.cp(sv("ROWS", np.s_[0:1, 0:8]), ps(0, np.s_[0:1, 0:8]))
        P.mm(ps(1, np.s_[0:16, 0:8]), ONE_F(np.s_[0:1, 0:16]), sv("ROWS", np.s_[0:1, 0:8]))
        P.tt(sv("WGN"), sv("R16B", np.s_[0:16, 0:8]), ps(1, np.s_[0:16, 0:8]), ALU.mult)
        wrh, wrb = eap("wred")
        P.dma("pool", sv("WREDS"),
              APB(wrh.ap().rearrange("(kc p) m -> p kc m", p=128), wrb))
        hb, hbl = DRAM["hbf"]
        sc_h, sc_bl = DRAM["scoresf"]
        for t in range(NTT):
            t0, t1 = t * TT, (t + 1) * TT
            HT = "HT0" if t % 2 == 0 else "HT1"
            load_fm_tile("sp", HT, "hbf", t)
            for kc in range(NKC):
                P.mm(ps(0, np.s_[0:16, :]),
                     APB(SB["WREDS"][0][:, kc * 16:(kc + 1) * 16], SB["WREDS"][1]),
                     sv(HT, np.s_[:, kc * TT:(kc + 1) * TT]),
                     start=(kc == 0), stop=(kc == NKC - 1))
            P.cp(sv("XRED", np.s_[0:16, 0:TT]), ps(0, np.s_[0:16, :]), eng="act")
            for s4 in range(4):
                c = t * 4 + s4
                P.mm(ps(1, np.s_[0:128, 0:8]),
                     sv("XRED", np.s_[0:16, s4 * 128:(s4 + 1) * 128]), sv("WGN"))
                P.red(sv("R16A", np.s_[:, 0:1]), ps(1, np.s_[0:128, 0:8]), ALU.max, negate=True)
                P.red(sv("R16A", np.s_[:, 3:4]), ps(1, np.s_[0:128, 0:8]), ALU.max)
                P.act(sv("R16B", np.s_[0:128, 0:8]), ps(1, np.s_[0:128, 0:8]), AF.Exp,
                      bias=sv("R16A", np.s_[:, 0:1]))
                P.red(sv("R16A", np.s_[:, 1:2]), sv("R16B", np.s_[0:128, 0:8]), ALU.add)
                P.recip(sv("R16A", np.s_[:, 2:3]), sv("R16A", np.s_[:, 1:2]))
                P.ts(sv("R16C", np.s_[0:128, 0:8]), ps(1, np.s_[0:128, 0:8]),
                     sv("R16A", np.s_[:, 3:4]), ALU.is_ge)
                P.dma("act", APB(sc_h[0:1, 128 * c:128 * c + 128].rearrange("a k -> k a"),
                                 sc_bl[c // 4]), sv("R16A", np.s_[:, 2:3]))
                if t == 0 and s4 == 0 and globals().get("_P6DBG"):
                    for nm2 in ("WGN", "XRED", "R16A", "R16B", "R16C"):
                        hh2, bb2 = SB[nm2]
                        dd = nc.dram_tensor("dbg6_" + nm2, list(hh2.shape), hh2.dtype,
                                            kind="ExternalOutput")
                        P.dma("sp", APB(dd.ap(), Buf()), APB(hh2.ap(), bb2))
                P.tp(ps(2, np.s_[0:8, 0:128]), sv("R16C", np.s_[0:128, 0:8]),
                     sv("IDENT"))
                P.cp(sv("CSB", np.s_[0:8, 128 * c:128 * c + 128]), ps(2, np.s_[0:8, 0:128]),
                     eng="act")
        ch, cbl = DRAM["cexp"]
        for t in range(NTT):
            P.dma("sp", APB(ch[:, t * TT:(t + 1) * TT], cbl[t]),
                  sv("CSB", np.s_[:, t * TT:(t + 1) * TT]), dst_name="cexp@%d" % (t % 2))

    def moe_experts(layer):
        w1n, w2n = ("w1_0", "w2_0") if layer == 0 else ("w1_1", "w2_1")
        hbl = DRAM["hbf"][1]
        hb = DRAM["hbf"][0]
        ch, cbl = DRAM["cexp"]
        mh, mbl = DRAM["macc"]
        if layer == 0:
            b2h, b2b = eap("b2_0")
            P.dma("pool", sv("B2SB"), APB(b2h[:, :], b2b))
            b1h, b1b = eap("b1t_0")
        w1slots = [["A0", "A1", "A2", "A3"], ["A8", "A9", "A10", "A11"]]
        w2slots = ["A4", "A5", "A6", "A7"]
        def wexp_load(name, e, slots, ncol):
            h, b = EXT[name]
            nkc = h.shape[1] // 128
            per = nkc // len(slots)
            for si, sl in enumerate(slots):
                hsl, bsl = SB[sl]
                src = h[e].rearrange("(kc p) m -> p kc m", p=128)
                P.dma("pool", APB(hsl[:, 0:per * ncol], bsl),
                      APB(src[:, per * si:per * si + per, :], b))
            def acc(kc, c0, c1):
                hsl, bsl = SB[slots[kc // per]]
                off = (kc % per) * ncol
                return APB(hsl[:, off + c0:off + c1], bsl)
            return acc
        w1a = [None, None]
        w1a[0] = wexp_load(w1n, 0, w1slots[0], 2048)
        for e in range(8):
            if e < 7:
                w1a[(e + 1) % 2] = wexp_load(w1n, e + 1, w1slots[(e + 1) % 2], 2048)
            w2a = wexp_load(w2n, e, w2slots, 1024)
            w1 = w1a[e % 2]
            if layer == 0:
                P.dma("sp", sv("B1T"), APB(b1h[e], b1b))
            for t in range(NTT):
                t0, t1 = t * TT, (t + 1) * TT
                HT = "HT0" if t % 2 == 0 else "HT1"
                load_fm_tile("sp", HT, "hbf", t)
                P.dma("sp", sv("CROW"), APB(ch[e:e + 1, t0:t1], cbl[t]))
                P.mm(ps(6, np.s_[:, :]), ONE_B(np.s_[0:1, 0:128]), sv("CROW"))
                P.cp(sv("CBC"), ps(6, np.s_[:, :]), eng="act")
                for hc in range(16):
                    bank = 4 + (hc % 2)
                    for kc in range(NKC):
                        P.mm(ps(bank, np.s_[:, :]), w1(kc, hc * 128, (hc + 1) * 128),
                             sv(HT, np.s_[:, kc * TT:(kc + 1) * TT]),
                             start=(kc == 0), stop=(kc == NKC - 1))
                    hv = sv("HID", np.s_[:, hc * TT:(hc + 1) * TT])
                    if layer == 0:
                        P.act(hv, ps(bank, np.s_[:, :]), AF.Gelu, bias=sv("B1T", np.s_[:, hc:hc + 1]))
                    else:
                        P.act(hv, ps(bank, np.s_[:, :]), AF.Silu)
                    P.tt(hv, hv, sv("CBC"), ALU.mult)
                half = (t % 2) * 4
                for m in range(NKC):
                    bank = m % 4
                    for hc in range(16):
                        P.mm(ps(bank, np.s_[:, :]), w2a(hc, m * 128, (m + 1) * 128),
                             sv("HID", np.s_[:, hc * TT:(hc + 1) * TT]),
                             start=(hc == 0), stop=(hc == 15 and not (layer == 0 and e == 0)))
                    if layer == 0 and e == 0:
                        P.mm(ps(bank, np.s_[:, :]),
                             APB(SB["B2SB"][0][0:8, m * 128:(m + 1) * 128], SB["B2SB"][1]),
                             sv("CSB", np.s_[0:8, t0:t1]), start=False, stop=True)
                    P.cp(sv("OUTS", np.s_[:, (half + bank) * TT:(half + bank + 1) * TT]),
                         ps(bank, np.s_[:, :]), eng="dve")
                    if m % 4 == 3:
                        g4 = m // 4
                        dst = mh[g4 * 512:(g4 + 1) * 512, t0:t1].rearrange(
                            "(mc p) q -> p mc q", p=128)
                        src3 = SB["OUTS"][0][:, half * TT:(half + 4) * TT].rearrange(
                            "p (mc q) -> p mc q", q=TT)
                        P.dma("pool", APB(dst, mbl[t]), APB(src3, SB["OUTS"][1]),
                              accum=(None if e == 0 else ALU.add),
                              dst_name="macc@%d" % (t % 2))

    def moe_epilogue_l0():
        mh, mbl = DRAM["macc"]
        sc_h, sc_bl = DRAM["scoresf"]
        for t in range(NTT):
            t0, t1 = t * TT, (t + 1) * TT
            P.dma("sp", sv("ROWS", np.s_[0:1, 0:512]), APB(sc_h[0:1, t0:t1], sc_bl[t]))
            P.mm(ps(7, np.s_[:, :]), ONE_F(np.s_[0:1, 0:128]), sv("ROWS", np.s_[0:1, 0:512]))
            P.cp(sv("SBC"), ps(7, np.s_[:, :]), eng="act")
            for m in range(NKC):
                mv = sv("CFB") if m % 2 == 0 else sv("CSB2")
                P.dma("sp", mv, APB(mh[m * 128:(m + 1) * 128, t0:t1], mbl[t]))
                xv = sv("XT0") if m % 2 == 0 else sv("XT1")
                P.dma("pool", xv, x_apb("xA", m, t0, t1, t))
                tv = sv("TMPA") if m % 2 == 0 else sv("TMPB")
                P.tt(tv, mv, sv("SBC"), ALU.mult)
                P.tt(tv, tv, xv, ALU.add)
                P.dma("act", x_apb("xB", m, t0, t1, t), tv, dst_name="xB@%d" % (t % 2))

    if has("p6"):
        moe_gating_l0()
        moe_experts(0)
        moe_epilogue_l0()

    def p8_l1_proj():
        wq = wload("wq_r", ["A0", "A1"])
        wk = wload("wk_r", ["A2", "A3"])
        wv = wload("wv_r", ["A4", "A5"])
        wg5h, wg5b = EXT["wg5"]
        hsl6, bsl6 = SB["A6"]
        P.dma("pool", APB(hsl6[:, 0:640], bsl6),
              APB(wg5h.ap().rearrange("(kc p) m -> p kc m", p=128), wg5b))
        wg5 = lambda kc: APB(hsl6[:, kc * 80:(kc + 1) * 80], bsl6)
        grh, grb = eap("brow5")
        P.dma("pool", sv("GROW"), APB(grh[:, :], grb))
        hb, hbl = DRAM["hbf"]
        qh, qbl = DRAM["qfm"]
        kprh, kprbl = DRAM["kpr"]
        vprh, vprbl = DRAM["vpr"]
        cfh, cfbl = DRAM["cfr"]
        csh, csbl = DRAM["csr"]
        NEG5 = sv("C1", np.s_[:, 1:2])
        for t in range(NTT):
            t0, t1 = t * TT, (t + 1) * TT
            HT = "HT0" if t % 2 == 0 else "HT1"
            load_fm_tile("sp", HT, "hbf", t)
            hT = lambda kc: sv(HT, np.s_[:, kc * TT:(kc + 1) * TT])
            hTs = lambda kc, s4: sv(HT, np.s_[:, kc * TT + s4 * 128:kc * TT + (s4 + 1) * 128])
            for m in range(NKC):
                bank = 2 + (m % 2)
                for kc in range(NKC):
                    P.mm(ps(bank, np.s_[:, :]), wq(kc, m * 128, (m + 1) * 128), hT(kc),
                         start=(kc == 0), stop=(kc == NKC - 1))
                ob = sv("OCB", np.s_[:, 0:512]) if m % 2 == 0 else sv("OCB2")
                P.act(ob, ps(bank, np.s_[:, :]), AF.Silu)
                P.dma("act", APB(qh[m * 128:(m + 1) * 128, t0:t1], qbl[t]), ob)
            for s4 in range(4):
                c = t * 4 + s4
                for kc in range(NKC):
                    P.mm(ps(0, np.s_[0:128, 0:80]), hTs(kc, s4), wg5(kc),
                         start=(kc == 0), stop=False)
                P.mm(ps(0, np.s_[0:128, 0:80]), ONE_B(np.s_[0:1, 0:128]), sv("GROW"),
                     start=False, stop=True)
                P.act(sv("GT"), ps(0, np.s_[0:128, 0:80]), AF.Sigmoid)
                for half, (dst, wacc) in enumerate((("KTV", wk), ("VTV", wv))):
                    pass
                for half in range(2):
                    for name_, wacc, bank in (("KTV", wk, 4), ("VTV", wv, 5)):
                        for kc in range(NKC):
                            P.mm(ps(bank, np.s_[:, :]), hTs(kc, s4),
                                 wacc(kc, half * 512, (half + 1) * 512),
                                 start=(kc == 0), stop=(kc == NKC - 1))
                        dstv = sv(name_, np.s_[:, half * 512:(half + 1) * 512])
                        P.act(dstv, ps(bank, np.s_[:, :]), AF.Silu)
                        bview = SB["GT"][0][:, half * 8:(half + 1) * 8].rearrange(
                            "p (h o) -> p h o", o=1).broadcast_to([128, 8, 64])
                        dv3 = SB[name_][0][:, half * 512:(half + 1) * 512].rearrange(
                            "p (h k) -> p h k", k=64)
                        P.tt(APB(dv3, SB[name_][1]), APB(dv3, SB[name_][1]),
                             APB(bview, SB["GT"][1]), ALU.mult)
                for hp in range(8):
                    name_ = "KTV" if True else None
                    P.dma("act", APB(kprh[hp, :, c, :], kprbl[t]),
                          sv("KTV", np.s_[:, hp * 128:(hp + 1) * 128]))
                    P.dma("act", APB(vprh[hp, :, c, :], vprbl[t]),
                          sv("VTV", np.s_[:, hp * 128:(hp + 1) * 128]))
                for half, tmpn in ((0, "TMPA"), (1, "TMPB")):
                    P.act(sv(tmpn), sv("KTV", np.s_[:, half * 512:(half + 1) * 512]), AF.Square)
                    P.red(sv("R16A", np.s_[:, half * 8:(half + 1) * 8]),
                          APB(SB[tmpn][0].ap().rearrange("p (h k) -> p h k", k=64), SB[tmpn][1]),
                          ALU.add)
                P.act(sv("R16A"), sv("R16A"), AF.Sqrt)
                P.ts(sv("R16A"), sv("R16A"), 1e-8, ALU.add)
                P.recip(sv("R16B"), sv("R16A"))
                for half, tmpn in ((0, "TMPA"), (1, "TMPB")):
                    P.act(sv(tmpn), sv("VTV", np.s_[:, half * 512:(half + 1) * 512]), AF.Square)
                    P.red(sv("R16A", np.s_[:, half * 8:(half + 1) * 8]),
                          APB(SB[tmpn][0].ap().rearrange("p (h k) -> p h k", k=64), SB[tmpn][1]),
                          ALU.add)
                P.act(sv("R16A"), sv("R16A"), AF.Sqrt)
                P.ts(sv("R16A"), sv("R16A"), 1e-8, ALU.add)
                P.recip(sv("R16C"), sv("R16A"))
                for half, tmpn in ((0, "TMPA"), (1, "TMPB")):
                    P.tt(sv(tmpn), sv("KTV", np.s_[:, half * 512:(half + 1) * 512]),
                         sv("VTV", np.s_[:, half * 512:(half + 1) * 512]), ALU.mult)
                    P.red(sv("R16D", np.s_[:, half * 8:(half + 1) * 8]),
                          APB(SB[tmpn][0].ap().rearrange("p (h k) -> p h k", k=64), SB[tmpn][1]),
                          ALU.add)
                P.tt(sv("R16D"), sv("R16D"), sv("R16B"), ALU.mult)
                P.tt(sv("R16D"), sv("R16D"), sv("R16C"), ALU.mult)
                P.act(sv("R16D"), sv("R16D"), AF.Abs)
                PSIc = sv("PSI", np.s_[:, c * 16:(c + 1) * 16])
                SURc = sv("SUR", np.s_[:, c * 16:(c + 1) * 16])
                P.act(PSIc, sv("R16D"), AF.Sigmoid, scale=3.0)
                P.act(SURc, PSIc, AF.Sigmoid, scale=10.0, bias=NEG5)
                for dn, gofs, coef, drow in (("f", 16, -0.1, "DEVN"), ("s", 32, -0.05, "DSLW")):
                    P.ts(sv("R16A"), PSIc, coef, ALU.mult, 1.0, ALU.add)
                    P.tt(sv("R16B"), APB(SB["GT"][0][:, gofs:gofs + 16], SB["GT"][1]),
                         sv("R16A"), ALU.mult)
                    P.mm(ps(1, np.s_[0:1, 0:16]), ONE_F(), sv("R16B"))
                    P.act(sv("ROWS", np.s_[0:1, 1024:1040]), ps(1, np.s_[0:1, 0:16]),
                          AF.Ln, scale=1.0 / 128)
                    P.act(sv("ROWS", np.s_[0:1, 1024:1040]), sv("ROWS", np.s_[0:1, 1024:1040]),
                          AF.Exp, scale=128.0)
                    P.cp(sv(drow, np.s_[0:1, c * 8:(c + 1) * 8]),
                         sv("ROWS", np.s_[0:1, 1024:1040:2]))
                    P.cp(sv(drow, np.s_[0:1, 256 + c * 8:256 + (c + 1) * 8]),
                         sv("ROWS", np.s_[0:1, 1025:1040:2]))
                P.ts(sv("R16A"), PSIc, 0.3, ALU.mult, 0.5, ALU.add)
                P.tt(sv("R16B"), sv("R16A"), APB(SB["GT"][0][:, 48:64], SB["GT"][1]), ALU.mult)
                P.ts(sv("R16C"), sv("R16A"), -1.0, ALU.mult, 1.0, ALU.add)
                P.tt(sv("R16C"), sv("R16C"), APB(SB["GT"][0][:, 64:80], SB["GT"][1]), ALU.mult)
                P.tp(ps(1, np.s_[0:16, 0:128]), sv("R16B"), sv("IDENT"))
                P.cp(sv("CBC", np.s_[0:16, 0:128]), ps(1, np.s_[0:16, 0:128]), eng="act")
                P.dma("act", APB(cfh[:, c * 128:(c + 1) * 128], cfbl[t]),
                      sv("CBC", np.s_[0:16, 0:128]))
                P.tp(ps(1, np.s_[0:16, 0:128]), sv("R16C"), sv("IDENT"))
                P.cp(sv("CBC", np.s_[0:16, 0:128]), ps(1, np.s_[0:16, 0:128]), eng="act")
                P.dma("act", APB(csh[:, c * 128:(c + 1) * 128], csbl[t]),
                      sv("CBC", np.s_[0:16, 0:128]))
        for drow, dbc in (("DEVN", "DBCF"), ("DSLW", "DBCS")):
            P.mm(ps(6, np.s_[0:64, 0:256]), ONE_F(np.s_[0:1, 0:64]),
                 sv(drow, np.s_[0:1, 0:256]))
            P.mm(ps(6, np.s_[64:128, 0:256]), ONE_F(np.s_[0:1, 0:64]),
                 sv(drow, np.s_[0:1, 256:512]), tile_position=(0, 64))
            P.cp(sv(dbc), ps(6, np.s_[:, 0:256]))

    def p9_recurrence():
        kprh, kprbl = DRAM["kpr"]
        vprh, vprbl = DRAM["vpr"]
        qh, qbl = DRAM["qfm"]
        ofh, ofbl = DRAM["ofm"]
        cfh, cfbl = DRAM["cfr"]
        csh, csbl = DRAM["csr"]
        for hp in range(8):
            sA = ["A0", "A1", "A2"] if hp % 2 == 0 else ["A3", "A4", "A5"]
            kst, vst, qst = sA
            P.dma("sp", sv(kst), APB(kprh[hp].rearrange("p c n -> p (c n)"), kprbl))
            P.dma("sp", sv(vst), APB(vprh[hp].rearrange("p c n -> p (c n)"), vprbl))
            P.dma("sp", sv(qst), APB(qh[hp * 128:(hp + 1) * 128, :], qbl))
            P.mset(sv("STF"), 0.0)
            P.mset(sv("STS"), 0.0)
            for c in range(32):
                cb = np.s_[:, c * 128:(c + 1) * 128]
                sview = SB["SUR"][0][:, c * 16 + 2 * hp:c * 16 + 2 * hp + 2].rearrange(
                    "p (h o) -> p h o", o=1).broadcast_to([128, 2, 64])
                for src_, dst_ in ((kst, "KSL"), (vst, "VSL")):
                    v3 = SB[src_][0][:, c * 128:(c + 1) * 128].rearrange(
                        "p (h k) -> p h k", k=64)
                    d3 = SB[dst_][0][:, :].rearrange("p (h k) -> p h k", k=64)
                    P.tt(APB(d3, SB[dst_][1]), APB(v3, SB[src_][1]),
                         APB(sview, SB["SUR"][1]), ALU.mult, eng="pool")
                for head in range(2):
                    o0 = c * 128 + head * 64
                    tp_ = None if head == 0 else (0, 64)
                    osl = np.s_[0:64, 0:64] if head == 0 else np.s_[64:128, 0:64]
                    P.mm(ps(0, osl), APB(SB[vst][0][:, o0:o0 + 64], SB[vst][1]),
                         APB(SB[kst][0][:, o0:o0 + 64], SB[kst][1]), tile_position=tp_)
                    P.mm(ps(1, osl), APB(SB["VSL"][0][:, head * 64:head * 64 + 64], SB["VSL"][1]),
                         APB(SB["KSL"][0][:, head * 64:head * 64 + 64], SB["KSL"][1]),
                         tile_position=tp_)
                P.stt(sv("STF"), sv("STF"), sv("DBCF", np.s_[:, c * 8 + hp:c * 8 + hp + 1]),
                      ps(0, np.s_[:, 0:64]), ALU.mult, ALU.add)
                P.cp(sv("STFB"), sv("STF"), eng="act")
                P.stt(sv("STS"), sv("STS"), sv("DBCS", np.s_[:, c * 8 + hp:c * 8 + hp + 1]),
                      ps(1, np.s_[:, 0:64]), ALU.mult, ALU.add)
                P.cp(sv("STSB"), sv("STS"), eng="act")
                co = np.s_[0:64, (c % 4) * 128:(c % 4) * 128 + 128]
                co2 = np.s_[64:128, (c % 4) * 128:(c % 4) * 128 + 128]
                qA = APB(SB[qst][0][0:64, c * 128:(c + 1) * 128], SB[qst][1])
                qB = APB(SB[qst][0][64:128, c * 128:(c + 1) * 128], SB[qst][1])
                P.mm(ps(2, co), sv("STFB", np.s_[0:64, :]), qA)
                P.mm(ps(3, co2), sv("STFB", np.s_[64:128, :]), qB, tile_position=(64, 64))
                P.mm(ps(4, co), sv("STSB", np.s_[0:64, :]), qA)
                P.mm(ps(5, co2), sv("STSB", np.s_[64:128, :]), qB, tile_position=(64, 64))
                if c % 4 == 3:
                    g = c // 4
                    g0, g1 = g * 512, (g + 1) * 512
                    P.dma("sp", sv("CF2"), APB(cfh[2 * hp:2 * hp + 2, g0:g1], cfbl[g]))
                    P.dma("sp", sv("CS2"), APB(csh[2 * hp:2 * hp + 2, g0:g1], csbl[g]))
                    P.mm(ps(6, np.s_[:, :]), sv("E2SB"), sv("CF2"))
                    P.mm(ps(7, np.s_[:, :]), sv("E2SB"), sv("CS2"))
                    P.cp(sv("CFB"), ps(6, np.s_[:, :]), eng="act")
                    P.cp(sv("CSB2"), ps(7, np.s_[:, :]), eng="act")
                    P.tt(sv("TMPA", np.s_[0:64, :]), ps(2, np.s_[0:64, :]),
                         sv("CFB", np.s_[0:64, :]), ALU.mult)
                    P.tt(sv("TMPA", np.s_[64:128, :]), ps(3, np.s_[64:128, :]),
                         sv("CFB", np.s_[64:128, :]), ALU.mult)
                    P.tt(sv("TMPB", np.s_[0:64, :]), ps(4, np.s_[0:64, :]),
                         sv("CSB2", np.s_[0:64, :]), ALU.mult)
                    P.tt(sv("TMPB", np.s_[64:128, :]), ps(5, np.s_[64:128, :]),
                         sv("CSB2", np.s_[64:128, :]), ALU.mult)
                    ob = sv("OCB", np.s_[:, 0:512]) if g % 2 == 0 else sv("OCB2")
                    P.tt(ob, sv("TMPA"), sv("TMPB"), ALU.add)
                    P.dma("act", APB(ofh[hp * 128:(hp + 1) * 128, g0:g1], ofbl[g]), ob,
                          dst_name="ofm9@%d" % (g % 2))

    if has("p7"):
        ln_phase("xB", "ln")
    if has("p8"):
        p8_l1_proj()
    if has("p9"):
        p9_recurrence()
    if has("p10"):
        wo = wload("wo_r", ["A6", "A7"])
        proj_residual("ofm", wo, "xB", "xA")

    def p12_mla_proj():
        wqm = wload("wq_m", ["A0", "A1"])
        wdkvh, wdkvb = EXT["wdkv"]
        h2, b2_ = SB["A2"]
        P.dma("pool", APB(h2[:, 0:4096], b2_),
              APB(wdkvh.ap().rearrange("(kc p) m -> p kc m", p=128), wdkvb))
        wdkv = lambda kc, c0, c1: APB(h2[:, kc * 512 + c0:kc * 512 + c1], b2_)
        wukh, wukb = EXT["wuk"]
        h3_, b3_ = SB["A3"]
        P.dma("pool", APB(h3_[:, 0:4096], b3_),
              APB(wukh.ap().rearrange("(kc p) m -> p kc m", p=128), wukb))
        wuk = lambda kc, c0, c1: APB(h3_[:, kc * 1024 + c0:kc * 1024 + c1], b3_)
        wuvh, wuvb = EXT["wuv"]
        h4_, b4_ = SB["A4"]
        P.dma("pool", APB(h4_[:, 0:4096], b4_),
              APB(wuvh.ap().rearrange("(kc p) m -> p kc m", p=128), wuvb))
        wuv = lambda kc, c0, c1: APB(h4_[:, kc * 1024 + c0:kc * 1024 + c1], b4_)
        hb, hbl = DRAM["hbf"]
        qh, qbl = DRAM["qfm"]
        ch, cbl = DRAM["cfm"]
        kh, kbl = DRAM["kfm"]
        vph, vpbl = DRAM["vpack1"]
        for t in range(NTT):
            t0, t1 = t * TT, (t + 1) * TT
            HT = "HT0" if t % 2 == 0 else "HT1"
            load_fm_tile("sp", HT, "hbf", t)
            hT = lambda kc: sv(HT, np.s_[:, kc * TT:(kc + 1) * TT])
            for m in range(NKC):
                bank = 0 + (m % 2)
                for kc in range(NKC):
                    P.mm(ps(bank, np.s_[:, :]), wqm(kc, m * 128, (m + 1) * 128), hT(kc),
                         start=(kc == 0), stop=(kc == NKC - 1))
                ob = sv("OCB", np.s_[:, 0:512]) if m % 2 == 0 else sv("OCB2")
                P.cp(ob, ps(bank, np.s_[:, :]), eng="act")
                P.dma("act", APB(qh[m * 128:(m + 1) * 128, t0:t1], qbl[t]), ob)
            for m in range(4):
                bank = 2 + (m % 2)
                for kc in range(NKC):
                    P.mm(ps(bank, np.s_[:, :]), wdkv(kc, m * 128, (m + 1) * 128), hT(kc),
                         start=(kc == 0), stop=(kc == NKC - 1))
                P.cp(sv("HID", np.s_[:, m * TT:(m + 1) * TT]), ps(bank, np.s_[:, :]), eng="act")
                P.dma("act", APB(ch[m * 128:(m + 1) * 128, t0:t1], cbl[t]),
                      sv("HID", np.s_[:, m * TT:(m + 1) * TT]))
            cT = lambda cc: sv("HID", np.s_[:, cc * TT:(cc + 1) * TT])
            for m in range(NKC):
                bank = 4 + (m % 2)
                for cc in range(4):
                    P.mm(ps(bank, np.s_[:, :]), wuk(cc, m * 128, (m + 1) * 128), cT(cc),
                         start=(cc == 0), stop=(cc == 3))
                ob = sv("OCB", np.s_[:, 0:512]) if m % 2 == 0 else sv("OCB2")
                P.cp(ob, ps(bank, np.s_[:, :]), eng="act")
                P.dma("act", APB(kh[m * 128:(m + 1) * 128, t0:t1], kbl[t]), ob)
            for s4 in range(4):
                c = t * 4 + s4
                for half in range(2):
                    bank = 6 + half
                    for cc in range(4):
                        P.mm(ps(bank, np.s_[:, :]),
                             sv("HID", np.s_[:, cc * TT + s4 * 128:cc * TT + (s4 + 1) * 128]),
                             wuv(cc, half * 512, (half + 1) * 512),
                             start=(cc == 0), stop=(cc == 3))
                    P.mset(sv("OCB", np.s_[:, 0:520]), 1.0, eng="pool")
                    ocv = SB["OCB"][0][:, 0:520].rearrange("p (g k) -> p g k", k=65)
                    P.cp(APB(ocv[:, 0:8, 0:64], SB["OCB"][1]),
                         APB(PS[bank][0][:, :].rearrange("p (g k) -> p g k", k=64), PS[bank][1]),
                         eng="act")
                    for j in range(8):
                        head = half * 8 + j
                        P.dma("act", APB(vph[head, :, c, :], vpbl[t]),
                              APB(ocv[:, j, 0:65], SB["OCB"][1]))

    def p13_mla_attention():
        qh, qbl = DRAM["qfm"]
        kh, kbl = DRAM["kfm"]
        vph, vpbl = DRAM["vpack1"]
        ofh, ofbl = DRAM["ofm"]
        wom = wload("wo_m", ["A0", "A1"])
        for qt in range(NTT):
            q0, q1 = qt * TT, (qt + 1) * TT
            nkt = 4 * qt + 4
            for hp in range(8):
                ksl = "A5" if hp % 2 == 0 else "A2"
                va = "A6" if hp % 2 == 0 else "A3"
                vb = "A7" if hp % 2 == 0 else "A4"
                P.dma("sp", sv(ksl, np.s_[:, 0:nkt * 128]),
                      APB(kh[hp * 128:(hp + 1) * 128, 0:nkt * 128], kbl))
                P.dma("sp", sv(va, np.s_[:, 0:nkt * 65]),
                      APB(vph[2 * hp, :, 0:nkt, :].rearrange("p c k -> p (c k)"), vpbl))
                P.dma("sp", sv(vb, np.s_[:, 0:nkt * 65]),
                      APB(vph[2 * hp + 1, :, 0:nkt, :].rearrange("p c k -> p (c k)"), vpbl))
                qb = "CBC" if hp % 2 == 0 else "CFB"
                P.dma("sp", sv(qb), APB(qh[hp * 128:(hp + 1) * 128, q0:q1], qbl[qt]))
                def emit_S(kt):
                    par = kt % 2
                    for head, base in ((0, 0), (1, 64)):
                        bank = (0 if par == 0 else 4) + head
                        P.mm(ps(bank, np.s_[:, :]),
                             sv(ksl, np.s_[base:base + 64, kt * 128:(kt + 1) * 128]),
                             sv(qb, np.s_[base:base + 64, :]))
                def emit_exp(kt):
                    par = kt % 2
                    for head in range(2):
                        bank = (0 if par == 0 else 4) + head
                        pts = np.s_[:, par * 1024 + head * 512:par * 1024 + (head + 1) * 512]
                        P.act(sv("PTS", pts), ps(bank, np.s_[:, :]), AF.Exp, scale=0.125)
                        if kt >= 4 * qt:
                            i = kt - 4 * qt
                            P.tt(sv("PTS", pts), sv("PTS", pts),
                                 sv("MKSB", np.s_[:, i * 512:(i + 1) * 512]), ALU.mult)
                def emit_PV(kt):
                    par = kt % 2
                    for head in range(2):
                        pts = np.s_[:, par * 1024 + head * 512:par * 1024 + (head + 1) * 512]
                        vsl = va if head == 0 else vb
                        P.mm(ps(2 + head, np.s_[0:65, :]),
                             sv(vsl, np.s_[:, kt * 65:(kt + 1) * 65]),
                             sv("PTS", pts), start=(kt == 0), stop=(kt == nkt - 1))
                emit_S(0)
                for kt in range(nkt):
                    emit_exp(kt)
                    if kt + 1 < nkt:
                        emit_S(kt + 1)
                    emit_PV(kt)
                for head, (rn, sq) in enumerate((("SBC", "SQ0"), ("TMPA", "SQ1"))):
                    P.act(sv(rn, np.s_[64:65, :]), ps(2 + head, np.s_[64:65, :]), AF.Ln)
                    P.act(sv(rn, np.s_[64:65, :]), sv(rn, np.s_[64:65, :]), AF.Exp, scale=-1.0)
                    P.mm(ps(4 + head, np.s_[0:64, :]), ONE_F(np.s_[64:65, 0:64]),
                         sv(rn, np.s_[64:65, :]))
                    P.cp(sv(sq, np.s_[0:64, :]), ps(4 + head, np.s_[0:64, :]))
                    ob = sv("OCB", np.s_[0:64, 0:512]) if head == 0 else sv("OCB2", np.s_[0:64, :])
                    P.tt(ob, ps(2 + head, np.s_[0:64, :]), sv(sq, np.s_[0:64, :]), ALU.mult)
                    P.dma("act", APB(ofh[hp * 128 + head * 64:hp * 128 + head * 64 + 64, q0:q1],
                                     ofbl[qt]), ob, dst_name="ofm@%d" % (qt % 2))
            proj_residual_tile(qt, "ofm", wom, "xC", "xB", banks=(6, 7))

    def moe_gating_l1():
        gwh, gwb = EXT["gatew"]
        P.dma("pool", sv("WREDS", np.s_[:, 0:64]),
              APB(gwh.ap().rearrange("(kc p) m -> p kc m", p=128), gwb))
        hb, hbl = DRAM["hbf"]
        for t in range(NTT):
            t0, t1 = t * TT, (t + 1) * TT
            HT = "HT0" if t % 2 == 0 else "HT1"
            load_fm_tile("sp", HT, "hbf", t)
            for s4 in range(4):
                c = t * 4 + s4
                for kc in range(NKC):
                    P.mm(ps(1, np.s_[0:128, 0:8]),
                         sv(HT, np.s_[:, kc * TT + s4 * 128:kc * TT + (s4 + 1) * 128]),
                         APB(SB["WREDS"][0][:, kc * 8:(kc + 1) * 8], SB["WREDS"][1]),
                         start=(kc == 0), stop=(kc == NKC - 1))
                P.red(sv("R16A", np.s_[:, 1:2]), ps(1, np.s_[0:128, 0:8]), ALU.max)
                P.ts(sv("R16B", np.s_[:, 0:8]), ps(1, np.s_[0:128, 0:8]),
                     sv("R16A", np.s_[:, 1:2]), ALU.is_ge)
                P.stt(sv("R16C", np.s_[:, 0:8]), sv("R16B", np.s_[:, 0:8]), -1e30,
                      ps(1, np.s_[0:128, 0:8]), ALU.mult, ALU.add)
                P.red(sv("R16A", np.s_[:, 2:3]), sv("R16C", np.s_[:, 0:8]), ALU.max)
                P.ts(sv("R16D", np.s_[:, 0:8]), sv("R16C", np.s_[:, 0:8]),
                     sv("R16A", np.s_[:, 2:3]), ALU.is_ge)
                P.tt(sv("R16A", np.s_[:, 3:4]), sv("R16A", np.s_[:, 1:2]),
                     sv("R16A", np.s_[:, 2:3]), ALU.subtract)
                P.act(sv("R16A", np.s_[:, 4:5]), sv("R16A", np.s_[:, 3:4]), AF.Sigmoid)
                P.ts(sv("R16A", np.s_[:, 5:6]), sv("R16A", np.s_[:, 4:5]), -1.0,
                     ALU.mult, 1.0, ALU.add)
                P.ts(sv("R16B", np.s_[:, 0:8]), sv("R16B", np.s_[:, 0:8]),
                     sv("R16A", np.s_[:, 4:5]), ALU.mult)
                P.ts(sv("R16D", np.s_[:, 0:8]), sv("R16D", np.s_[:, 0:8]),
                     sv("R16A", np.s_[:, 5:6]), ALU.mult)
                P.tt(sv("R16C", np.s_[:, 0:8]), sv("R16B", np.s_[:, 0:8]),
                     sv("R16D", np.s_[:, 0:8]), ALU.add)
                P.tp(ps(2, np.s_[0:8, 0:128]), sv("R16C", np.s_[0:128, 0:8]), sv("IDENT"))
                P.cp(sv("CSB", np.s_[0:8, 128 * c:128 * c + 128]), ps(2, np.s_[0:8, 0:128]),
                     eng="act")
        ch, cbl = DRAM["cexp"]
        for t in range(NTT):
            P.dma("sp", APB(ch[:, t * TT:(t + 1) * TT], cbl[t]),
                  sv("CSB", np.s_[:, t * TT:(t + 1) * TT]), dst_name="cexp@%d" % (t % 2))

    def final_epilogue():
        mh, mbl = DRAM["macc"]
        for t in range(NTT):
            t0, t1 = t * TT, (t + 1) * TT
            for m in range(NKC):
                mv = sv("CFB") if m % 2 == 0 else sv("CSB2")
                P.dma("sp", mv, APB(mh[m * 128:(m + 1) * 128, t0:t1], mbl[t]))
                xv = sv("XT0") if m % 2 == 0 else sv("XT1")
                P.dma("pool", xv, x_apb("xA", m, t0, t1, t))
                bv = sv("TMPA") if m % 2 == 0 else sv("TMPB")
                P.dma("act", bv, x_apb("xB", m, t0, t1, t))
                ov = sv("SQ0") if m % 2 == 0 else sv("SQ1")
                P.tt(ov, mv, xv, ALU.add)
                P.tt(ov, ov, bv, ALU.add)
                P.dma("act", x_apb("y", m, t0, t1, t), ov, dst_name="y@%d" % (t % 2))

    if has("p11"):
        ln_phase("xA", "ln+rms", g="G1T", ln_out="xC")
    if has("p12"):
        p12_mla_proj()
    if has("p13"):
        p13_mla_attention()
    if has("p15"):
        ln_phase("xB", "rms", g="G2T")
    if has("p16"):
        moe_gating_l1()
        moe_experts(1)
        final_epilogue()



    return nc, P, EXT, DRAM, SB, y


def _emit_dbg(nc, P, DRAM, dbg):
    outs = []
    for name in dbg:
        h, bl = DRAM[name]
        sh = list(h.shape) if hasattr(h, "shape") else None
        o = nc.dram_tensor("dbg_" + name, sh, h.dtype)
    return outs


def _build_full(phases=None, dbg=()):
    nc, P, EXT, DRAM, SB, y = build(phases, dbg)
    for name in dbg:
        h, bl = DRAM[name]
        o = nc.dram_tensor("dbg_" + name, list(h.shape), h.dtype, kind="ExternalOutput")
        P.dma("sp", APB(o.ap(), Buf("dbg")), APB(h.ap(), bl[0]))
        for b in bl[1:]:
            P._record("sp", lambda E: E.nop(), [b], [])
    P.finalize_and_emit()
    return nc


def kernel(**inputs):
    S = _host_prep(inputs)
    x = np.asarray(inputs["L0_attn_x"], dtype=np.float32)
    nc = _build_full()
    in_maps = []
    for b in range(8):
        m = dict(S)
        m["x"] = np.ascontiguousarray(x[b].T)
        in_maps.append(m)
    res = run_bass_kernel_spmd(nc, in_maps, core_ids=list(range(8)))
    out = np.stack([np.asarray(r["y"]).T for r in res.results], axis=0)
    return out


# version 12
# speedup vs baseline: 1.0358x; 1.0183x over previous
import numpy as np
import ml_dtypes
from contextlib import ExitStack

import concourse.bass as bass
import concourse.mybir as mybir
from concourse.bass_utils import run_bass_kernel_spmd

F32 = mybir.dt.float32
BF16 = mybir.dt.bfloat16
AF = mybir.ActivationFunctionType
ALU = mybir.AluOpType
AX = mybir.AxisListType
BFNP = ml_dtypes.bfloat16

P3_LIMIT = (8, 16, 5)
T = 4096
DM = 1024
NKC = 8
TT = 512
NTT = 8


class Buf:
    __slots__ = ("w", "rd", "name")
    def __init__(self, name=""):
        self.w = None
        self.rd = {}
        self.name = name

class APB:
    __slots__ = ("ap", "buf")
    def __init__(self, ap, buf):
        self.ap = ap
        self.buf = buf

def _a(x):
    return x.ap if isinstance(x, APB) else x

def _small(*xs):
    for x in xs:
        if isinstance(x, APB):
            try:
                if x.ap.free_size < 128:
                    return True
            except Exception:
                return True
    return False

def _bufs(*xs):
    out = []
    for x in xs:
        if isinstance(x, APB) and x.buf is not None:
            if isinstance(x.buf, list):
                out.extend(x.buf)
            else:
                out.append(x.buf)
    return out

class Rec:
    __slots__ = ("emit", "waits", "eff", "tick", "is_dma", "inc")
    def __init__(self, emit, waits, eff, tick, is_dma):
        self.emit = emit
        self.waits = waits
        self.eff = eff
        self.tick = tick
        self.is_dma = is_dma
        self.inc = None

ENG_ORDER = ["pe", "dve", "act", "pool", "sp"]
ENG_SECTION = {"pe": "tensor", "dve": "vector", "act": "scalar",
               "pool": "gpsimd", "sp": "sync"}

class Prog:
    def __init__(self, nc):
        self.nc = nc
        self.streams = list(ENG_ORDER)
        self.sidx = {n: i for i, n in enumerate(self.streams)}
        self.cnt = [0] * 5
        self.known = [dict() for _ in range(5)]
        self.recs = {n: [] for n in ENG_ORDER}
        self.needed = None
        self.dma_issuer = {}

    def _stream_for_dma(self, queue, dst_name):
        key = "q:%s:%s" % (queue, dst_name)
        if key not in self.sidx:
            self.sidx[key] = len(self.streams)
            self.streams.append(key)
            self.cnt.append(0)
            self.dma_issuer[self.sidx[key]] = self.sidx[queue]
        return self.sidx[key]

    def _record(self, eng, emit, reads, writes, eff=None, is_dma=False, small=True):
        ei = self.sidx[eng]
        if eff is None:
            eff = ei
        need = {}
        def add(st, t, src_small=True):
            if st == ei and st < 5 and not is_dma:
                if ei in (0, 2):
                    return
                if ei == 1 and not src_small:
                    return
            if st >= 5:
                t = self.cnt[st]
            if self.known[ei].get(st, 0) >= t:
                return
            if need.get(st, 0) < t:
                need[st] = t
        for b in reads:
            if b.w:
                add(b.w[0], b.w[1], b.w[2])
        for b in writes:
            if b.w:
                if not (is_dma and b.w[0] == eff):
                    add(b.w[0], b.w[1], b.w[2])
            for st, (t, sm) in b.rd.items():
                add(st, t, sm)
        for st, t in need.items():
            self.known[ei][st] = t
        if is_dma:
            self.cnt[eff] += 16
        else:
            self.cnt[eff] += 1
        tick = self.cnt[eff]
        rec = Rec(emit, sorted(need.items()), eff, tick, is_dma)
        self.recs[eng].append(rec)
        for b in reads:
            prev = b.rd.get(eff)
            b.rd[eff] = (tick, small or (prev[1] if prev else False))
        for b in writes:
            b.w = (eff, tick, small)
            b.rd = {}
        return rec

    def mm(self, out, lhsT, rhs, start=True, stop=True, tile_position=None):
        oa, la, ra = _a(out), _a(lhsT), _a(rhs)
        tp_ = tile_position
        def emit(E, oa=oa, la=la, ra=ra, start=start, stop=stop, tp_=tp_):
            return E.matmul(oa, la, ra, start=start, stop=stop,
                            tile_position=tp_, skip_group_check=True)
        self._record("pe", emit, _bufs(lhsT, rhs), _bufs(out), small=False)

    def tp(self, out, in_, ident):
        oa, ia, da = _a(out), _a(in_), _a(ident)
        def emit(E, oa=oa, ia=ia, da=da):
            return E.transpose(oa, ia, da)
        self._record("pe", emit, _bufs(in_, ident), _bufs(out), small=False)

    def act(self, out, in_, func, bias=0.0, scale=1.0, eng="act"):
        oa, ia = _a(out), _a(in_)
        ba, sa = _a(bias), _a(scale)
        def emit(E, oa=oa, ia=ia, func=func, ba=ba, sa=sa):
            return E.activation(oa, ia, func, bias=ba, scale=sa)
        self._record(eng, emit, _bufs(in_, bias, scale), _bufs(out),
                     small=_small(out, in_))

    def cp(self, out, in_, eng="dve"):
        oa, ia = _a(out), _a(in_)
        if eng == "act":
            def emit(E, oa=oa, ia=ia):
                return E.copy(oa, ia)
        else:
            def emit(E, oa=oa, ia=ia):
                return E.tensor_copy(oa, ia)
        self._record(eng, emit, _bufs(in_), _bufs(out), small=_small(out, in_))

    def tt(self, out, a, b, alu, eng="dve"):
        oa, aa, ba = _a(out), _a(a), _a(b)
        def emit(E, oa=oa, aa=aa, ba=ba, alu=alu):
            return E.tensor_tensor(oa, aa, ba, alu)
        self._record(eng, emit, _bufs(a, b), _bufs(out), small=_small(out, a, b))

    def ts(self, out, a, s1, op0, s2=None, op1=None, eng="dve"):
        oa, aa = _a(out), _a(a)
        s1a, s2a = _a(s1), _a(s2) if s2 is not None else None
        if s2 is None:
            def emit(E, oa=oa, aa=aa, s1a=s1a, op0=op0):
                return E.tensor_single_scalar(oa, aa, s1a, op0)
        else:
            def emit(E, oa=oa, aa=aa, s1a=s1a, s2a=s2a, op0=op0, op1=op1):
                return E.tensor_scalar(oa, aa, s1a, s2a, op0, op1)
        self._record(eng, emit, _bufs(a, s1, s2), _bufs(out), small=_small(out, a))

    def stt(self, out, in0, scalar, in1, op0, op1, eng="dve"):
        oa, i0, sc, i1 = _a(out), _a(in0), _a(scalar), _a(in1)
        def emit(E, oa=oa, i0=i0, sc=sc, i1=i1, op0=op0, op1=op1):
            return E.scalar_tensor_tensor(oa, i0, sc, i1, op0, op1)
        self._record(eng, emit, _bufs(in0, scalar, in1), _bufs(out),
                     small=_small(out, in0, in1))

    def red(self, out, in_, alu, negate=False, absval=False, eng="dve"):
        oa, ia = _a(out), _a(in_)
        def emit(E, oa=oa, ia=ia, alu=alu, negate=negate, absval=absval):
            return E.tensor_reduce(oa, ia, AX.X, alu,
                                   apply_absolute_value=absval, negate=negate)
        self._record(eng, emit, _bufs(in_), _bufs(out), small=_small(out, in_))

    def recip(self, out, in_):
        oa, ia = _a(out), _a(in_)
        def emit(E, oa=oa, ia=ia):
            return E.reciprocal(oa, ia)
        self._record("dve", emit, _bufs(in_), _bufs(out), small=_small(out, in_))

    def mset(self, dst, val, eng="pool"):
        oa = _a(dst)
        def emit(E, oa=oa, val=val):
            return E.memset(oa, val)
        self._record(eng, emit, [], _bufs(dst), small=_small(dst))

    def barrier(self):
        snap = [(st, self.cnt[st]) for st in range(len(self.streams)) if self.cnt[st] > 0]
        for eng in ENG_ORDER:
            ei = self.sidx[eng]
            waits = []
            for st, v in snap:
                if st == ei:
                    continue
                if self.known[ei].get(st, 0) >= v:
                    continue
                waits.append((st, v))
                self.known[ei][st] = v
            def emit(E):
                return E.nop()
            self.cnt[ei] += 1
            rec = Rec(emit, sorted(waits), ei, self.cnt[ei], False)
            self.recs[eng].append(rec)

    def dma(self, queue, dst, src, dst_name=None, accum=None):
        oa, ia = _a(dst), _a(src)
        if dst_name is None:
            dst_name = oa.tensor.name if hasattr(oa, "tensor") else "anon"
        st = self._stream_for_dma(queue, dst_name)
        if accum is None:
            def emit(E, oa=oa, ia=ia):
                return E.dma_start(oa, ia)
        else:
            def emit(E, oa=oa, ia=ia, accum=accum):
                return E.dma_start(oa, ia, accum_op=accum)
        self._record(queue, emit, _bufs(src), _bufs(dst), eff=st, is_dma=True)

    def finalize_and_emit(self):
        nc = self.nc
        needed = [set() for _ in range(len(self.streams))]
        for eng in ENG_ORDER:
            for rec in self.recs[eng]:
                for st, v in rec.waits:
                    if st < 5:
                        needed[st].add(v)
        import bisect
        ranks = {}
        for si in range(5):
            ranks[si] = sorted(needed[si])
        def remap(st, v):
            if st >= 5:
                return v
            return bisect.bisect_right(ranks[st], v)
        for eng in ENG_ORDER:
            ei = self.sidx[eng]
            for rec in self.recs[eng]:
                rec.waits = [(st, remap(st, v)) for st, v in rec.waits]
                if rec.is_dma:
                    rec.inc = 16
                elif rec.tick in needed[rec.eff]:
                    rec.inc = 1
        final_sp = [(st, self.cnt[st]) for st in range(5, len(self.streams))
                    if self.cnt[st] > 0]

        with ExitStack() as ctx:
            sems = []
            for i, name in enumerate(self.streams):
                sems.append(ctx.enter_context(nc.semaphore("sem%d" % i)))
            block = ctx.enter_context(nc.Block())
            for eng in ENG_ORDER:
                recs = self.recs[eng]
                if not recs and eng != "sp":
                    continue
                sec = getattr(block, ENG_SECTION[eng])
                def body(E, recs=recs, eng=eng):
                    for rec in recs:
                        for st, v in rec.waits:
                            E.wait_ge(sems[st], v)
                        ins = rec.emit(E)
                        if rec.inc:
                            ins.then_inc(sems[rec.eff], rec.inc)
                    if eng == "sp":
                        for st, v in final_sp:
                            E.wait_ge(sems[st], v)
                sec(body)
        return nc



def _hs_perms():
    p_hs = np.concatenate([np.arange(0, 64, 2), np.arange(1, 64, 2)])
    p_sw = np.concatenate([p_hs[32:], p_hs[:32]])
    return p_hs, p_sw

def _permute_cols(w, perm):
    nh = w.shape[1] // 64
    w2 = np.asarray(w).reshape(w.shape[0], nh, 64)
    return np.ascontiguousarray(w2[:, :, perm].reshape(w.shape[0], -1))

def _host_prep(I):
    S = {}
    f32 = lambda a: np.ascontiguousarray(np.asarray(a, dtype=np.float32))
    p_hs, p_sw = _hs_perms()
    S["wq1a"] = _permute_cols(I["L0_attn_Wq1"], p_hs)
    S["wq1b"] = _permute_cols(I["L0_attn_Wq1"], p_sw)
    S["wq2a"] = _permute_cols(I["L0_attn_Wq2"], p_hs)
    S["wq2b"] = _permute_cols(I["L0_attn_Wq2"], p_sw)
    S["wk1a"] = _permute_cols(I["L0_attn_Wk1"], p_hs)
    S["wk1b"] = _permute_cols(I["L0_attn_Wk1"], p_sw)
    S["wk2a"] = _permute_cols(I["L0_attn_Wk2"], p_hs)
    S["wk2b"] = _permute_cols(I["L0_attn_Wk2"], p_sw)
    S["wv0"] = f32(I["L0_attn_Wv"])
    S["wout0"] = f32(I["L0_attn_Wout"])
    S["bout_t"] = f32(np.asarray(I["L0_attn_bout"]).reshape(8, 128).T)
    S["lrow"] = f32(np.stack([I["L0_attn_lq1"], I["L0_attn_lk1"],
                              I["L0_attn_lq2"], I["L0_attn_lk2"]], 0))
    pos = np.arange(T, dtype=np.float64)
    div = np.exp(np.arange(0, 64, 2, dtype=np.float64) * (-np.log(10000.0) / 64.0))
    ang = div[:, None] * pos[None, :]
    cos = np.cos(ang).astype(np.float32)
    sin = np.sin(ang).astype(np.float32)
    S["cc"] = np.ascontiguousarray(np.tile(cos, (4, 1))).astype(BFNP)
    S["ss"] = np.ascontiguousarray(np.tile(np.concatenate([-sin, sin], 0), (2, 1))).astype(BFNP)
    S["wred"] = f32(I["L0_ffn_w_red"])
    S["wg"] = f32(I["L0_ffn_wg"])
    S["wgt"] = f32(np.asarray(I["L0_ffn_wg"]).T)
    S["w1_0"] = f32(I["L0_ffn_W1"])
    S["w2_0"] = f32(I["L0_ffn_W2"])
    S["b1t_0"] = f32(np.asarray(I["L0_ffn_b1"]).reshape(8, 16, 128).transpose(0, 2, 1))
    S["b2_0"] = f32(I["L0_ffn_b2"])
    S["wq_r"] = f32(I["L1_attn_Wq"])
    S["wk_r"] = f32(I["L1_attn_Wk"])
    S["wv_r"] = f32(I["L1_attn_Wv"])
    S["wo_r"] = f32(I["L1_attn_Wo"])
    wg5 = np.concatenate([I["L1_attn_Wb"], I["L1_attn_Wfd"], I["L1_attn_Wsd"],
                          I["L1_attn_Wfg"], I["L1_attn_Wsg"]], axis=1)
    S["wg5"] = f32(wg5)
    brow = np.zeros((1, 80), np.float32)
    brow[0, 16:32] = np.asarray(I["L1_attn_fd_bias"])
    brow[0, 32:48] = np.asarray(I["L1_attn_sd_bias"])
    S["brow5"] = brow
    S["g1t"] = f32(np.asarray(I["L1_ffn_g1"]).reshape(8, 128).T)
    S["g2t"] = f32(np.asarray(I["L1_ffn_g2"]).reshape(8, 128).T)
    S["wq_m"] = f32(I["L1_ffn_Wq"])
    S["wdkv"] = f32(I["L1_ffn_Wdkv"])
    S["wuk"] = f32(I["L1_ffn_Wuk"])
    S["wuv"] = f32(I["L1_ffn_Wuv"])
    S["wo_m"] = f32(I["L1_ffn_Wo"])
    S["gatew"] = f32(I["L1_ffn_gate_w"])
    S["w1_1"] = f32(I["L1_ffn_w1"])
    S["w2_1"] = f32(I["L1_ffn_w2"])
    S["ident"] = np.eye(128, dtype=np.float32)
    e2 = np.zeros((2, 128), np.float32)
    e2[0, 0:64] = 1.0
    e2[1, 64:128] = 1.0
    S["e2"] = e2.astype(BFNP)
    mk = np.zeros((4, 128, 512), np.float32)
    kp = np.arange(128)[:, None]
    qf = np.arange(512)[None, :]
    for i in range(4):
        mk[i] = (qf >= kp + 128 * i).astype(np.float32)
    S["mk"] = mk.astype(BFNP)
    return S



def build(phases=None, dbg=()):
    nc = bass.Bass("TRN2", target_bir_lowering=False)
    P = Prog(nc)

    def has(name):
        return phases is None or name in phases

    EXT = {}
    def ein(name, shape, dt=F32):
        h = nc.dram_tensor(name, list(shape), dt, kind="ExternalInput")
        EXT[name] = (h, Buf(name))
        return EXT[name]
    def eap(name):
        h, b = EXT[name]
        return h, b

    ein("x", (DM, T))
    for n, shape in [
        ("wq1a", (DM, DM)), ("wq1b", (DM, DM)), ("wq2a", (DM, DM)), ("wq2b", (DM, DM)),
        ("wk1a", (DM, 128)), ("wk1b", (DM, 128)), ("wk2a", (DM, 128)), ("wk2b", (DM, 128)),
        ("wv0", (DM, 128)), ("wout0", (DM, DM)), ("bout_t", (128, 8)), ("lrow", (4, 64)),
        ("wred", (DM, 16)), ("wg", (8, 16)), ("wgt", (16, 8)),
        ("w1_0", (8, DM, 2048)), ("w2_0", (8, 2048, DM)),
        ("b1t_0", (8, 128, 16)), ("b2_0", (8, DM)),
        ("wq_r", (DM, DM)), ("wk_r", (DM, DM)), ("wv_r", (DM, DM)), ("wo_r", (DM, DM)),
        ("wg5", (DM, 80)), ("brow5", (1, 80)),
        ("g1t", (128, 8)), ("g2t", (128, 8)),
        ("wq_m", (DM, DM)), ("wdkv", (DM, 512)), ("wuk", (512, DM)), ("wuv", (512, DM)),
        ("wo_m", (DM, DM)), ("gatew", (DM, 8)),
        ("w1_1", (8, DM, 2048)), ("w2_1", (8, 2048, DM)),
        ("ident", (128, 128)),
    ]:
        ein(n, shape)
    ein("e2", (2, 128), BF16)
    ein("cc", (128, T), BF16)
    ein("ss", (128, T), BF16)
    ein("mk", (4, 128, 512), BF16)

    y = nc.dram_tensor("y", [DM, T], F32, kind="ExternalOutput")
    y_b = [Buf("y%d" % i) for i in range(NTT)]

    DRAM = {}
    def dscr(name, shape, dt, nb=NTT):
        h = nc.dram_tensor(name, list(shape), dt)
        DRAM[name] = (h, [Buf("%s/%d" % (name, i)) for i in range(nb)])
        return DRAM[name]
    dscr("xA", (DM, T), F32)
    dscr("xB", (DM, T), F32)
    dscr("xC", (DM, T), F32)
    dscr("hbf", (DM, T), BF16)
    dscr("q1r", (DM, T), BF16)
    dscr("q2r", (DM, T), BF16)
    dscr("k1r", (128, T), BF16)
    dscr("k2r", (128, T), BF16)
    dscr("vpack0", (2, 128, 32, 65), BF16)
    dscr("a0", (DM, T), BF16)
    dscr("scores2d", (32, 128), F32, nb=NTT)
    dscr("macc", (DM, T), BF16)
    dscr("qfm", (DM, T), BF16)
    dscr("kpr", (8, 128, 32, 128), BF16)
    dscr("vpr", (8, 128, 32, 128), BF16)
    dscr("cfr", (16, T), BF16)
    dscr("csr", (16, T), BF16)
    dscr("ofm", (DM, T), BF16)
    dscr("cfm", (512, T), BF16)
    dscr("kfm", (DM, T), BF16)
    dscr("vpack1", (16, 128, 32, 65), BF16)
    dscr("cexp", (8, T), BF16)
    dscr("scoresf", (1, T), F32)

    def dv(name, idx, k=0):
        h, bl = DRAM[name]
        return APB(h[idx], bl[k])
    def dvm(name, idx, ks):
        h, bl = DRAM[name]
        return h[idx], [bl[k] for k in ks]

    SB = {}
    def sal(name, shape, dt):
        h = nc.alloc_sbuf_tensor(name, list(shape), dt)
        SB[name] = (h, Buf(name))
        return SB[name]
    def sv(name, idx=np.s_[:, :]):
        h, b = SB[name]
        return APB(h[idx], b)

    for i in range(12):
        sal("A%d" % i, (128, 4096), BF16)
    sal("HT0", (128, NKC * TT), BF16)
    sal("HT1", (128, NKC * TT), BF16)
    sal("XT0", (128, TT), F32)
    sal("XT1", (128, TT), F32)
    sal("SQ0", (128, TT), F32)
    sal("SQ1", (128, TT), F32)
    sal("HID", (128, 16 * TT), BF16)
    sal("OUTS", (128, 8 * TT), BF16)
    sal("OCB2", (128, 512), BF16)
    sal("CBC", (128, TT), BF16)
    sal("SBC", (128, TT), F32)
    sal("ROWS", (1, 2048), F32)
    sal("CSB", (8, T), BF16)
    sal("XRED", (16, TT), BF16)
    sal("PSI", (128, TT), F32)
    sal("SUR", (128, TT), F32)
    sal("GT", (128, 80), F32)
    sal("DBCF", (128, 256), F32)
    sal("DBCS", (128, 256), F32)
    sal("DEVN", (1, 512), F32)
    sal("DSLW", (1, 512), F32)
    sal("CCT0", (128, TT), BF16)
    sal("CCT1", (128, TT), BF16)
    sal("SST0", (128, TT), BF16)
    sal("SST1", (128, TT), BF16)
    sal("MKSB", (128, 4 * 512), BF16)
    sal("IDENT", (128, 128), F32)
    sal("ONESF", (128, 128), F32)
    sal("ONESB", (128, 128), BF16)
    sal("C1", (128, 8), F32)
    sal("WGN", (16, 8), BF16)
    sal("WREDS", (128, 8 * 16), BF16)
    sal("B2SB", (8, DM), BF16)
    sal("B1T", (128, 16), F32)
    sal("GROW", (1, 80), BF16)
    sal("E2SB", (2, 128), BF16)
    sal("LSMALL", (4, 64), F32)
    sal("BOUT", (128, 8), F32)
    sal("CROW", (1, 512), BF16)
    sal("G1T", (128, 8), F32)
    sal("G2T", (128, 8), F32)
    sal("TMPA", (128, TT), F32)
    sal("TMPB", (128, TT), F32)
    sal("KTV", (128, 1024), BF16)
    sal("VTV", (128, 1024), BF16)
    sal("R16A", (128, 16), F32)
    sal("R16B", (128, 16), F32)
    sal("R16C", (128, 16), F32)
    sal("R16D", (128, 16), F32)
    sal("STF", (128, 64), F32)
    sal("STS", (128, 64), F32)
    sal("STFB", (128, 64), BF16)
    sal("STSB", (128, 64), BF16)
    sal("KSL", (128, 128), BF16)
    sal("VSL", (128, 128), BF16)
    sal("CFB", (128, 512), BF16)
    sal("CSB2", (128, 512), BF16)
    sal("CF2", (2, 512), BF16)
    sal("CS2", (2, 512), BF16)
    sal("OCB", (128, 520), BF16)
    sal("PTS", (128, 2048), BF16)

    used = 224 * 1024 - 16 * 1024 - nc.sbuf_bytes_remaining if hasattr(nc, "sbuf_bytes_remaining") else -1

    PS = []
    for i in range(8):
        h = nc.alloc_psum_tensor("ps%d" % i, [128, 512], F32)
        PS.append((h, Buf("ps%d" % i)))
    def ps(i, idx=np.s_[:, :]):
        h, b = PS[i]
        return APB(h[idx], b)

    ONE_F = lambda idx=np.s_[:, 0:1]: sv("ONESF", idx)
    ONE_B = lambda idx=np.s_[:, 0:1]: sv("ONESB", idx)

    P.mset(sv("ONESF"), 1.0)
    P.mset(sv("ONESB"), 1.0)
    ih, ib = eap("ident")
    P.dma("sp", sv("IDENT"), APB(ih[:, :], ib))
    P.mset(sv("C1", np.s_[:, 1:2]), -5.0)
    P.mset(sv("C1", np.s_[:, 2:3]), 1e-5)
    P.mset(sv("C1", np.s_[:, 3:4]), 1e-6)
    SKIP_LAM = globals().get("_SKIP_LAM", False)
    lh, lb = eap("lrow")
    P.dma("sp", sv("LSMALL"), APB(lh[:, :], lb))
    if SKIP_LAM:
        P.mset(sv("C1", np.s_[:, 0:1]), -0.5)
    if not SKIP_LAM:
      P.tp(ps(0, np.s_[0:64, 0:4]), sv("LSMALL"), sv("IDENT", np.s_[0:4, 0:4]))
      LT = sv("TMPA", np.s_[0:64, 0:4])
      P.cp(LT, ps(0, np.s_[0:64, 0:4]))
      P.mm(ps(1, np.s_[0:1, 0:1]), APB(SB["TMPA"][0][0:64, 0:1], SB["TMPA"][1]),
         APB(SB["TMPA"][0][0:64, 1:2], SB["TMPA"][1]))
      P.mm(ps(1, np.s_[0:1, 1:2]), APB(SB["TMPA"][0][0:64, 2:3], SB["TMPA"][1]),
         APB(SB["TMPA"][0][0:64, 3:4], SB["TMPA"][1]))
      P.act(sv("ROWS", np.s_[0:1, 0:2]), ps(1, np.s_[0:1, 0:2]), AF.Exp)
      P.tt(sv("ROWS", np.s_[0:1, 2:3]), sv("ROWS", np.s_[0:1, 0:1]), sv("ROWS", np.s_[0:1, 1:2]), ALU.subtract)
      P.ts(sv("ROWS", np.s_[0:1, 2:3]), sv("ROWS", np.s_[0:1, 2:3]), 0.8, ALU.add, 0.1, ALU.max)
      P.ts(sv("ROWS", np.s_[0:1, 2:3]), sv("ROWS", np.s_[0:1, 2:3]), 0.9, ALU.min, -1.0, ALU.mult)
      P.mm(ps(1, np.s_[0:128, 2:3]), ONE_F(np.s_[0:1, 0:128]), sv("ROWS", np.s_[0:1, 2:3]))
      P.cp(sv("C1", np.s_[:, 0:1]), ps(1, np.s_[0:128, 2:3]))
    mh, mb = eap("mk")
    P.dma("sp", APB(SB["MKSB"][0].ap().rearrange("p (a q) -> p a q", q=512), SB["MKSB"][1]),
          APB(mh.ap().rearrange("a p q -> p a q"), mb))
    e2h, e2b = eap("e2")
    P.dma("sp", sv("E2SB"), APB(e2h[:, :], e2b))
    for _n, _s in (("bout_t", "BOUT"), ("g1t", "G1T"), ("g2t", "G2T")):
        _h, _b = eap(_n)
        P.dma("sp", sv(_s), APB(_h[:, :], _b))

    XV = {"x": None}

    def x_apb(name, kc, t0, t1, k):
        if name == "x":
            h, b = eap("x")
            return APB(h[kc * 128:(kc + 1) * 128, t0:t1], b)
        if name == "y":
            return APB(y[kc * 128:(kc + 1) * 128, t0:t1], y_b[k])
        return dv(name, np.s_[kc * 128:(kc + 1) * 128, t0:t1], k)

    def load_fm_tile(queue, dst_name, dram_name, t, nkc=NKC, cw=TT):
        h, bl = DRAM[dram_name]
        hs, bs = SB[dst_name]
        src3 = h.ap().rearrange("(kc p) q -> p kc q", p=128)[:, :, t * cw:(t + 1) * cw]
        dst3 = hs[:, 0:nkc * cw].rearrange("p (kc q) -> p kc q", q=cw)
        P.dma(queue, APB(dst3, bs), APB(src3, bl[t]))

    def ln_phase(src, kind="ln", g=None, ln_out=None):
        eps_ln, eps_rms = 1e-5, 1e-6
        hidf = SB["HID"][0].ap().bitcast(F32)
        hidb = SB["HID"][1]
        xt_ = lambda kc: APB(hidf[:, kc * TT:(kc + 1) * TT], hidb)
        for t in range(NTT):
            t0, t1 = t * TT, (t + 1) * TT
            SQ = [sv("SQ0"), sv("SQ1")]
            h_, bl_ = (EXT["x"][0], [EXT["x"][1]] * NTT) if src == "x" else DRAM[src]
            src3 = h_.ap().rearrange("(kc p) q -> p kc q", p=128)[:, :, t0:t1]
            dst3 = APB(hidf.rearrange("p (kc q) -> p kc q", q=TT), hidb)
            P.dma("sp", dst3, APB(src3, bl_[t]))
            for kc in range(NKC):
                xs = xt_(kc)
                if kind != "rms":
                    P.mm(ps(0, np.s_[0:1, :]), ONE_F(), xs, start=(kc == 0), stop=(kc == NKC - 1))
                P.act(SQ[kc % 2], xs, AF.Square)
                P.mm(ps(1, np.s_[0:1, :]), ONE_F(), SQ[kc % 2], start=(kc == 0), stop=(kc == NKC - 1))
            R = lambda a, b_: sv("ROWS", np.s_[0:1, a:b_])
            if kind == "rms":
                P.act(R(0, 512), ps(1, np.s_[0:1, :]), AF.Ln, bias=sv("C1", np.s_[0:1, 3:4]), scale=1.0 / DM)
                P.act(R(0, 512), R(0, 512), AF.Exp, scale=-0.5)
                P.mm(ps(2, np.s_[:, :]), ONE_F(np.s_[0:1, 0:128]), R(0, 512))
                P.cp(sv("TMPA"), ps(2, np.s_[:, :]), eng="act")
                for kc in range(NKC):
                    xs = xt_(kc)
                    gcol = sv(g, np.s_[:, kc:kc + 1])
                    P.stt(sv("HT0" if t % 2 == 0 else "HT1",
                             np.s_[:, kc * TT:(kc + 1) * TT]), xs, gcol, sv("TMPA"),
                          ALU.mult, ALU.mult)
                    P.dma("act", dv("hbf", np.s_[kc * 128:(kc + 1) * 128, t0:t1], t),
                          sv("HT0" if t % 2 == 0 else "HT1", np.s_[:, kc * TT:(kc + 1) * TT]),
                          dst_name="hbf@%d" % (t % 2))
                continue
            P.ts(R(0, 512), ps(0, np.s_[0:1, :]), 1.0 / DM, ALU.mult)
            P.ts(R(512, 1024), ps(1, np.s_[0:1, :]), 1.0 / DM, ALU.mult)
            P.tt(R(1024, 1536), R(0, 512), R(0, 512), ALU.mult)
            P.tt(R(512, 1024), R(512, 1024), R(1024, 1536), ALU.subtract)
            P.act(R(1024, 1536), R(512, 1024), AF.Ln, bias=sv("C1", np.s_[0:1, 2:3]))
            P.act(R(1024, 1536), R(1024, 1536), AF.Exp, scale=-0.5)
            if kind == "ln+rms":
                P.tt(R(1536, 2048), R(1024, 1536), R(1024, 1536), ALU.mult)
                P.tt(R(1536, 2048), R(1536, 2048), R(512, 1024), ALU.mult)
                P.act(R(1536, 2048), R(1536, 2048), AF.Ln, bias=sv("C1", np.s_[0:1, 3:4]))
                P.act(R(1536, 2048), R(1536, 2048), AF.Exp, scale=-0.5)
            P.tt(R(512, 1024), R(0, 512), R(1024, 1536), ALU.mult)
            P.mm(ps(2, np.s_[:, :]), ONE_F(np.s_[0:1, 0:128]), R(1024, 1536))
            P.mm(ps(3, np.s_[:, :]), ONE_F(np.s_[0:1, 0:128]), R(512, 1024))
            P.cp(sv("TMPA"), ps(2, np.s_[:, :]), eng="act")
            P.cp(sv("TMPB"), ps(3, np.s_[:, :]), eng="act")
            if kind == "ln+rms":
                P.mm(ps(2, np.s_[:, :]), ONE_F(np.s_[0:1, 0:128]), R(1536, 2048))
                P.cp(sv("SBC"), ps(2, np.s_[:, :]), eng="act")
            HT = "HT0" if t % 2 == 0 else "HT1"
            for kc in range(NKC):
                xs = xt_(kc)
                P.tt(sv("SQ0") if kc % 2 == 0 else sv("SQ1"), xs, sv("TMPA"), ALU.mult)
                tv = sv("SQ0") if kc % 2 == 0 else sv("SQ1")
                if kind == "ln":
                    P.tt(sv(HT, np.s_[:, kc * TT:(kc + 1) * TT]), tv, sv("TMPB"), ALU.subtract)
                    P.dma("act", dv("hbf", np.s_[kc * 128:(kc + 1) * 128, t0:t1], t),
                          sv(HT, np.s_[:, kc * TT:(kc + 1) * TT]),
                          dst_name="hbf@%d" % (t % 2))
                else:
                    P.tt(tv, tv, sv("TMPB"), ALU.subtract)
                    P.dma("sp", dv(ln_out, np.s_[kc * 128:(kc + 1) * 128, t0:t1], t), tv,
                          dst_name="%s@%d" % (ln_out, t % 2))
                    gcol = sv(g, np.s_[:, kc:kc + 1])
                    P.stt(sv(HT, np.s_[:, kc * TT:(kc + 1) * TT]), tv, gcol, sv("SBC"),
                          ALU.mult, ALU.mult)
                    P.dma("act", dv("hbf", np.s_[kc * 128:(kc + 1) * 128, t0:t1], t),
                          sv(HT, np.s_[:, kc * TT:(kc + 1) * TT]),
                          dst_name="hbf@%d" % (t % 2))

    def wload(ext_name, slots, rows=DM, cols=DM, row_off=0, e=None):
        h, b = EXT[ext_name]
        nkc = rows // 128
        per = nkc // len(slots)
        assert per * len(slots) == nkc
        for si, sl in enumerate(slots):
            hsl, bsl = SB[sl]
            src = h.ap() if e is None else h[e]
            src = src.rearrange("(kc p) m -> p kc m", p=128)
            P.dma("pool", APB(hsl[:, 0:per * cols], bsl),
                  APB(src[:, row_off // 128 + si * per:(row_off // 128) + (si + 1) * per, 0:cols], b))
        def acc(kc, c0, c1):
            si, ko = kc // per, kc % per
            hsl, bsl = SB[slots[si]]
            return APB(hsl[:, ko * cols + c0:ko * cols + c1], bsl)
        return acc

    if has("p1"):
        ln_phase("x", "ln")

    if has("p2"):
        _mark_p2 = True
        wq1a = wload("wq1a", ["A0", "A1"])
        wq1b = wload("wq1b", ["A2", "A3"])
        wq2a = wload("wq2a", ["A4", "A5"])
        wq2b = wload("wq2b", ["A6", "A7"])
        wk1a = wload("wk1a", ["A8"], cols=128)
        def wk_load(name, slot, coff):
            h, b = EXT[name]
            hsl, bsl = SB[slot]
            src = h.ap().rearrange("(kc p) m -> p kc m", p=128)
            P.dma("pool", APB(hsl[:, coff:coff + 1024], bsl), APB(src, b))
            def acc(kc, c0, c1):
                return APB(hsl[:, coff + kc * 128 + c0:coff + kc * 128 + c1], bsl)
            return acc
        wk1b = wk_load("wk1b", "A8", 1024)
        wk2a = wk_load("wk2a", "A8", 2048)
        wk2b = wk_load("wk2b", "A8", 3072)
        wv0 = wk_load("wv0", "A9", 0)
        cch, ccb = eap("cc")
        ssh, ssb = eap("ss")
        for t in range(NTT):
            t0, t1 = t * TT, (t + 1) * TT
            HT = "HT0" if t % 2 == 0 else "HT1"
            load_fm_tile("sp", HT, "hbf", t)
            CCT = "CCT0" if t % 2 == 0 else "CCT1"
            SST = "SST0" if t % 2 == 0 else "SST1"
            P.dma("sp", sv(CCT), APB(cch[:, t0:t1], ccb))
            P.dma("sp", sv(SST), APB(ssh[:, t0:t1], ssb))
            hT = lambda kc: sv(HT, np.s_[:, kc * TT:(kc + 1) * TT])
            def rope_proj(wa, wb_, dst, mchunks, kdim):
                for m in range(mchunks):
                    for kc in range(NKC):
                        P.mm(ps(0 + (m % 2) * 2, np.s_[:, :]), wa(kc, m * 128, (m + 1) * 128),
                             hT(kc), start=(kc == 0), stop=(kc == NKC - 1))
                        P.mm(ps(1 + (m % 2) * 2, np.s_[:, :]), wb_(kc, m * 128, (m + 1) * 128),
                             hT(kc), start=(kc == 0), stop=(kc == NKC - 1))
                    ob = sv("OCB", np.s_[:, 0:512]) if m % 2 == 0 else sv("OCB2")
                    P.tt(sv("TMPA"), ps(0 + (m % 2) * 2), sv(CCT), ALU.mult)
                    P.tt(sv("TMPB"), ps(1 + (m % 2) * 2), sv(SST), ALU.mult)
                    P.tt(ob, sv("TMPA"), sv("TMPB"), ALU.add)
                    P.dma("act", dv(dst, np.s_[m * 128:(m + 1) * 128, t0:t1], t), ob)
            rope_proj(wq1a, wq1b, "q1r", 8, DM)
            rope_proj(wq2a, wq2b, "q2r", 8, DM)
            rope_proj(wk1a, wk1b, "k1r", 1, 128)
            rope_proj(wk2a, wk2b, "k2r", 1, 128)
            for s4 in range(4):
                for kc in range(NKC):
                    P.mm(ps(4, np.s_[:, 0:128]),
                         sv(HT, np.s_[:, kc * TT + s4 * 128:kc * TT + (s4 + 1) * 128]),
                         wv0(kc, 0, 128), start=(kc == 0), stop=(kc == NKC - 1))
                P.mset(sv("OCB", np.s_[:, 0:130]), 1.0, eng="pool")
                ocv = SB["OCB"][0][:, 0:130].rearrange("p (g k) -> p g k", k=65)
                P.cp(APB(ocv[:, 0:2, 0:64], SB["OCB"][1]),
                     APB(PS[4][0][:, 0:128].rearrange("p (g k) -> p g k", k=64), PS[4][1]),
                     eng="act")
                c = t * 4 + s4
                vp, vpb = DRAM["vpack0"]
                for g in range(2):
                    P.dma("act", APB(vp[g, :, c, :], vpb[t]),
                          APB(ocv[:, g, 0:65], SB["OCB"][1]))

    if has("p3"):
        HP_N, WIN_N, P3_PARTS = P3_LIMIT
        q1h, q1bl = DRAM["q1r"]
        q2h, q2bl = DRAM["q2r"]
        P.dma("sp", sv("A2"), APB(DRAM["k1r"][0].ap(), DRAM["k1r"][1]))
        P.dma("sp", sv("A3"), APB(DRAM["k2r"][0].ap(), DRAM["k2r"][1]))
        vph, vpbl = DRAM["vpack0"]
        P.dma("sp", sv("A4", np.s_[:, 0:2080]),
              APB(vph[0].rearrange("p c k -> p (c k)"), vpbl))
        P.dma("sp", sv("A5", np.s_[:, 0:2080]),
              APB(vph[1].rearrange("p c k -> p (c k)"), vpbl))
        neglam = lambda: sv("C1", np.s_[0:64, 0:1])
        for hp in range(HP_N):
            P.dma("sp", sv("A0"), APB(q1h[hp * 128:(hp + 1) * 128, :], q1bl))
            P.dma("sp", sv("A1"), APB(q2h[hp * 128:(hp + 1) * 128, :], q2bl))
            def emit_S3(n):
                w0 = 128 * n
                for si, (kt, qt) in enumerate((("A2", "A0"), ("A3", "A1"))):
                    for ks in range(2):
                        for head, base in ((0, 0), (1, 64)):
                            bk, co = head * 2 + si, ks * 256
                            P.mm(ps(bk, np.s_[:, co:co + 256]),
                                 sv(kt, np.s_[base:base + 64, w0 + ks * 128:w0 + (ks + 1) * 128]),
                                 sv(qt, np.s_[base:base + 64, w0:w0 + 256]))
            PTS2 = lambda n, a, b: sv("PTS" if n % 2 == 0 else "HID", np.s_[:, a:b])
            def emit_exp3(n):
                for si in range(2):
                    for head in range(2):
                        bk = head * 2 + si
                        idx2 = si * 2 + head
                        P.act(PTS2(n, idx2 * 512, (idx2 + 1) * 512),
                              ps(bk, np.s_[:, :]), AF.Exp, scale=0.125)
            def emit_PV3(n):
                ob = 4 if n % 2 == 0 else 6
                for si in range(2):
                    for head in range(2):
                        idx2 = si * 2 + head
                        vsl = "A4" if head == 0 else "A5"
                        for ks in range(2):
                            P.mm(ps(ob + idx2 // 2,
                                    np.s_[0:65, (idx2 % 2) * 256:(idx2 % 2) * 256 + 256]),
                                 sv(vsl, np.s_[:, (n + ks) * 65:(n + ks + 1) * 65]),
                                 PTS2(n, idx2 * 512 + ks * 256, idx2 * 512 + ks * 256 + 256),
                                 start=(ks == 0), stop=(ks == 1))
            def emit_tail3(n):
                ob = 4 if n % 2 == 0 else 6
                rb = 6 if n % 2 == 0 else 4
                rdst = [("SBC", 0), ("TMPA", 0)]
                for bi in range(2):
                    rn, rc = rdst[bi]
                    P.act(sv(rn, np.s_[64:65, 0:512]), ps(ob + bi, np.s_[64:65, :]), AF.Ln)
                    P.act(sv(rn, np.s_[64:65, 0:512]), sv(rn, np.s_[64:65, 0:512]),
                          AF.Exp, scale=-1.0)
                for idx2 in range(4):
                    rn, rc = rdst[idx2 // 2]
                    P.mm(ps(rb + idx2 // 2, np.s_[0:64, (idx2 % 2) * 256:(idx2 % 2) * 256 + 256]),
                         ONE_F(np.s_[64:65, 0:64]),
                         sv(rn, np.s_[64:65, (idx2 % 2) * 256:(idx2 % 2) * 256 + 256]))
                for idx2 in range(4):
                    si, head = idx2 // 2, idx2 % 2
                    dstn = "SQ0" if si == 0 else "SQ1"
                    P.cp(sv(dstn, np.s_[0:64, head * 256:head * 256 + 256]),
                         ps(rb + idx2 // 2, np.s_[0:64, (idx2 % 2) * 256:(idx2 % 2) * 256 + 256]))
                for head in range(2):
                    tmpn = "TMPB" if head == 0 else "TMPA"
                    o1i, o2i = head, 2 + head
                    P.tt(sv(tmpn, np.s_[0:64, 0:256]),
                         ps(ob + o1i // 2, np.s_[0:64, (o1i % 2) * 256:(o1i % 2) * 256 + 256]),
                         sv("SQ0", np.s_[0:64, head * 256:head * 256 + 256]), ALU.mult)
                    P.tt(sv(tmpn, np.s_[0:64, 256:512]),
                         ps(ob + o2i // 2, np.s_[0:64, (o2i % 2) * 256:(o2i % 2) * 256 + 256]),
                         sv("SQ1", np.s_[0:64, head * 256:head * 256 + 256]), ALU.mult)
                    P.stt(sv("OCB", np.s_[0:64, head * 256:head * 256 + 256]),
                          sv(tmpn, np.s_[0:64, 256:512]), neglam(),
                          sv(tmpn, np.s_[0:64, 0:256]), ALU.mult, ALU.add)
                a0h, a0bl = DRAM["a0"]
                for head in range(2):
                    P.dma("act", APB(a0h[hp * 128 + head * 64:hp * 128 + head * 64 + 64,
                                         256 * n:256 * n + 256], a0bl[n // 2]),
                          sv("OCB", np.s_[0:64, head * 256:head * 256 + 256]))
            emit_S3(0)
            for n in range(WIN_N):
                emit_exp3(n)
                if n + 1 < WIN_N:
                    emit_S3(n + 1)
                emit_PV3(n)
                emit_tail3(n)

    def proj_residual_tile(t, act_name, w_acc, src_x, dst_x, bias_col=None, banks=(0, 1)):
        t0, t1 = t * TT, (t + 1) * TT
        HT = "HT0" if t % 2 == 0 else "HT1"
        load_fm_tile("sp", HT, act_name, t)
        for m in range(NKC):
            bk = banks[m % 2]
            for kc in range(NKC):
                P.mm(ps(bk, np.s_[:, :]), w_acc(kc, m * 128, (m + 1) * 128),
                     sv(HT, np.s_[:, kc * TT:(kc + 1) * TT]),
                     start=(kc == 0), stop=(kc == NKC - 1))
            xv = sv("XT0") if m % 2 == 0 else sv("XT1")
            P.dma("pool", xv, x_apb(src_x, m, t0, t1, t))
            ov = sv("SQ0") if m % 2 == 0 else sv("SQ1")
            if bias_col is not None:
                P.stt(ov, ps(bk, np.s_[:, :]), bias_col(m), xv, ALU.add, ALU.add)
            else:
                P.tt(ov, ps(bk, np.s_[:, :]), xv, ALU.add)
            P.dma("act", x_apb(dst_x, m, t0, t1, t), ov,
                  dst_name="%s@%d" % (dst_x, t % 2))

    def proj_residual(act_name, w_acc, src_x, dst_x, bias_col=None):
        for t in range(NTT):
            proj_residual_tile(t, act_name, w_acc, src_x, dst_x, bias_col)

    if has("p4"):
        wout = wload("wout0", ["A0", "A1"])
        proj_residual("a0", wout, "x", "xA",
                      bias_col=lambda m: sv("BOUT", np.s_[:, m:m + 1]))

    if has("p5"):
        ln_phase("xA", "ln")

    def moe_gating_l0():
        wgh, wgb = eap("wg")
        wgth, wgtb = eap("wgt")
        P.dma("sp", sv("R16A", np.s_[0:8, 0:16]), APB(wgh[:, :], wgb))
        P.dma("sp", sv("R16B", np.s_[0:16, 0:8]), APB(wgth[:, :], wgtb))
        P.act(sv("R16C", np.s_[0:8, 0:16]), sv("R16A", np.s_[0:8, 0:16]), AF.Square)
        P.red(sv("R16D", np.s_[0:8, 0:1]), sv("R16C", np.s_[0:8, 0:16]), ALU.add)
        P.act(sv("R16D", np.s_[0:8, 0:1]), sv("R16D", np.s_[0:8, 0:1]), AF.Sqrt)
        P.ts(sv("R16D", np.s_[0:8, 0:1]), sv("R16D", np.s_[0:8, 0:1]), 1e-4, ALU.max)
        P.recip(sv("R16D", np.s_[0:8, 0:1]), sv("R16D", np.s_[0:8, 0:1]))
        P.tp(ps(0, np.s_[0:1, 0:8]), sv("R16D", np.s_[0:8, 0:1]), sv("IDENT", np.s_[0:8, 0:8]))
        P.cp(sv("ROWS", np.s_[0:1, 0:8]), ps(0, np.s_[0:1, 0:8]))
        P.mm(ps(1, np.s_[0:16, 0:8]), ONE_F(np.s_[0:1, 0:16]), sv("ROWS", np.s_[0:1, 0:8]))
        P.tt(sv("WGN"), sv("R16B", np.s_[0:16, 0:8]), ps(1, np.s_[0:16, 0:8]), ALU.mult)
        wrh, wrb = eap("wred")
        P.dma("pool", sv("WREDS"),
              APB(wrh.ap().rearrange("(kc p) m -> p kc m", p=128), wrb))
        hb, hbl = DRAM["hbf"]
        sc_h, sc_bl = DRAM["scoresf"]
        for t in range(NTT):
            t0, t1 = t * TT, (t + 1) * TT
            HT = "HT0" if t % 2 == 0 else "HT1"
            load_fm_tile("sp", HT, "hbf", t)
            for kc in range(NKC):
                P.mm(ps(0, np.s_[0:16, :]),
                     APB(SB["WREDS"][0][:, kc * 16:(kc + 1) * 16], SB["WREDS"][1]),
                     sv(HT, np.s_[:, kc * TT:(kc + 1) * TT]),
                     start=(kc == 0), stop=(kc == NKC - 1))
            P.cp(sv("XRED", np.s_[0:16, 0:TT]), ps(0, np.s_[0:16, :]), eng="act")
            for s4 in range(4):
                c = t * 4 + s4
                P.mm(ps(1, np.s_[0:128, 0:8]),
                     sv("XRED", np.s_[0:16, s4 * 128:(s4 + 1) * 128]), sv("WGN"))
                P.red(sv("R16A", np.s_[:, 0:1]), ps(1, np.s_[0:128, 0:8]), ALU.max, negate=True)
                P.red(sv("R16A", np.s_[:, 3:4]), ps(1, np.s_[0:128, 0:8]), ALU.max)
                P.act(sv("R16B", np.s_[0:128, 0:8]), ps(1, np.s_[0:128, 0:8]), AF.Exp,
                      bias=sv("R16A", np.s_[:, 0:1]))
                P.red(sv("R16A", np.s_[:, 1:2]), sv("R16B", np.s_[0:128, 0:8]), ALU.add)
                P.recip(sv("R16A", np.s_[:, 2:3]), sv("R16A", np.s_[:, 1:2]))
                P.ts(sv("R16C", np.s_[0:128, 0:8]), ps(1, np.s_[0:128, 0:8]),
                     sv("R16A", np.s_[:, 3:4]), ALU.is_ge)
                P.dma("act", APB(sc_h[0:1, 128 * c:128 * c + 128].rearrange("a k -> k a"),
                                 sc_bl[c // 4]), sv("R16A", np.s_[:, 2:3]))
                if t == 0 and s4 == 0 and globals().get("_P6DBG"):
                    for nm2 in ("WGN", "XRED", "R16A", "R16B", "R16C"):
                        hh2, bb2 = SB[nm2]
                        dd = nc.dram_tensor("dbg6_" + nm2, list(hh2.shape), hh2.dtype,
                                            kind="ExternalOutput")
                        P.dma("sp", APB(dd.ap(), Buf()), APB(hh2.ap(), bb2))
                P.tp(ps(2, np.s_[0:8, 0:128]), sv("R16C", np.s_[0:128, 0:8]),
                     sv("IDENT"))
                P.cp(sv("CSB", np.s_[0:8, 128 * c:128 * c + 128]), ps(2, np.s_[0:8, 0:128]),
                     eng="act")
        ch, cbl = DRAM["cexp"]
        for t in range(NTT):
            P.dma("sp", APB(ch[:, t * TT:(t + 1) * TT], cbl[t]),
                  sv("CSB", np.s_[:, t * TT:(t + 1) * TT]), dst_name="cexp@%d" % (t % 2))

    def moe_experts(layer):
        w1n, w2n = ("w1_0", "w2_0") if layer == 0 else ("w1_1", "w2_1")
        hbl = DRAM["hbf"][1]
        hb = DRAM["hbf"][0]
        ch, cbl = DRAM["cexp"]
        mh, mbl = DRAM["macc"]
        if layer == 0:
            b2h, b2b = eap("b2_0")
            P.dma("pool", sv("B2SB"), APB(b2h[:, :], b2b))
            b1h, b1b = eap("b1t_0")
        w1slots = [["A0", "A1", "A2", "A3"], ["A8", "A9", "A10", "A11"]]
        w2slots = ["A4", "A5", "A6", "A7"]
        def wexp_load(name, e, slots, ncol):
            h, b = EXT[name]
            nkc = h.shape[1] // 128
            per = nkc // len(slots)
            for si, sl in enumerate(slots):
                hsl, bsl = SB[sl]
                src = h[e].rearrange("(kc p) m -> p kc m", p=128)
                P.dma("pool", APB(hsl[:, 0:per * ncol], bsl),
                      APB(src[:, per * si:per * si + per, :], b))
            def acc(kc, c0, c1):
                hsl, bsl = SB[slots[kc // per]]
                off = (kc % per) * ncol
                return APB(hsl[:, off + c0:off + c1], bsl)
            return acc
        w1a = [None, None]
        w1a[0] = wexp_load(w1n, 0, w1slots[0], 2048)
        for e in range(8):
            if e < 7:
                w1a[(e + 1) % 2] = wexp_load(w1n, e + 1, w1slots[(e + 1) % 2], 2048)
            w2a = wexp_load(w2n, e, w2slots, 1024)
            w1 = w1a[e % 2]
            if layer == 0:
                P.dma("sp", sv("B1T"), APB(b1h[e], b1b))
            for t in range(NTT):
                t0, t1 = t * TT, (t + 1) * TT
                HT = "HT0" if t % 2 == 0 else "HT1"
                load_fm_tile("sp", HT, "hbf", t)
                P.dma("sp", sv("CROW"), APB(ch[e:e + 1, t0:t1], cbl[t]))
                P.mm(ps(6, np.s_[:, :]), ONE_B(np.s_[0:1, 0:128]), sv("CROW"))
                P.cp(sv("CBC"), ps(6, np.s_[:, :]), eng="act")
                for hc in range(16):
                    bank = 4 + (hc % 2)
                    for kc in range(NKC):
                        P.mm(ps(bank, np.s_[:, :]), w1(kc, hc * 128, (hc + 1) * 128),
                             sv(HT, np.s_[:, kc * TT:(kc + 1) * TT]),
                             start=(kc == 0), stop=(kc == NKC - 1))
                    hv = sv("HID", np.s_[:, hc * TT:(hc + 1) * TT])
                    if layer == 0:
                        P.act(hv, ps(bank, np.s_[:, :]), AF.Gelu, bias=sv("B1T", np.s_[:, hc:hc + 1]))
                    else:
                        P.act(hv, ps(bank, np.s_[:, :]), AF.Silu)
                    P.tt(hv, hv, sv("CBC"), ALU.mult)
                half = (t % 2) * 4
                for m in range(NKC):
                    bank = m % 4
                    for hc in range(16):
                        P.mm(ps(bank, np.s_[:, :]), w2a(hc, m * 128, (m + 1) * 128),
                             sv("HID", np.s_[:, hc * TT:(hc + 1) * TT]),
                             start=(hc == 0), stop=(hc == 15 and not (layer == 0 and e == 0)))
                    if layer == 0 and e == 0:
                        P.mm(ps(bank, np.s_[:, :]),
                             APB(SB["B2SB"][0][0:8, m * 128:(m + 1) * 128], SB["B2SB"][1]),
                             sv("CSB", np.s_[0:8, t0:t1]), start=False, stop=True)
                    P.cp(sv("OUTS", np.s_[:, (half + bank) * TT:(half + bank + 1) * TT]),
                         ps(bank, np.s_[:, :]), eng="dve")
                    if m % 4 == 3:
                        g4 = m // 4
                        dst = mh[g4 * 512:(g4 + 1) * 512, t0:t1].rearrange(
                            "(mc p) q -> p mc q", p=128)
                        src3 = SB["OUTS"][0][:, half * TT:(half + 4) * TT].rearrange(
                            "p (mc q) -> p mc q", q=TT)
                        P.dma("pool", APB(dst, mbl[t]), APB(src3, SB["OUTS"][1]),
                              accum=(None if e == 0 else ALU.add),
                              dst_name="macc@%d" % (t % 2))

    def moe_epilogue_l0():
        mh, mbl = DRAM["macc"]
        sc_h, sc_bl = DRAM["scoresf"]
        for t in range(NTT):
            t0, t1 = t * TT, (t + 1) * TT
            P.dma("sp", sv("ROWS", np.s_[0:1, 0:512]), APB(sc_h[0:1, t0:t1], sc_bl[t]))
            P.mm(ps(7, np.s_[:, :]), ONE_F(np.s_[0:1, 0:128]), sv("ROWS", np.s_[0:1, 0:512]))
            P.cp(sv("SBC"), ps(7, np.s_[:, :]), eng="act")
            for m in range(NKC):
                mv = sv("CFB") if m % 2 == 0 else sv("CSB2")
                P.dma("sp", mv, APB(mh[m * 128:(m + 1) * 128, t0:t1], mbl[t]))
                xv = sv("XT0") if m % 2 == 0 else sv("XT1")
                P.dma("pool", xv, x_apb("xA", m, t0, t1, t))
                tv = sv("TMPA") if m % 2 == 0 else sv("TMPB")
                P.tt(tv, mv, sv("SBC"), ALU.mult)
                P.tt(tv, tv, xv, ALU.add)
                P.dma("act", x_apb("xB", m, t0, t1, t), tv, dst_name="xB@%d" % (t % 2))

    if has("p6"):
        moe_gating_l0()
        moe_experts(0)
        moe_epilogue_l0()

    def p8_l1_proj():
        wq = wload("wq_r", ["A0", "A1"])
        wk = wload("wk_r", ["A2", "A3"])
        wv = wload("wv_r", ["A4", "A5"])
        wg5h, wg5b = EXT["wg5"]
        hsl6, bsl6 = SB["A6"]
        P.dma("pool", APB(hsl6[:, 0:640], bsl6),
              APB(wg5h.ap().rearrange("(kc p) m -> p kc m", p=128), wg5b))
        wg5 = lambda kc: APB(hsl6[:, kc * 80:(kc + 1) * 80], bsl6)
        grh, grb = eap("brow5")
        P.dma("pool", sv("GROW"), APB(grh[:, :], grb))
        hb, hbl = DRAM["hbf"]
        qh, qbl = DRAM["qfm"]
        kprh, kprbl = DRAM["kpr"]
        vprh, vprbl = DRAM["vpr"]
        cfh, cfbl = DRAM["cfr"]
        csh, csbl = DRAM["csr"]
        NEG5 = sv("C1", np.s_[:, 1:2])
        for t in range(NTT):
            t0, t1 = t * TT, (t + 1) * TT
            HT = "HT0" if t % 2 == 0 else "HT1"
            load_fm_tile("sp", HT, "hbf", t)
            hT = lambda kc: sv(HT, np.s_[:, kc * TT:(kc + 1) * TT])
            hTs = lambda kc, s4: sv(HT, np.s_[:, kc * TT + s4 * 128:kc * TT + (s4 + 1) * 128])
            for m in range(NKC):
                bank = 2 + (m % 2)
                for kc in range(NKC):
                    P.mm(ps(bank, np.s_[:, :]), wq(kc, m * 128, (m + 1) * 128), hT(kc),
                         start=(kc == 0), stop=(kc == NKC - 1))
                ob = sv("OCB", np.s_[:, 0:512]) if m % 2 == 0 else sv("OCB2")
                P.act(ob, ps(bank, np.s_[:, :]), AF.Silu)
                P.dma("act", APB(qh[m * 128:(m + 1) * 128, t0:t1], qbl[t]), ob)
            for s4 in range(4):
                c = t * 4 + s4
                for kc in range(NKC):
                    P.mm(ps(0, np.s_[0:128, 0:80]), hTs(kc, s4), wg5(kc),
                         start=(kc == 0), stop=False)
                P.mm(ps(0, np.s_[0:128, 0:80]), ONE_B(np.s_[0:1, 0:128]), sv("GROW"),
                     start=False, stop=True)
                P.act(sv("GT"), ps(0, np.s_[0:128, 0:80]), AF.Sigmoid)
                for half, (dst, wacc) in enumerate((("KTV", wk), ("VTV", wv))):
                    pass
                for half in range(2):
                    for name_, wacc, bank in (("KTV", wk, 4), ("VTV", wv, 5)):
                        for kc in range(NKC):
                            P.mm(ps(bank, np.s_[:, :]), hTs(kc, s4),
                                 wacc(kc, half * 512, (half + 1) * 512),
                                 start=(kc == 0), stop=(kc == NKC - 1))
                        dstv = sv(name_, np.s_[:, half * 512:(half + 1) * 512])
                        P.act(dstv, ps(bank, np.s_[:, :]), AF.Silu)
                        bview = SB["GT"][0][:, half * 8:(half + 1) * 8].rearrange(
                            "p (h o) -> p h o", o=1).broadcast_to([128, 8, 64])
                        dv3 = SB[name_][0][:, half * 512:(half + 1) * 512].rearrange(
                            "p (h k) -> p h k", k=64)
                        P.tt(APB(dv3, SB[name_][1]), APB(dv3, SB[name_][1]),
                             APB(bview, SB["GT"][1]), ALU.mult)
                for hp in range(8):
                    name_ = "KTV" if True else None
                    P.dma("act", APB(kprh[hp, :, c, :], kprbl[t]),
                          sv("KTV", np.s_[:, hp * 128:(hp + 1) * 128]))
                    P.dma("act", APB(vprh[hp, :, c, :], vprbl[t]),
                          sv("VTV", np.s_[:, hp * 128:(hp + 1) * 128]))
                for half, tmpn in ((0, "TMPA"), (1, "TMPB")):
                    P.act(sv(tmpn), sv("KTV", np.s_[:, half * 512:(half + 1) * 512]), AF.Square)
                    P.red(sv("R16A", np.s_[:, half * 8:(half + 1) * 8]),
                          APB(SB[tmpn][0].ap().rearrange("p (h k) -> p h k", k=64), SB[tmpn][1]),
                          ALU.add)
                P.act(sv("R16A"), sv("R16A"), AF.Sqrt)
                P.ts(sv("R16A"), sv("R16A"), 1e-8, ALU.add)
                P.recip(sv("R16B"), sv("R16A"))
                for half, tmpn in ((0, "TMPA"), (1, "TMPB")):
                    P.act(sv(tmpn), sv("VTV", np.s_[:, half * 512:(half + 1) * 512]), AF.Square)
                    P.red(sv("R16A", np.s_[:, half * 8:(half + 1) * 8]),
                          APB(SB[tmpn][0].ap().rearrange("p (h k) -> p h k", k=64), SB[tmpn][1]),
                          ALU.add)
                P.act(sv("R16A"), sv("R16A"), AF.Sqrt)
                P.ts(sv("R16A"), sv("R16A"), 1e-8, ALU.add)
                P.recip(sv("R16C"), sv("R16A"))
                for half, tmpn in ((0, "TMPA"), (1, "TMPB")):
                    P.tt(sv(tmpn), sv("KTV", np.s_[:, half * 512:(half + 1) * 512]),
                         sv("VTV", np.s_[:, half * 512:(half + 1) * 512]), ALU.mult)
                    P.red(sv("R16D", np.s_[:, half * 8:(half + 1) * 8]),
                          APB(SB[tmpn][0].ap().rearrange("p (h k) -> p h k", k=64), SB[tmpn][1]),
                          ALU.add)
                P.tt(sv("R16D"), sv("R16D"), sv("R16B"), ALU.mult)
                P.tt(sv("R16D"), sv("R16D"), sv("R16C"), ALU.mult)
                P.act(sv("R16D"), sv("R16D"), AF.Abs)
                PSIc = sv("PSI", np.s_[:, c * 16:(c + 1) * 16])
                SURc = sv("SUR", np.s_[:, c * 16:(c + 1) * 16])
                P.act(PSIc, sv("R16D"), AF.Sigmoid, scale=3.0)
                P.act(SURc, PSIc, AF.Sigmoid, scale=10.0, bias=NEG5)
                for dn, gofs, coef, drow in (("f", 16, -0.1, "DEVN"), ("s", 32, -0.05, "DSLW")):
                    P.ts(sv("R16A"), PSIc, coef, ALU.mult, 1.0, ALU.add)
                    P.tt(sv("R16B"), APB(SB["GT"][0][:, gofs:gofs + 16], SB["GT"][1]),
                         sv("R16A"), ALU.mult)
                    P.mm(ps(1, np.s_[0:1, 0:16]), ONE_F(), sv("R16B"))
                    P.act(sv("ROWS", np.s_[0:1, 1024:1040]), ps(1, np.s_[0:1, 0:16]),
                          AF.Ln, scale=1.0 / 128)
                    P.act(sv("ROWS", np.s_[0:1, 1024:1040]), sv("ROWS", np.s_[0:1, 1024:1040]),
                          AF.Exp, scale=128.0)
                    P.cp(sv(drow, np.s_[0:1, c * 8:(c + 1) * 8]),
                         sv("ROWS", np.s_[0:1, 1024:1040:2]))
                    P.cp(sv(drow, np.s_[0:1, 256 + c * 8:256 + (c + 1) * 8]),
                         sv("ROWS", np.s_[0:1, 1025:1040:2]))
                P.ts(sv("R16A"), PSIc, 0.3, ALU.mult, 0.5, ALU.add)
                P.tt(sv("R16B"), sv("R16A"), APB(SB["GT"][0][:, 48:64], SB["GT"][1]), ALU.mult)
                P.ts(sv("R16C"), sv("R16A"), -1.0, ALU.mult, 1.0, ALU.add)
                P.tt(sv("R16C"), sv("R16C"), APB(SB["GT"][0][:, 64:80], SB["GT"][1]), ALU.mult)
                P.tp(ps(1, np.s_[0:16, 0:128]), sv("R16B"), sv("IDENT"))
                P.cp(sv("CBC", np.s_[0:16, 0:128]), ps(1, np.s_[0:16, 0:128]), eng="act")
                P.dma("act", APB(cfh[:, c * 128:(c + 1) * 128], cfbl[t]),
                      sv("CBC", np.s_[0:16, 0:128]))
                P.tp(ps(1, np.s_[0:16, 0:128]), sv("R16C"), sv("IDENT"))
                P.cp(sv("CBC", np.s_[0:16, 0:128]), ps(1, np.s_[0:16, 0:128]), eng="act")
                P.dma("act", APB(csh[:, c * 128:(c + 1) * 128], csbl[t]),
                      sv("CBC", np.s_[0:16, 0:128]))
        for drow, dbc in (("DEVN", "DBCF"), ("DSLW", "DBCS")):
            P.mm(ps(6, np.s_[0:64, 0:256]), ONE_F(np.s_[0:1, 0:64]),
                 sv(drow, np.s_[0:1, 0:256]))
            P.mm(ps(6, np.s_[64:128, 0:256]), ONE_F(np.s_[0:1, 0:64]),
                 sv(drow, np.s_[0:1, 256:512]), tile_position=(0, 64))
            P.cp(sv(dbc), ps(6, np.s_[:, 0:256]))

    def p9_recurrence():
        kprh, kprbl = DRAM["kpr"]
        vprh, vprbl = DRAM["vpr"]
        qh, qbl = DRAM["qfm"]
        ofh, ofbl = DRAM["ofm"]
        cfh, cfbl = DRAM["cfr"]
        csh, csbl = DRAM["csr"]
        for hp in range(8):
            sA = ["A0", "A1", "A2"] if hp % 2 == 0 else ["A3", "A4", "A5"]
            kst, vst, qst = sA
            P.dma("sp", sv(kst), APB(kprh[hp].rearrange("p c n -> p (c n)"), kprbl))
            P.dma("sp", sv(vst), APB(vprh[hp].rearrange("p c n -> p (c n)"), vprbl))
            P.dma("sp", sv(qst), APB(qh[hp * 128:(hp + 1) * 128, :], qbl))
            P.mset(sv("STF"), 0.0)
            P.mset(sv("STS"), 0.0)
            for c in range(32):
                cb = np.s_[:, c * 128:(c + 1) * 128]
                sview = SB["SUR"][0][:, c * 16 + 2 * hp:c * 16 + 2 * hp + 2].rearrange(
                    "p (h o) -> p h o", o=1).broadcast_to([128, 2, 64])
                for src_, dst_ in ((kst, "KSL"), (vst, "VSL")):
                    v3 = SB[src_][0][:, c * 128:(c + 1) * 128].rearrange(
                        "p (h k) -> p h k", k=64)
                    d3 = SB[dst_][0][:, :].rearrange("p (h k) -> p h k", k=64)
                    P.tt(APB(d3, SB[dst_][1]), APB(v3, SB[src_][1]),
                         APB(sview, SB["SUR"][1]), ALU.mult, eng="pool")
                for head in range(2):
                    o0 = c * 128 + head * 64
                    tp_ = None if head == 0 else (0, 64)
                    osl = np.s_[0:64, 0:64] if head == 0 else np.s_[64:128, 0:64]
                    P.mm(ps(0, osl), APB(SB[vst][0][:, o0:o0 + 64], SB[vst][1]),
                         APB(SB[kst][0][:, o0:o0 + 64], SB[kst][1]), tile_position=tp_)
                    P.mm(ps(1, osl), APB(SB["VSL"][0][:, head * 64:head * 64 + 64], SB["VSL"][1]),
                         APB(SB["KSL"][0][:, head * 64:head * 64 + 64], SB["KSL"][1]),
                         tile_position=tp_)
                P.stt(sv("STF"), sv("STF"), sv("DBCF", np.s_[:, c * 8 + hp:c * 8 + hp + 1]),
                      ps(0, np.s_[:, 0:64]), ALU.mult, ALU.add)
                P.cp(sv("STFB"), sv("STF"), eng="act")
                P.stt(sv("STS"), sv("STS"), sv("DBCS", np.s_[:, c * 8 + hp:c * 8 + hp + 1]),
                      ps(1, np.s_[:, 0:64]), ALU.mult, ALU.add)
                P.cp(sv("STSB"), sv("STS"), eng="act")
                co = np.s_[0:64, (c % 4) * 128:(c % 4) * 128 + 128]
                co2 = np.s_[64:128, (c % 4) * 128:(c % 4) * 128 + 128]
                qA = APB(SB[qst][0][0:64, c * 128:(c + 1) * 128], SB[qst][1])
                qB = APB(SB[qst][0][64:128, c * 128:(c + 1) * 128], SB[qst][1])
                P.mm(ps(2, co), sv("STFB", np.s_[0:64, :]), qA)
                P.mm(ps(3, co2), sv("STFB", np.s_[64:128, :]), qB, tile_position=(64, 64))
                P.mm(ps(4, co), sv("STSB", np.s_[0:64, :]), qA)
                P.mm(ps(5, co2), sv("STSB", np.s_[64:128, :]), qB, tile_position=(64, 64))
                if c % 4 == 3:
                    g = c // 4
                    g0, g1 = g * 512, (g + 1) * 512
                    P.dma("sp", sv("CF2"), APB(cfh[2 * hp:2 * hp + 2, g0:g1], cfbl[g]))
                    P.dma("sp", sv("CS2"), APB(csh[2 * hp:2 * hp + 2, g0:g1], csbl[g]))
                    P.mm(ps(6, np.s_[:, :]), sv("E2SB"), sv("CF2"))
                    P.mm(ps(7, np.s_[:, :]), sv("E2SB"), sv("CS2"))
                    P.cp(sv("CFB"), ps(6, np.s_[:, :]), eng="act")
                    P.cp(sv("CSB2"), ps(7, np.s_[:, :]), eng="act")
                    P.tt(sv("TMPA", np.s_[0:64, :]), ps(2, np.s_[0:64, :]),
                         sv("CFB", np.s_[0:64, :]), ALU.mult)
                    P.tt(sv("TMPA", np.s_[64:128, :]), ps(3, np.s_[64:128, :]),
                         sv("CFB", np.s_[64:128, :]), ALU.mult)
                    P.tt(sv("TMPB", np.s_[0:64, :]), ps(4, np.s_[0:64, :]),
                         sv("CSB2", np.s_[0:64, :]), ALU.mult)
                    P.tt(sv("TMPB", np.s_[64:128, :]), ps(5, np.s_[64:128, :]),
                         sv("CSB2", np.s_[64:128, :]), ALU.mult)
                    ob = sv("OCB", np.s_[:, 0:512]) if g % 2 == 0 else sv("OCB2")
                    P.tt(ob, sv("TMPA"), sv("TMPB"), ALU.add)
                    P.dma("act", APB(ofh[hp * 128:(hp + 1) * 128, g0:g1], ofbl[g]), ob,
                          dst_name="ofm9@%d" % (g % 2))

    if has("p7"):
        ln_phase("xB", "ln")
    if has("p8"):
        p8_l1_proj()
    if has("p9"):
        p9_recurrence()
    if has("p10"):
        wo = wload("wo_r", ["A6", "A7"])
        proj_residual("ofm", wo, "xB", "xA")

    def p12_mla_proj():
        wqm = wload("wq_m", ["A0", "A1"])
        wdkvh, wdkvb = EXT["wdkv"]
        h2, b2_ = SB["A2"]
        P.dma("pool", APB(h2[:, 0:4096], b2_),
              APB(wdkvh.ap().rearrange("(kc p) m -> p kc m", p=128), wdkvb))
        wdkv = lambda kc, c0, c1: APB(h2[:, kc * 512 + c0:kc * 512 + c1], b2_)
        wukh, wukb = EXT["wuk"]
        h3_, b3_ = SB["A3"]
        P.dma("pool", APB(h3_[:, 0:4096], b3_),
              APB(wukh.ap().rearrange("(kc p) m -> p kc m", p=128), wukb))
        wuk = lambda kc, c0, c1: APB(h3_[:, kc * 1024 + c0:kc * 1024 + c1], b3_)
        wuvh, wuvb = EXT["wuv"]
        h4_, b4_ = SB["A4"]
        P.dma("pool", APB(h4_[:, 0:4096], b4_),
              APB(wuvh.ap().rearrange("(kc p) m -> p kc m", p=128), wuvb))
        wuv = lambda kc, c0, c1: APB(h4_[:, kc * 1024 + c0:kc * 1024 + c1], b4_)
        hb, hbl = DRAM["hbf"]
        qh, qbl = DRAM["qfm"]
        ch, cbl = DRAM["cfm"]
        kh, kbl = DRAM["kfm"]
        vph, vpbl = DRAM["vpack1"]
        for t in range(NTT):
            t0, t1 = t * TT, (t + 1) * TT
            HT = "HT0" if t % 2 == 0 else "HT1"
            load_fm_tile("sp", HT, "hbf", t)
            hT = lambda kc: sv(HT, np.s_[:, kc * TT:(kc + 1) * TT])
            for m in range(NKC):
                bank = 0 + (m % 2)
                for kc in range(NKC):
                    P.mm(ps(bank, np.s_[:, :]), wqm(kc, m * 128, (m + 1) * 128), hT(kc),
                         start=(kc == 0), stop=(kc == NKC - 1))
                ob = sv("OCB", np.s_[:, 0:512]) if m % 2 == 0 else sv("OCB2")
                P.cp(ob, ps(bank, np.s_[:, :]), eng="act")
                P.dma("act", APB(qh[m * 128:(m + 1) * 128, t0:t1], qbl[t]), ob)
            for m in range(4):
                bank = 2 + (m % 2)
                for kc in range(NKC):
                    P.mm(ps(bank, np.s_[:, :]), wdkv(kc, m * 128, (m + 1) * 128), hT(kc),
                         start=(kc == 0), stop=(kc == NKC - 1))
                P.cp(sv("HID", np.s_[:, m * TT:(m + 1) * TT]), ps(bank, np.s_[:, :]), eng="act")
                P.dma("act", APB(ch[m * 128:(m + 1) * 128, t0:t1], cbl[t]),
                      sv("HID", np.s_[:, m * TT:(m + 1) * TT]))
            cT = lambda cc: sv("HID", np.s_[:, cc * TT:(cc + 1) * TT])
            for m in range(NKC):
                bank = 4 + (m % 2)
                for cc in range(4):
                    P.mm(ps(bank, np.s_[:, :]), wuk(cc, m * 128, (m + 1) * 128), cT(cc),
                         start=(cc == 0), stop=(cc == 3))
                ob = sv("OCB", np.s_[:, 0:512]) if m % 2 == 0 else sv("OCB2")
                P.cp(ob, ps(bank, np.s_[:, :]), eng="act")
                P.dma("act", APB(kh[m * 128:(m + 1) * 128, t0:t1], kbl[t]), ob)
            for s4 in range(4):
                c = t * 4 + s4
                for half in range(2):
                    bank = 6 + half
                    for cc in range(4):
                        P.mm(ps(bank, np.s_[:, :]),
                             sv("HID", np.s_[:, cc * TT + s4 * 128:cc * TT + (s4 + 1) * 128]),
                             wuv(cc, half * 512, (half + 1) * 512),
                             start=(cc == 0), stop=(cc == 3))
                    P.mset(sv("OCB", np.s_[:, 0:520]), 1.0, eng="pool")
                    ocv = SB["OCB"][0][:, 0:520].rearrange("p (g k) -> p g k", k=65)
                    P.cp(APB(ocv[:, 0:8, 0:64], SB["OCB"][1]),
                         APB(PS[bank][0][:, :].rearrange("p (g k) -> p g k", k=64), PS[bank][1]),
                         eng="act")
                    for j in range(8):
                        head = half * 8 + j
                        P.dma("act", APB(vph[head, :, c, :], vpbl[t]),
                              APB(ocv[:, j, 0:65], SB["OCB"][1]))

    def p13_mla_attention():
        qh, qbl = DRAM["qfm"]
        kh, kbl = DRAM["kfm"]
        vph, vpbl = DRAM["vpack1"]
        ofh, ofbl = DRAM["ofm"]
        wom = wload("wo_m", ["A0", "A1"])
        for qt in range(NTT):
            q0, q1 = qt * TT, (qt + 1) * TT
            nkt = 4 * qt + 4
            for hp in range(8):
                ksl = "A5" if hp % 2 == 0 else "A2"
                va = "A6" if hp % 2 == 0 else "A3"
                vb = "A7" if hp % 2 == 0 else "A4"
                P.dma("sp", sv(ksl, np.s_[:, 0:nkt * 128]),
                      APB(kh[hp * 128:(hp + 1) * 128, 0:nkt * 128], kbl))
                P.dma("sp", sv(va, np.s_[:, 0:nkt * 65]),
                      APB(vph[2 * hp, :, 0:nkt, :].rearrange("p c k -> p (c k)"), vpbl))
                P.dma("sp", sv(vb, np.s_[:, 0:nkt * 65]),
                      APB(vph[2 * hp + 1, :, 0:nkt, :].rearrange("p c k -> p (c k)"), vpbl))
                qb = "CBC" if hp % 2 == 0 else "CFB"
                P.dma("sp", sv(qb), APB(qh[hp * 128:(hp + 1) * 128, q0:q1], qbl[qt]))
                def emit_S(kt):
                    par = kt % 2
                    for head, base in ((0, 0), (1, 64)):
                        bank = (0 if par == 0 else 4) + head
                        P.mm(ps(bank, np.s_[:, :]),
                             sv(ksl, np.s_[base:base + 64, kt * 128:(kt + 1) * 128]),
                             sv(qb, np.s_[base:base + 64, :]))
                def emit_exp(kt):
                    par = kt % 2
                    for head in range(2):
                        bank = (0 if par == 0 else 4) + head
                        pts = np.s_[:, par * 1024 + head * 512:par * 1024 + (head + 1) * 512]
                        P.act(sv("PTS", pts), ps(bank, np.s_[:, :]), AF.Exp, scale=0.125)
                        if kt >= 4 * qt:
                            i = kt - 4 * qt
                            P.tt(sv("PTS", pts), sv("PTS", pts),
                                 sv("MKSB", np.s_[:, i * 512:(i + 1) * 512]), ALU.mult)
                def emit_PV(kt):
                    par = kt % 2
                    for head in range(2):
                        pts = np.s_[:, par * 1024 + head * 512:par * 1024 + (head + 1) * 512]
                        vsl = va if head == 0 else vb
                        P.mm(ps(2 + head, np.s_[0:65, :]),
                             sv(vsl, np.s_[:, kt * 65:(kt + 1) * 65]),
                             sv("PTS", pts), start=(kt == 0), stop=(kt == nkt - 1))
                emit_S(0)
                for kt in range(nkt):
                    emit_exp(kt)
                    if kt + 1 < nkt:
                        emit_S(kt + 1)
                    emit_PV(kt)
                for head, (rn, sq) in enumerate((("SBC", "SQ0"), ("TMPA", "SQ1"))):
                    P.act(sv(rn, np.s_[64:65, :]), ps(2 + head, np.s_[64:65, :]), AF.Ln)
                    P.act(sv(rn, np.s_[64:65, :]), sv(rn, np.s_[64:65, :]), AF.Exp, scale=-1.0)
                    P.mm(ps(4 + head, np.s_[0:64, :]), ONE_F(np.s_[64:65, 0:64]),
                         sv(rn, np.s_[64:65, :]))
                    P.cp(sv(sq, np.s_[0:64, :]), ps(4 + head, np.s_[0:64, :]))
                    ob = sv("OCB", np.s_[0:64, 0:512]) if head == 0 else sv("OCB2", np.s_[0:64, :])
                    P.tt(ob, ps(2 + head, np.s_[0:64, :]), sv(sq, np.s_[0:64, :]), ALU.mult)
                    P.dma("act", APB(ofh[hp * 128 + head * 64:hp * 128 + head * 64 + 64, q0:q1],
                                     ofbl[qt]), ob, dst_name="ofm@%d" % (qt % 2))
            proj_residual_tile(qt, "ofm", wom, "xC", "xB", banks=(6, 7))

    def moe_gating_l1():
        gwh, gwb = EXT["gatew"]
        P.dma("pool", sv("WREDS", np.s_[:, 0:64]),
              APB(gwh.ap().rearrange("(kc p) m -> p kc m", p=128), gwb))
        hb, hbl = DRAM["hbf"]
        for t in range(NTT):
            t0, t1 = t * TT, (t + 1) * TT
            HT = "HT0" if t % 2 == 0 else "HT1"
            load_fm_tile("sp", HT, "hbf", t)
            for s4 in range(4):
                c = t * 4 + s4
                for kc in range(NKC):
                    P.mm(ps(1, np.s_[0:128, 0:8]),
                         sv(HT, np.s_[:, kc * TT + s4 * 128:kc * TT + (s4 + 1) * 128]),
                         APB(SB["WREDS"][0][:, kc * 8:(kc + 1) * 8], SB["WREDS"][1]),
                         start=(kc == 0), stop=(kc == NKC - 1))
                P.red(sv("R16A", np.s_[:, 1:2]), ps(1, np.s_[0:128, 0:8]), ALU.max)
                P.ts(sv("R16B", np.s_[:, 0:8]), ps(1, np.s_[0:128, 0:8]),
                     sv("R16A", np.s_[:, 1:2]), ALU.is_ge)
                P.stt(sv("R16C", np.s_[:, 0:8]), sv("R16B", np.s_[:, 0:8]), -1e30,
                      ps(1, np.s_[0:128, 0:8]), ALU.mult, ALU.add)
                P.red(sv("R16A", np.s_[:, 2:3]), sv("R16C", np.s_[:, 0:8]), ALU.max)
                P.ts(sv("R16D", np.s_[:, 0:8]), sv("R16C", np.s_[:, 0:8]),
                     sv("R16A", np.s_[:, 2:3]), ALU.is_ge)
                P.tt(sv("R16A", np.s_[:, 3:4]), sv("R16A", np.s_[:, 1:2]),
                     sv("R16A", np.s_[:, 2:3]), ALU.subtract)
                P.act(sv("R16A", np.s_[:, 4:5]), sv("R16A", np.s_[:, 3:4]), AF.Sigmoid)
                P.ts(sv("R16A", np.s_[:, 5:6]), sv("R16A", np.s_[:, 4:5]), -1.0,
                     ALU.mult, 1.0, ALU.add)
                P.ts(sv("R16B", np.s_[:, 0:8]), sv("R16B", np.s_[:, 0:8]),
                     sv("R16A", np.s_[:, 4:5]), ALU.mult)
                P.ts(sv("R16D", np.s_[:, 0:8]), sv("R16D", np.s_[:, 0:8]),
                     sv("R16A", np.s_[:, 5:6]), ALU.mult)
                P.tt(sv("R16C", np.s_[:, 0:8]), sv("R16B", np.s_[:, 0:8]),
                     sv("R16D", np.s_[:, 0:8]), ALU.add)
                P.tp(ps(2, np.s_[0:8, 0:128]), sv("R16C", np.s_[0:128, 0:8]), sv("IDENT"))
                P.cp(sv("CSB", np.s_[0:8, 128 * c:128 * c + 128]), ps(2, np.s_[0:8, 0:128]),
                     eng="act")
        ch, cbl = DRAM["cexp"]
        for t in range(NTT):
            P.dma("sp", APB(ch[:, t * TT:(t + 1) * TT], cbl[t]),
                  sv("CSB", np.s_[:, t * TT:(t + 1) * TT]), dst_name="cexp@%d" % (t % 2))

    def final_epilogue():
        mh, mbl = DRAM["macc"]
        for t in range(NTT):
            t0, t1 = t * TT, (t + 1) * TT
            for m in range(NKC):
                mv = sv("CFB") if m % 2 == 0 else sv("CSB2")
                P.dma("sp", mv, APB(mh[m * 128:(m + 1) * 128, t0:t1], mbl[t]))
                xv = sv("XT0") if m % 2 == 0 else sv("XT1")
                P.dma("pool", xv, x_apb("xA", m, t0, t1, t))
                bv = sv("TMPA") if m % 2 == 0 else sv("TMPB")
                P.dma("act", bv, x_apb("xB", m, t0, t1, t))
                ov = sv("SQ0") if m % 2 == 0 else sv("SQ1")
                P.tt(ov, mv, xv, ALU.add)
                P.tt(ov, ov, bv, ALU.add)
                P.dma("act", x_apb("y", m, t0, t1, t), ov, dst_name="y@%d" % (t % 2))

    if has("p11"):
        ln_phase("xA", "ln+rms", g="G1T", ln_out="xC")
    if has("p12"):
        p12_mla_proj()
    if has("p13"):
        p13_mla_attention()
    if has("p15"):
        ln_phase("xB", "rms", g="G2T")
    if has("p16"):
        moe_gating_l1()
        moe_experts(1)
        final_epilogue()



    return nc, P, EXT, DRAM, SB, y


def _emit_dbg(nc, P, DRAM, dbg):
    outs = []
    for name in dbg:
        h, bl = DRAM[name]
        sh = list(h.shape) if hasattr(h, "shape") else None
        o = nc.dram_tensor("dbg_" + name, sh, h.dtype)
    return outs


def _build_full(phases=None, dbg=()):
    nc, P, EXT, DRAM, SB, y = build(phases, dbg)
    for name in dbg:
        h, bl = DRAM[name]
        o = nc.dram_tensor("dbg_" + name, list(h.shape), h.dtype, kind="ExternalOutput")
        P.dma("sp", APB(o.ap(), Buf("dbg")), APB(h.ap(), bl[0]))
        for b in bl[1:]:
            P._record("sp", lambda E: E.nop(), [b], [])
    P.finalize_and_emit()
    return nc


def kernel(**inputs):
    S = _host_prep(inputs)
    x = np.asarray(inputs["L0_attn_x"], dtype=np.float32)
    nc = _build_full()
    in_maps = []
    for b in range(8):
        m = dict(S)
        m["x"] = np.ascontiguousarray(x[b].T)
        in_maps.append(m)
    res = run_bass_kernel_spmd(nc, in_maps, core_ids=list(range(8)))
    out = np.stack([np.asarray(r["y"]).T for r in res.results], axis=0)
    return out
